# version 34
# baseline (speedup 1.0000x reference)
import numpy as np
import ml_dtypes
import concourse.bass as bass
import concourse.mybir as mybir
from concourse.bass_utils import run_bass_kernel_spmd

F32 = mybir.dt.float32
BF16 = mybir.dt.bfloat16
FP16 = mybir.dt.float16
E4M3 = mybir.dt.float8e4
AF = mybir.ActivationFunctionType
ALU = mybir.AluOpType
AX = mybir.AxisListType

E = 1024
NH = 16
HD = 64
L = 4096
TQ = 2048
NTK = 32
NTQ = 16
ECH = 8
HPn = 8
QBn = 4
KC = 32
HF = 4096
HCn = 32
EPS = 1e-5


def build_nc():
    nc = bass.Bass("TRN2", target_bir_lowering=False)

    x_in = nc.dram_tensor("x_in", [L, E], F32, kind="ExternalInput")
    wqT = nc.dram_tensor("wqT", [E, E], F32, kind="ExternalInput")
    wkT = nc.dram_tensor("wkT", [E, E], F32, kind="ExternalInput")
    wvT = nc.dram_tensor("wvT", [E, E], F32, kind="ExternalInput")
    woT = nc.dram_tensor("woT", [E, E], F32, kind="ExternalInput")
    w1T = nc.dram_tensor("w1T", [E, HF], F32, kind="ExternalInput")
    w2T = nc.dram_tensor("w2T", [HF, E], F32, kind="ExternalInput")
    bq_i = nc.dram_tensor("bq_i", [1, E], F32, kind="ExternalInput")
    bk_i = nc.dram_tensor("bk_i", [1, E], F32, kind="ExternalInput")
    bv_i = nc.dram_tensor("bv_i", [1, E], F32, kind="ExternalInput")
    bo_i = nc.dram_tensor("bo_i", [1, E], F32, kind="ExternalInput")
    b2_i = nc.dram_tensor("b2_i", [1, E], F32, kind="ExternalInput")
    b1_i = nc.dram_tensor("b1_i", [128, HCn], F32, kind="ExternalInput")
    rqc_i = nc.dram_tensor("rqc_i", [128, NTQ * 32], F32, kind="ExternalInput")
    rqs_i = nc.dram_tensor("rqs_i", [128, NTQ * 32], F32, kind="ExternalInput")
    rkc_i = nc.dram_tensor("rkc_i", [128, NTK * 32], F32, kind="ExternalInput")
    rks_i = nc.dram_tensor("rks_i", [128, NTK * 32], F32, kind="ExternalInput")
    out_o = nc.dram_tensor("out", [TQ, E], F32, kind="ExternalOutput")

    kt_scr = nc.dram_tensor("kt_scr", [E, L], BF16)
    qt_scr = nc.dram_tensor("qt_scr", [E, TQ], BF16)
    v_scr = nc.dram_tensor("v_scr", [L, E], BF16)
    x2_scr = nc.dram_tensor("x2_scr", [TQ, E], F32)
    w1q_scr = nc.dram_tensor("w1q_scr", [E, HF], BF16)
    w2q_scr = nc.dram_tensor("w2q_scr", [HF, E], E4M3)

    identN = np.eye(128, dtype=ml_dtypes.bfloat16)
    zindN = np.zeros((128, 128), dtype=np.float32)
    zindN[0, 0:64] = 1.0
    zindN[64, 0:64] = 1.0
    zindN[32, 64:128] = 1.0
    zindN[96, 64:128] = 1.0
    zonesN = np.zeros((128, 32), dtype=ml_dtypes.bfloat16)
    zonesN[:, 0] = 1.0
    ident_t = nc.inline_tensor(identN, "ident_t")
    zind_t = nc.inline_tensor(zindN, "zind_t")
    zones_t = nc.inline_tensor(zonesN, "zones_t")
    onesb_t = nc.inline_tensor(np.ones((1, 128), dtype=ml_dtypes.bfloat16), "onesb_t")
    onesf_t = nc.inline_tensor(np.ones((1, 128), dtype=np.float32), "onesf_t")

    from contextlib import ExitStack
    with ExitStack() as _ctx:
        _sb = lambda *a, **k: _ctx.enter_context(nc.sbuf_tensor(*a, **k))
        _ps = lambda *a, **k: _ctx.enter_context(nc.psum_tensor(*a, **k))
        _sem = lambda n: _ctx.enter_context(nc.semaphore(n))
        BIG_A = _sb("BIG_A", [128, 32768], BF16)
        BIG_B = _sb("BIG_B", [128, 32768], BF16)
        CTXB = _sb("CTX", [128, 16384], BF16)
        QTS = _sb("QTS", [128, 2 * TQ], BF16)
        xin = _sb("xin", [128, 2 * E], F32)
        xlnb = _sb("xlnb", [128, 2 * E], BF16)
        krb = _sb("krb", [128, 6 * 512], BF16)
        mnv = _sb("mnv", [128, 32], F32)
        vrv = _sb("vrv", [128, 32], F32)
        rstb = _sb("rstb", [128, 32], F32)
        st12 = _sb("st12", [128, 12], F32)
        mv2 = _sb("mv2", [128, 2], F32)
        rs1 = _sb("rs1", [128, 1], F32)
        qm2 = _sb("qm2", [128, 16], F32)
        km2 = _sb("km2", [128, 16], F32)
        nrm8 = _sb("nrm8", [128, 8], F32)
        mx16 = _sb("mx16", [128, 32], BF16)
        c16 = _sb("c16", [16, 4], F32)
        cnegb = _sb("cnegb", [16, 1], BF16)
        cnr = _sb("cnr", [1, 16], BF16)
        tmpn = _sb("tmpn", [128, E], F32)
        cbias = _sb("cbias", [128, 16], F32)
        cpairb = _sb("cpairb", [128, 8], F32)
        zindb = _sb("zindb", [128, 128], F32)
        zonesb = _sb("zonesb", [128, 32], BF16)
        identb = _sb("identb", [128, 128], BF16)
        onesbb = _sb("onesbb", [1, 128], BF16)
        onesfb = _sb("onesfb", [1, 128], F32)
        bobr = _sb("bobr", [1, E], BF16)
        bqr = _sb("bqr", [1, E], BF16)
        bkr = _sb("bkr", [1, E], BF16)
        bvr = _sb("bvr", [1, E], BF16)
        b1s = _sb("b1s", [128, HCn], F32)
        zse = _sb("zse", [128, 512], F32)
        rzb = _sb("rzb", [128, 1024], F32)
        P8 = _ps("P8", [128, 4096], F32)
        s_ci = _sem("s_ci")
        s_w = _sem("s_w")
        s_b2 = _sem("s_b2")
        s_x1p = [_sem("s_x1a"), _sem("s_x1b")]
        s_bn1 = _sem("s_bn1")
        s_r1 = _sem("s_r1")
        s_x2p = [_sem("s_x2a"), _sem("s_x2b")]
        s_ln = _sem("s_ln")
        s_tpa = _sem("s_tpa")
        s_eva = _sem("s_eva")
        s_qkv = _sem("s_qkv")
        s_qe = _sem("s_qe")
        s_rope = _sem("s_rope")
        s_tpb = _sem("s_tpb")
        s_kte = _sem("s_kte")
        s_scwp = [_sem("s_scwa"), _sem("s_scwb")]
        s_ve = _sem("s_ve")
        s_nrm = _sem("s_nrm")
        s_o2s = _sem("s_o2s")
        s_o2f = _sem("s_o2f")
        s_vwp = [_sem("s_vwa"), _sem("s_vwb")]
        s_cb = _sem("s_cb")
        s_sbp = [_sem("s_sba"), _sem("s_sbb")]
        s_sc = _sem("s_sc")
        s_exp = _sem("s_exp")
        s_pvz = _sem("s_pvz")
        s_pv = _sem("s_pv")
        s_zse = _sem("s_zse")
        s_fold = _sem("s_fold")
        s_rz = _sem("s_rz")
        s_ctx = _sem("s_ctx")
        s_op = _sem("s_op")
        s_bnc = _sem("s_bnc")
        s_rsc = _sem("s_rsc")
        s_xrp = [_sem("s_xra"), _sem("s_xrb")]
        s_x2n = _sem("s_x2n")
        s_x2wp = [_sem("s_x2wa"), _sem("s_x2wb")]
        s_w1 = _sem("s_w1")
        s_w2p = [_sem("s_w2a"), _sem("s_w2b")]
        s_w2c = _sem("s_w2c")
        s_w1c = _sem("s_w1c")
        s_w1h = [_sem("s_w1ha"), _sem("s_w1hb")]
        s_l3sp = [_sem("s_l3sa"), _sem("s_l3sb")]
        s_bnd = _sem("s_bnd")
        s_b3e = _sem("s_b3e")
        s_rs3 = _sem("s_rs3")
        s_l3ap = [_sem("s_l3aa"), _sem("s_l3ab")]
        s_ln3 = _sem("s_ln3")
        s_tpd = _sem("s_tpd")
        s_evd = _sem("s_evd")
        s_mm1 = _sem("s_mm1")
        s_hq = _sem("s_hq")
        s_mm2 = _sem("s_mm2")
        s_o2 = _sem("s_o2")
        s_rs4 = _sem("s_rs4")
        s_frp = [_sem("s_fra"), _sem("s_frb")]
        s_outp = [_sem("s_outa"), _sem("s_outb")]
        block = _ctx.enter_context(nc.Block())
        xT = BIG_A[:]
        PT_A = BIG_A[:, 0:16384]
        PT_B = BIG_A[:, 16384:32768]
        w1T_sb = BIG_A[:]

        wq_sb = BIG_B[:, 0:8192]
        wk_sb = BIG_B[:, 8192:16384]
        wv_sb = BIG_B[:, 16384:24576]
        wo_sb = BIG_B[:, 24576:32768]

        def kts(s):
            return BIG_B[:, s * 10240: s * 10240 + 4096]

        def vstr(s):
            return BIG_B[:, s * 10240 + 4096: s * 10240 + 8192]

        def qts(s):
            return QTS[:, s * TQ:(s + 1) * TQ]

        w2_sb = BIG_B[:].bitcast(E4M3)[:, 0:32768]
        xT2 = BIG_B[:, 16384:24576]
        o2q = BIG_B[:].bitcast(FP16)[:, 24576:32768]
        ctxT = CTXB[:]
        hqT = CTXB[:].bitcast(E4M3)

        ps_qkv = [P8[:, i * 512:(i + 1) * 512] for i in range(4)]
        ps_tp = P8[:, 2048:3072].bitcast(BF16)
        ps_rtp = P8[:, 3072:4096].bitcast(BF16)
        ps_bc = P8[:, 3072:3584]
        ps_st = [P8[:, m * 1024:(m + 1) * 1024] for m in range(3)]
        ps_ot = P8[:, 3072:3584]
        ps_zf = P8[:, 3584:4096]
        ps_m2 = [P8[:, 3072:3584], P8[:, 3584:4096]]

        CTXF = CTXB[:].bitcast(F32)
        sqb = CTXF[:, 3072:4096]
        nrmall = CTXF[:, 0:768]
        zltv = tmpn[:][:, 0:512]
        b2qb = rzb[:]
        b2row = CTXF[0:1, 5120:6144]
        b2h = CTXB[:].bitcast(FP16)[0:1, 12288:13312]
        QTSF = QTS[:].bitcast(F32)
        rkcs = QTSF[:, 0:1024]
        rkss = QTSF[:, 1024:2048]
        rqcs = CTXF[:, 4096:4608]
        rqss = CTXF[:, 4608:5120]
        outb = QTSF[:, 0:2048]
        ktst = CTXB[:, 13312:14336]
        vtl = CTXB[:, 14336:15360]

        NCI = 9
        NW = 8

        fbs_all = []
        for i in range(NTK):
            fl = [("k", 0, i), ("k", 1, i), ("v", 0, i), ("v", 1, i)]
            if i < NTQ:
                fl += [("q", 0, i), ("q", 1, i)]
            fbs_all += fl
        NROPE = NTK * 2 + NTQ * 2
        NV = NTK * 2

        def emit(eng, e):
            if eng == "sync":
                e.dma_start(out=identb[:], in_=ident_t.ap()[:, :]).then_inc(s_ci, 16)
                e.dma_start(out=zindb[:], in_=zind_t.ap()[:, :]).then_inc(s_ci, 16)
                e.dma_start(out=zonesb[:], in_=zones_t.ap()[:, :]).then_inc(s_ci, 16)
                e.dma_start(out=onesbb[:], in_=onesb_t.ap()[:, :]).then_inc(s_ci, 16)
                e.dma_start(out=rqcs, in_=rqc_i.ap()[:, :]).then_inc(s_ci, 16)
                e.dma_start(out=rqss, in_=rqs_i.ap()[:, :]).then_inc(s_ci, 16)
                e.dma_start(out=rkcs, in_=rkc_i.ap()[:, :]).then_inc(s_ci, 16)
                e.dma_start(out=rkss, in_=rks_i.ap()[:, :]).then_inc(s_ci, 16)
                e.dma_start(out=b1s[:], in_=b1_i.ap()[:, :]).then_inc(s_ci, 16)
            if eng == "gpsimd":
                e.dma_start(out=wq_sb.rearrange("p (c f) -> p c f", c=8),
                            in_=wqT.ap()[:, :].rearrange("(c p) f -> p c f", p=128)).then_inc(s_w, 16)
                e.dma_start(out=wk_sb.rearrange("p (c f) -> p c f", c=8),
                            in_=wkT.ap()[:, :].rearrange("(c p) f -> p c f", p=128)).then_inc(s_w, 16)
                e.dma_start(out=wv_sb.rearrange("p (c f) -> p c f", c=8),
                            in_=wvT.ap()[:, :].rearrange("(c p) f -> p c f", p=128)).then_inc(s_w, 16)
                e.dma_start(out=wo_sb.rearrange("p (c f) -> p c f", c=8),
                            in_=woT.ap()[:, :].rearrange("(c p) f -> p c f", p=128)).then_inc(s_w, 16)
                e.dma_start(out=bobr[:], in_=bo_i.ap()[0:1, :]).then_inc(s_w, 16)
                e.dma_start(out=bqr[:], in_=bq_i.ap()[0:1, :]).then_inc(s_w, 16)
                e.dma_start(out=bkr[:], in_=bk_i.ap()[0:1, :]).then_inc(s_w, 16)
                e.dma_start(out=bvr[:], in_=bv_i.ap()[0:1, :]).then_inc(s_w, 16)
            def fbs_of_tile(t):
                fl = [("k", 0, t), ("k", 1, t), ("v", 0, t), ("v", 1, t)]
                if t < NTQ:
                    fl += [("q", 0, t), ("q", 1, t)]
                return fl
            fb_j = {}
            fb_nq = {}
            fb_nv = {}
            _j = _nq = _nv = 0
            cons = []
            for t in range(NTK):
                for fb in fbs_of_tile(t):
                    fb_j[(fb[0], fb[1], t)] = _j
                    if fb[0] == "v":
                        fb_nv[(fb[1], t)] = _nv
                        cons.append(("ve", _nv + 1))
                        _nv += 1
                    else:
                        fb_nq[(fb[0], fb[1], t)] = _nq
                        cons.append(("qe", _nq + 1))
                        _nq += 1
                    _j += 1

            if eng == "sync":
                for i in range(NTK):
                    if i >= 2:
                        e.wait_ge(s_ln, i - 1)
                    e.dma_start(out=xin[:, (i % 2) * E:(i % 2 + 1) * E],
                                in_=x_in.ap()[i * 128:(i + 1) * 128, :]).then_inc(s_x1p[i % 2], 16)

            if eng == "vector":
                def _norm_x(i):
                    e.wait_ge(s_r1, i + 1)
                    if i >= 2:
                        e.wait_ge(s_tpa, i - 1)
                    xi = xin[:, (i % 2) * E:(i % 2 + 1) * E]
                    e.tensor_scalar(xlnb[:, (i % 2) * E:(i % 2 + 1) * E], xi,
                                    mnv[:, i:i + 1], rstb[:, i:i + 1],
                                    ALU.subtract, ALU.mult).then_inc(s_ln, 1)
                    e.drain()
                def _evac_xT(i):
                    e.wait_ge(s_tpa, i + 1)
                    pt = ps_tp[:, (i % 2) * 1024:(i % 2 + 1) * 1024]
                    e.tensor_copy(
                        xT.rearrange("p (c t) -> p c t", c=8)[:, :, i * 128:(i + 1) * 128],
                        pt.rearrange("p (c t) -> p c t", c=8)).then_inc(s_eva, 1)
                    e.drain()
                def _rope_fb(kind, half, t):
                    j = fb_j[(kind, half, t)]
                    nq = fb_nq[(kind, half, t)]
                    e.wait_ge(s_qkv, j + 1)
                    ps = ps_qkv[j % 4]
                    ct = (rqcs if kind == "q" else rkcs)[:, t * 32: t * 32 + 32]
                    stt = (rqss if kind == "q" else rkss)[:, t * 32: t * 32 + 32]
                    if nq >= 6:
                        e.wait_ge(s_tpb, nq - 5)
                    rsl = krb[:, (nq % 6) * 512:(nq % 6) * 512 + 512]
                    kv = ps.rearrange("p (h two f) -> p h two f", h=8, two=2)
                    rv = rsl.rearrange("p (h two f) -> p h two f", h=8, two=2)
                    sq3 = sqb[:, 0:512].rearrange("p (h two f) -> p h two f", h=8, two=2)
                    cB = ct.unsqueeze(1).broadcast_to([128, 8, 32])
                    sB = stt.unsqueeze(1).broadcast_to([128, 8, 32])
                    qe_, qo = kv[:, :, 0, :], kv[:, :, 1, :]
                    re_, ro = rv[:, :, 0, :], rv[:, :, 1, :]
                    e.tensor_tensor(re_, qe_, cB, ALU.mult)
                    e.tensor_tensor(ro, qo, cB, ALU.mult)
                    e.tensor_tensor(sq3[:, :, 0, :], qo, sB, ALU.mult)
                    e.tensor_tensor(sq3[:, :, 1, :], qe_, sB,
                                    ALU.mult).then_inc(s_qe, 1)
                    e.drain()
                    e.tensor_sub(re_, re_, sq3[:, :, 0, :])
                    e.tensor_add(ro, ro, sq3[:, :, 1, :]).then_inc(s_rope, 1)
                    e.drain()
                def _sq_fb(kind, half, t):
                    nq = fb_nq[(kind, half, t)]
                    rsl = krb[:, (nq % 6) * 512:(nq % 6) * 512 + 512]
                    e.tensor_tensor(sqb[:, 512:1024], rsl, rsl, ALU.mult)
                    e.drain()
                    e.tensor_reduce(nrmall[:, nq * 8: nq * 8 + 8],
                                    sqb[:, 512:1024].rearrange("p (h f) -> p h f", h=8),
                                    AX.X, ALU.add)
                    e.drain()
                for i in range(NTK + 3):
                    if 0 <= i - 1 < NTK:
                        _norm_x(i - 1)
                    if 0 <= i - 2 < NTK:
                        _evac_xT(i - 2)
                    if i < NTK:
                        e.wait_ge(s_x1p[i % 2], 16 * (i // 2 + 1))
                        xi = xin[:, (i % 2) * E:(i % 2 + 1) * E]
                        e.bn_stats(st12[:, 0:6], xi[:, 0:512])
                        e.bn_stats(st12[:, 6:12], xi[:, 512:1024])
                        e.drain()
                        e.bn_aggr(mv2[:], st12[:])
                        e.drain()
                        e.tensor_copy(mnv[:, i:i + 1], mv2[:, 0:1])
                        e.tensor_scalar_add(vrv[:, i:i + 1], mv2[:, 1:2],
                                            float(EPS)).then_inc(s_bn1, 1)
                        e.drain()
                    if 0 <= i - 3 < NTK:
                        for fb in fbs_of_tile(i - 3):
                            if fb[0] != "v":
                                _rope_fb(*fb)
                        for fb in fbs_of_tile(i - 3):
                            if fb[0] != "v":
                                _sq_fb(*fb)

            if eng == "scalar":
                def _evac_rope(r):
                    e.wait_ge(s_tpb, r + 1)
                    if r >= 2:
                        e.wait_ge(s_scwp[r % 2], 16 * (r // 2))
                    ptr = ps_rtp[:, (r % 2) * 1024:(r % 2) * 1024 + 512]
                    e.activation(ktst[:, (r % 2) * 512:(r % 2) * 512 + 512],
                                 ptr, AF.Copy).then_inc(s_kte, 1)
                nev = [0]
                for i in range(NTK + 4):
                    if i < NTK:
                        e.wait_ge(s_bn1, i + 1)
                        e.activation(rstb[:, i:i + 1], vrv[:, i:i + 1], AF.Ln)
                        e.drain()
                        e.activation(rstb[:, i:i + 1], rstb[:, i:i + 1], AF.Exp,
                                     scale=-0.5).then_inc(s_r1, 1)
                        e.drain()
                    if 0 <= i - 4 < NTK:
                        for fb in fbs_of_tile(i - 4):
                            if fb[0] != "v":
                                _evac_rope(nev[0])
                                nev[0] += 1
                    if 0 <= i - 3 < NTK:
                        for fb in fbs_of_tile(i - 3):
                            if fb[0] == "v":
                                j = fb_j[fb]
                                nv = fb_nv[(fb[1], fb[2])]
                                e.wait_ge(s_qkv, j + 1)
                                if nv >= 2:
                                    e.wait_ge(s_vwp[nv % 2], 16 * (nv // 2))
                                ps = ps_qkv[j % 4]
                                slot = vtl[:, (nv % 2) * 512:(nv % 2) * 512 + 512]
                                e.activation(slot, ps, AF.Copy).then_inc(s_ve, 1)
                while nev[0] < NROPE:
                    _evac_rope(nev[0])
                    nev[0] += 1

            if eng == "tensor":
                e.wait_ge(s_ci, 16 * NCI)
                e.wait_ge(s_w, 16 * NW)
                def _tp2(i):
                    e.wait_ge(s_ln, i + 1)
                    if i >= 2:
                        e.wait_ge(s_eva, i - 1)
                    srcx = xlnb[:, (i % 2) * E:(i % 2 + 1) * E]
                    pt = ps_tp[:, (i % 2) * 1024:(i % 2 + 1) * 1024]
                    for c in range(8):
                        mm = e.transpose(pt[:, c * 128:(c + 1) * 128],
                                         srcx[:, c * 128:(c + 1) * 128], identb[:])
                    mm.then_inc(s_tpa, 1)
                def _qkv_tile(t):
                    e.wait_ge(s_eva, t + 1)
                    for fb in fbs_of_tile(t):
                        kind, half, _t = fb
                        j = fb_j[fb]
                        if j >= 4:
                            sem, val = cons[j - 4]
                            e.wait_ge(s_ve if sem == "ve" else s_qe, val)
                        ps = ps_qkv[j % 4]
                        wsb = {"q": wq_sb, "k": wk_sb, "v": wv_sb}[kind]
                        brow = {"q": bqr, "k": bkr, "v": bvr}[kind]
                        for ec in range(ECH):
                            e.matmul(ps, xT[:, ec * 4096 + _t * 128: ec * 4096 + (_t + 1) * 128],
                                     wsb[:, ec * 1024 + half * 512: ec * 1024 + half * 512 + 512],
                                     start=(ec == 0), stop=False)
                        e.matmul(ps, onesbb[:], brow[:, half * 512: half * 512 + 512],
                                 start=False, stop=True).then_inc(s_qkv, 1)
                ntp = [0]
                def _rope_tp_tile(t):
                    for fb in fbs_of_tile(t):
                        if fb[0] == "v":
                            continue
                        r = ntp[0]
                        e.wait_ge(s_rope, r + 1)
                        if r >= 2:
                            e.wait_ge(s_kte, r - 1)
                        srcr = krb[:, (r % 6) * 512:(r % 6) * 512 + 512]
                        ptr = ps_rtp[:, (r % 2) * 1024:(r % 2) * 1024 + 512]
                        for c in range(4):
                            mm = e.transpose(ptr[:, c * 128:(c + 1) * 128],
                                             srcr[:, c * 128:(c + 1) * 128], identb[:])
                        mm.then_inc(s_tpb, 1)
                        ntp[0] += 1
                for i in range(NTK + 3):
                    if 0 <= i - 1 < NTK:
                        _tp2(i - 1)
                    if 0 <= i - 2 < NTK:
                        _qkv_tile(i - 2)
                    if 0 <= i - 3 < NTK:
                        _rope_tp_tile(i - 3)

            if eng == "gpsimd":
                r = 0
                nv = 0
                for t in range(NTK):
                    for (kind, half, i) in fbs_of_tile(t):
                        if kind == "v":
                            e.wait_ge(s_ve, nv + 1)
                            slot = vtl[:, (nv % 2) * 512:(nv % 2) * 512 + 512]
                            e.dma_start(out=v_scr.ap()[i * 128:(i + 1) * 128,
                                                       half * 512: half * 512 + 512],
                                        in_=slot).then_inc(s_vwp[nv % 2], 16)
                            nv += 1
                        else:
                            e.wait_ge(s_kte, r + 1)
                            scr = kt_scr if kind == "k" else qt_scr
                            st3 = ktst[:, (r % 2) * 512:(r % 2) * 512 + 512].rearrange(
                                "p (c t) -> p c t", c=4)
                            dst = scr.ap()[:, :].rearrange("(cc p) t -> p cc t", p=128)[
                                :, half * 4:(half + 1) * 4, i * 128:(i + 1) * 128]
                            e.dma_start(out=dst, in_=st3).then_inc(s_scwp[r % 2], 16)
                            r += 1

            if eng == "vector":
                nv3 = nrmall.rearrange("p (n h) -> p n h", h=8)
                def _cls_max(dst, idxs):
                    first = True
                    for (base, step, cnt) in idxs:
                        view = bass.AP(nrmall, base * 8,
                                       [[1, 128], [8 * step, cnt], [1, 8]]) if False else                             nv3[:, base:base + step * (cnt - 1) + 1:step, :]
                        e.tensor_reduce(nrm8[:], view.transpose([0, 2, 1]), AX.X, ALU.max)
                        e.drain()
                        if first:
                            e.tensor_copy(dst, nrm8[:])
                        else:
                            e.tensor_max(dst, dst, nrm8[:])
                        e.drain()
                        first = False
                _cls_max(km2[:, 0:8], [(0, 4, 16), (64, 2, 16)])
                _cls_max(km2[:, 8:16], [(1, 4, 16), (65, 2, 16)])
                _cls_max(qm2[:, 0:8], [(2, 4, 16)])
                _cls_max(qm2[:, 8:16], [(3, 4, 16)])
                e.tensor_copy(mx16[:, 0:16], qm2[:])
                e.tensor_copy(mx16[:, 16:32], km2[:]).then_inc(s_cb, 1)
                e.drain()
            if eng == "tensor":
                e.wait_ge(s_cb, 1)
                e.wait_ge(s_kte, NROPE)
                e.transpose(ps_tp[0:16, 0:128], mx16[:, 0:16], identb[:])
                e.transpose(ps_tp[0:16, 128:256], mx16[:, 16:32],
                            identb[:]).then_inc(s_cb, 1)
            if eng == "vector":
                e.wait_ge(s_cb, 2)
                e.tensor_reduce(c16[:, 0:1], ps_tp[0:16, 0:128], AX.X, ALU.max)
                e.tensor_reduce(c16[:, 1:2], ps_tp[0:16, 128:256], AX.X, ALU.max)
                e.drain()
                e.tensor_tensor(c16[:, 2:3], c16[:, 0:1], c16[:, 1:2],
                                ALU.mult).then_inc(s_cb, 1)
                e.drain()
            if eng == "scalar":
                e.wait_ge(s_cb, 3)
                e.activation(c16[:, 3:4], c16[:, 2:3], AF.Ln)
                e.drain()
                e.activation(c16[:, 3:4], c16[:, 3:4], AF.Exp, scale=0.5).then_inc(s_cb, 1)
                e.drain()
            if eng == "vector":
                e.wait_ge(s_cb, 4)
                e.tensor_scalar(cnegb[:], c16[:, 3:4], 1.0, -1.0,
                                ALU.add, ALU.mult).then_inc(s_cb, 1)
                e.drain()
            if eng == "tensor":
                e.wait_ge(s_cb, 5)
                e.matmul(ps_bc[0:1, 32:160], cnegb[:], identb[0:16, 0:128],
                         start=True, stop=True).then_inc(s_cb, 1)
            if eng == "vector":
                e.wait_ge(s_cb, 6)
                e.tensor_copy(cnr[:], ps_bc[0:1, 32:48]).then_inc(s_cb, 1)
                e.drain()
            if eng == "tensor":
                e.wait_ge(s_cb, 7)
                e.matmul(ps_bc[:, 0:16], onesbb[:], cnr[:],
                         start=True, stop=True).then_inc(s_cb, 1)
            if eng == "vector":
                e.wait_ge(s_cb, 8)
                e.tensor_copy(cbias[:], ps_bc[:, 0:16])
                e.drain()
                cv = cbias[:].rearrange("p (a b) -> p a b", b=2)
                e.tensor_tensor(cpairb[:], cv[:, :, 0], cv[:, :, 1],
                                ALU.min).then_inc(s_cb, 1)
                e.drain()

            for hp in range(HPn):
                if eng == "gpsimd":
                    if hp == 0:
                        e.wait_ge(s_scwp[0], 16 * (NROPE // 2))
                        e.wait_ge(s_scwp[1], 16 * (NROPE // 2))
                        e.wait_ge(s_vwp[0], 16 * (NV // 2))
                        e.wait_ge(s_vwp[1], 16 * (NV // 2))
                    if hp >= 2:
                        e.wait_ge(s_ctx, (hp - 1) * QBn)
                    s = hp % 2
                    e.dma_start(out=kts(s),
                                in_=kt_scr.ap()[hp * 128:(hp + 1) * 128, :]).then_inc(s_sbp[s], 16)
                    e.dma_start(out=vstr(s).rearrange("p (kc f) -> p kc f", kc=32),
                                in_=v_scr.ap()[:, :].rearrange("(kc p) f -> p kc f", p=128)[
                                    :, :, hp * 128:(hp + 1) * 128]).then_inc(s_sbp[s], 16)
                    e.dma_start(out=qts(s),
                                in_=qt_scr.ap()[hp * 128:(hp + 1) * 128, :]).then_inc(s_sbp[s], 16)
                for qb in range(QBn):
                    idx = hp * QBn + qb
                    s = hp % 2
                    if eng == "tensor" and idx == 0:
                        NID = HPn * QBn
                        def _pv(jdx, kc):
                            sj = (jdx // QBn) % 2
                            e.matmul(ps_ot[0:64, :],
                                     vstr(sj)[:, kc * 128: kc * 128 + 64],
                                     PT_A[:, kc * 512:(kc + 1) * 512],
                                     start=(kc == 0), stop=(kc == KC - 1),
                                     tile_position=(0, 0))
                            m2 = e.matmul(ps_ot[64:128, :],
                                     vstr(sj)[:, kc * 128 + 64: kc * 128 + 128],
                                     PT_B[:, kc * 512:(kc + 1) * 512],
                                     start=(kc == 0), stop=(kc == KC - 1),
                                     tile_position=(0, 64))
                            if kc == KC - 1:
                                m2.then_inc(s_pv, 1)
                        def _z(jdx, g2):
                            for tt, (pt, kc) in enumerate((
                                    (PT_A, 2 * g2), (PT_B, 2 * g2),
                                    (PT_A, 2 * g2 + 1), (PT_B, 2 * g2 + 1))):
                                mm = e.matmul(ps_zf[tt * 32:(tt + 1) * 32, :], zonesb[:],
                                              pt[:, kc * 512:(kc + 1) * 512],
                                              start=(g2 == 0),
                                              stop=(g2 == 15),
                                              tile_position=(0, tt * 32))
                            return mm
                        ZLAG, PVLAG = 5, 8
                        for t in range(NID * 32 + PVLAG):
                            jdx, kc = t // 32, t % 32
                            if jdx < NID:
                                hpj, qbj = jdx // QBn, jdx % QBn
                                sj = hpj % 2
                                if kc == 0 and qbj == 0:
                                    e.wait_ge(s_sbp[sj], 48 * (hpj // 2 + 1))
                                    if jdx == 0:
                                        e.wait_ge(s_cb, 9)
                                if kc == 5 and jdx > 0:
                                    e.wait_ge(s_zse, jdx)
                                    e.matmul(ps_zf, zindb[:], zse[:],
                                             start=True, stop=True).then_inc(s_fold, 1)
                                if t >= 3:
                                    e.wait_ge(s_exp, t - 2)
                                m = t % 3
                                e.matmul(ps_st[m][:, 0:512],
                                         kts(sj)[0:64, kc * 128:(kc + 1) * 128],
                                         qts(sj)[0:64, qbj * 512:(qbj + 1) * 512],
                                         start=True, stop=True, tile_position=(0, 0))
                                e.matmul(ps_st[m][:, 512:1024],
                                         kts(sj)[64:128, kc * 128:(kc + 1) * 128],
                                         qts(sj)[64:128, qbj * 512:(qbj + 1) * 512],
                                         start=True, stop=True,
                                         tile_position=(64, 0)).then_inc(s_sc, 1)
                            u = t - ZLAG
                            if u >= 0 and u < NID * 32 and u % 2 == 1:
                                pdxz, uz = u // 32, u % 32
                                p = uz // 2
                                if jdx >= NID:
                                    e.wait_ge(s_exp, pdxz * 32 + 2 * p + 2)
                                if p == 0 and pdxz > 0:
                                    e.wait_ge(s_rz, pdxz)
                                zm = _z(pdxz, p)
                                if p == 15:
                                    zm.then_inc(s_pvz, 1)
                            v = t - PVLAG
                            if v >= 0:
                                pdx, pkc = v // 32, v % 32
                                if jdx >= NID:
                                    e.wait_ge(s_exp, pdx * 32 + pkc + 1)
                                if pkc == 0 and pdx > 0:
                                    e.wait_ge(s_ctx, pdx)
                                _pv(pdx, pkc)
                        e.wait_ge(s_zse, NID)
                        e.wait_ge(s_rz, NID - 1)
                        e.matmul(ps_zf, zindb[:], zse[:],
                                 start=True, stop=True).then_inc(s_fold, 1)
                    if eng == "scalar":
                        for kc in range(KC):
                            if kc == 6 and idx > 0:
                                j = idx - 1
                                e.wait_ge(s_fold, idx)
                                if j >= 1:
                                    e.wait_ge(s_ctx, j)
                                e.activation(zltv, ps_zf, AF.Ln)
                                e.drain()
                                e.activation(rzb[:, (j % 2) * 512:(j % 2) * 512 + 512],
                                             zltv, AF.Exp, scale=-1.0).then_inc(s_rz, 1)
                                e.drain()
                            e.wait_ge(s_sc, idx * 32 + kc + 1)
                            m = (idx * 32 + kc) % 3
                            outv = BIG_A[:].rearrange("p (h x) -> p h x", h=2)[
                                :, :, kc * 512:(kc + 1) * 512]
                            e.activation(outv, ps_st[m], AF.Exp,
                                         bias=cpairb[:, hp:hp + 1]).then_inc(s_exp, 1)
                        if idx == HPn * QBn - 1:
                            j = idx
                            e.wait_ge(s_fold, idx + 1)
                            e.wait_ge(s_ctx, j)
                            e.activation(zltv, ps_zf, AF.Ln)
                            e.drain()
                            e.activation(rzb[:, (j % 2) * 512:(j % 2) * 512 + 512],
                                         zltv, AF.Exp, scale=-1.0).then_inc(s_rz, 1)
                            e.drain()
                    if eng == "vector":
                        if idx > 0:
                            j = idx - 1
                            hpj, qbj = j // QBn, j % QBn
                            e.wait_ge(s_rz, j + 1)
                            e.wait_ge(s_pv, j + 1)
                            e.tensor_tensor(
                                ctxT[:, hpj * 2048 + qbj * 512: hpj * 2048 + (qbj + 1) * 512],
                                ps_ot,
                                rzb[:, (j % 2) * 512:(j % 2) * 512 + 512],
                                ALU.mult).then_inc(s_ctx, 1)
                            e.drain()
                        e.wait_ge(s_pvz, idx + 1)
                        e.tensor_copy(zse[:], ps_zf).then_inc(s_zse, 1)
                        e.drain()

            if eng == "vector":
                j = HPn * QBn - 1
                hpj, qbj = j // QBn, j % QBn
                e.wait_ge(s_rz, j + 1)
                e.wait_ge(s_pv, j + 1)
                e.tensor_tensor(
                    ctxT[:, hpj * 2048 + qbj * 512: hpj * 2048 + (qbj + 1) * 512],
                    ps_ot,
                    rzb[:, (j % 2) * 512:(j % 2) * 512 + 512],
                    ALU.mult).then_inc(s_ctx, 1)
                e.drain()

            if eng == "gpsimd":
                e.dma_start(out=w1q_scr.ap()[:, :],
                            in_=w1T.ap()[:, :]).then_inc(s_w1c, 16)
                e.dma_start(out=w2q_scr.ap()[:, :],
                            in_=w2T.ap()[:, :]).then_inc(s_w2c, 16)

            if eng == "sync":
                e.wait_ge(s_ctx, HPn * QBn)
                e.wait_ge(s_rz, HPn * QBn)
                e.wait_ge(s_w1c, 16)
                e.wait_ge(s_w2c, 16)
                for wh in range(2):
                    e.dma_start(
                        out=w1T_sb.rearrange("p (c f) -> p c f", c=8)[:, :, wh * 2048:(wh + 1) * 2048],
                        in_=w1q_scr.ap()[:, :].rearrange("(c p) f -> p c f", p=128)[:, :, wh * 2048:(wh + 1) * 2048]
                        ).then_inc(s_w1h[wh], 16)
                e.dma_start(out=w2_sb.rearrange("p (c f) -> p c f", c=32),
                            in_=w2q_scr.ap()[:, :].rearrange("(c p) f -> p c f", p=128)
                            ).then_inc(s_w2p[0], 16)
                e.dma_start(out=b2qb,
                            in_=b2_i.ap()[0:1, :].partition_broadcast(128)).then_inc(s_b2, 16)
                for t in range(NTQ):
                    if t >= 2:
                        e.wait_ge(s_x2wp[t % 2], 16 * (t // 2))
                        e.wait_ge(s_ln3, min(8, t - 1))
                        e.wait_ge(s_bnd, t - 1)
                    e.dma_start(out=outb[:, (t % 2) * E:(t % 2 + 1) * E],
                                in_=x_in.ap()[t * 128:(t + 1) * 128, :]).then_inc(s_xrp[t % 2], 16)
            if eng == "tensor":
                ps_o6 = P8[:, 3072:4096]
                def _pso(t, dh):
                    if t % 3 == 2:
                        return ps_o6[:, dh * 512:(dh + 1) * 512]
                    return ps_qkv[(t % 3) * 2 + dh]
                def _tp3(i):
                    e.wait_ge(s_ln3, i + 1)
                    if i == 0:
                        e.wait_ge(s_exp, HPn * QBn * 32)
                    if i >= 2:
                        e.wait_ge(s_evd, i - 1)
                    src3 = xlnb[:, (i % 2) * E:(i % 2 + 1) * E]
                    pt3 = ps_tp[:, (i % 2) * 1024:(i % 2 + 1) * 1024]
                    for c in range(8):
                        mm3 = e.transpose(pt3[:, c * 128:(c + 1) * 128],
                                          src3[:, c * 128:(c + 1) * 128], identb[:])
                    mm3.then_inc(s_tpd, 1)
                for t in range(NTQ):
                    if t == 2:
                        e.wait_ge(s_zse, HPn * QBn)
                        e.wait_ge(s_rz, HPn * QBn)
                        e.wait_ge(s_ctx, HPn * QBn)
                    if 0 <= t - 2 < 8:
                        _tp3(t - 2)
                    if t >= 3:
                        e.wait_ge(s_x2n, t - 2)
                    for dh in range(2):
                        ps = _pso(t, dh)
                        for c in range(ECH):
                            e.matmul(ps, ctxT[:, c * 2048 + t * 128: c * 2048 + (t + 1) * 128],
                                     wo_sb[:, c * 1024 + dh * 512: c * 1024 + dh * 512 + 512],
                                     start=(c == 0), stop=False)
                        mm = e.matmul(ps, onesbb[:], bobr[:, dh * 512: dh * 512 + 512],
                                      start=False, stop=True)
                    mm.then_inc(s_op, 1)
            if eng == "vector":
                ps_o6v = P8[:, 3072:4096]
                def _psov(t, dh):
                    if t % 3 == 2:
                        return ps_o6v[:, dh * 512:(dh + 1) * 512]
                    return ps_qkv[(t % 3) * 2 + dh]
                for t in range(NTQ):
                    e.wait_ge(s_op, t + 1)
                    p0 = _psov(t, 0)
                    p1 = _psov(t, 1)
                    e.bn_stats(st12[:, 0:6], p0)
                    e.bn_stats(st12[:, 6:12], p1)
                    e.drain()
                    e.bn_aggr(mv2[:], st12[:])
                    e.drain()
                    e.tensor_scalar_add(rs1[:], mv2[:, 1:2], float(EPS)).then_inc(s_bnc, 1)
                    e.drain()
                    e.wait_ge(s_rsc, t + 1)
                    e.wait_ge(s_xrp[t % 2], 16 * (t // 2 + 1))
                    for dh in range(2):
                        ps = _psov(t, dh)
                        e.tensor_scalar(tmpn[:, dh * 512: dh * 512 + 512], ps,
                                        mv2[:, 0:1], rs1[:], ALU.subtract, ALU.mult)
                    e.drain()
                    e.tensor_add(outb[:, (t % 2) * E:(t % 2 + 1) * E],
                                 tmpn[:],
                                 outb[:, (t % 2) * E:(t % 2 + 1) * E]).then_inc(s_x2n, 1)
                    e.drain()
                    c3 = t if t < 8 else t + 8
                    ob = outb[:, (t % 2) * E:(t % 2 + 1) * E]
                    e.bn_stats(st12[:, 0:6], ob[:, 0:512])
                    e.bn_stats(st12[:, 6:12], ob[:, 512:1024])
                    e.drain()
                    e.bn_aggr(mv2[:], st12[:])
                    e.drain()
                    e.tensor_copy(mnv[:, c3:c3 + 1], mv2[:, 0:1])
                    e.tensor_scalar_add(vrv[:, c3:c3 + 1], mv2[:, 1:2],
                                        float(EPS)).then_inc(s_bnd, 1)
                    e.drain()
                    if 0 <= t - 1 < 8:
                        i = t - 1
                        e.wait_ge(s_rs3, i + 1)
                        if i >= 2:
                            e.wait_ge(s_tpd, i - 1)
                        ob = outb[:, (i % 2) * E:(i % 2 + 1) * E]
                        e.tensor_scalar(xlnb[:, (i % 2) * E:(i % 2 + 1) * E], ob,
                                        mnv[:, i:i + 1], rstb[:, i:i + 1],
                                        ALU.subtract, ALU.mult).then_inc(s_ln3, 1)
                        e.drain()
            if eng == "scalar":
                def _evac3a(i):
                    e.wait_ge(s_tpd, i + 1)
                    pt3 = ps_tp[:, (i % 2) * 1024:(i % 2 + 1) * 1024]
                    e.activation(
                        xT2.rearrange("p (c t) -> p c t", c=8)[:, :, i * 128:(i + 1) * 128],
                        pt3.rearrange("p (c t) -> p c t", c=8), AF.Copy).then_inc(s_evd, 1)
                for t in range(NTQ):
                    e.wait_ge(s_bnc, t + 1)
                    e.activation(rs1[:], rs1[:], AF.Ln)
                    e.drain()
                    e.activation(rs1[:], rs1[:], AF.Exp, scale=-0.5).then_inc(s_rsc, 1)
                    e.drain()
                    if 0 <= t - 3 < 8:
                        _evac3a(t - 3)
                    if 0 <= t - 1 < NTQ:
                        i3 = t - 1
                        cc = i3 if i3 < 8 else i3 + 8
                        e.wait_ge(s_bnd, i3 + 1)
                        e.activation(rstb[:, cc:cc + 1], vrv[:, cc:cc + 1], AF.Ln)
                        e.drain()
                        e.activation(rstb[:, cc:cc + 1], rstb[:, cc:cc + 1], AF.Exp,
                                     scale=-0.5).then_inc(s_rs3, 1)
                        e.drain()
                    if t == NTQ - 1:
                        e.wait_ge(s_bnd, NTQ)
                        e.activation(rstb[:, 23:24], vrv[:, 23:24], AF.Ln)
                        e.drain()
                        e.activation(rstb[:, 23:24], rstb[:, 23:24], AF.Exp,
                                     scale=-0.5).then_inc(s_rs3, 1)
                        e.drain()
            if eng == "gpsimd":
                for t in range(NTQ):
                    e.wait_ge(s_x2n, t + 1)
                    e.dma_start(out=x2_scr.ap()[t * 128:(t + 1) * 128, :],
                                in_=outb[:, (t % 2) * E:(t % 2 + 1) * E]).then_inc(s_x2wp[t % 2], 16)

            for th in range(2):
                tb0 = th * 8
                if eng == "sync" and th == 1:
                    e.wait_ge(s_o2f, 8)
                    for i in range(8):
                        t = tb0 + i
                        g = th * 8 + i
                        e.wait_ge(s_x2wp[t % 2], 16 * (t // 2 + 1))
                        if i >= 2:
                            e.wait_ge(s_ln3, th * 8 + i - 1)
                        e.dma_start(out=xin[:, (i % 2) * E:(i % 2 + 1) * E],
                                    in_=x2_scr.ap()[t * 128:(t + 1) * 128, :]).then_inc(s_l3sp[i % 2], 16)
                if eng == "vector" and th == 1:
                    def _norm3(i):
                        e.wait_ge(s_rs3, th * 8 + i + 1)
                        if i >= 2:
                            e.wait_ge(s_tpd, th * 8 + i - 1)
                        xi = xin[:, (i % 2) * E:(i % 2 + 1) * E]
                        e.tensor_scalar(xlnb[:, (i % 2) * E:(i % 2 + 1) * E], xi,
                                        mnv[:, 16 + i:17 + i], rstb[:, 16 + i:17 + i],
                                        ALU.subtract, ALU.mult).then_inc(s_ln3, 1)
                        e.drain()
                    def _evac3(i):
                        e.wait_ge(s_tpd, th * 8 + i + 1)
                        pt = ps_tp[:, (i % 2) * 1024:(i % 2 + 1) * 1024]
                        e.tensor_copy(
                            xT2.rearrange("p (c t) -> p c t", c=8)[:, :, i * 128:(i + 1) * 128],
                            pt.rearrange("p (c t) -> p c t", c=8)).then_inc(s_evd, 1)
                        e.drain()
                    for i in range(8):
                        e.wait_ge(s_l3sp[i % 2], 16 * (i // 2 + 1))
                        _norm3(i)
                        if i >= 2:
                            _evac3(i - 2)
                    _evac3(6)
                    _evac3(7)
                if eng == "tensor" and th == 1:
                    for i in range(8):
                        e.wait_ge(s_ln3, th * 8 + i + 1)
                        if i >= 2:
                            e.wait_ge(s_evd, th * 8 + i - 1)
                        src = xlnb[:, (i % 2) * E:(i % 2 + 1) * E]
                        pt = ps_tp[:, (i % 2) * 1024:(i % 2 + 1) * 1024]
                        for c in range(8):
                            mm = e.transpose(pt[:, c * 128:(c + 1) * 128],
                                             src[:, c * 128:(c + 1) * 128], identb[:])
                        mm.then_inc(s_tpd, 1)
                if eng == "tensor":
                    if th == 0:
                        e.wait_ge(s_x2n, NTQ)
                        e.wait_ge(s_zse, HPn * QBn)
                        e.wait_ge(s_rz, HPn * QBn)
                        e.wait_ge(s_ctx, HPn * QBn)
                    def _mm1(tb, hc):
                        j = tb * HCn + hc
                        if th == 0 and tb == 0 and hc in (0, 16):
                            e.wait_ge(s_w1h[hc // 16], 16)
                        if j >= 4:
                            e.wait_ge(s_hq, th * 64 + j - 3)
                        ps = ps_qkv[j % 4]
                        for ec in range(ECH):
                            mm = e.matmul(ps,
                                          w1T_sb[:, ec * 4096 + hc * 128: ec * 4096 + (hc + 1) * 128],
                                          xT2[:, ec * 1024 + tb * 512: ec * 1024 + tb * 512 + 512],
                                          start=(ec == 0), stop=(ec == ECH - 1))
                        mm.then_inc(s_mm1, 1)
                    hq3 = hqT.rearrange("p (hc x) -> p hc x", hc=HCn)
                    w23 = w2_sb.rearrange("p (hc x) -> p hc x", hc=HCn)
                    def _mm2(i):
                        if th == 0 and i == 0:
                            e.wait_ge(s_w2p[0], 16)
                        e.wait_ge(s_hq, th * 64 + (i // 4 + 1) * 32)
                        if th * 8 + i >= 1:
                            e.wait_ge(s_o2, th * 8 + i)
                        for dh in range(2):
                            ps = ps_m2[dh]
                            for h in range(HCn // 2):
                                mm = e.matmul(ps,
                                              hq3[:, 2 * h:2 * h + 2, i * 128:(i + 1) * 128],
                                              w23[:, 2 * h:2 * h + 2, dh * 512:dh * 512 + 512],
                                              start=(h == 0), stop=(h == HCn // 2 - 1),
                                              perf_mode=mybir.MatmulPerfMode.DoubleRow)
                        mm.then_inc(s_mm2, 1)
                    e.wait_ge(s_evd, th * 8 + 4)
                    for hc in range(HCn):
                        _mm1(0, hc)
                    e.wait_ge(s_evd, th * 8 + 8)
                    for i in range(4):
                        _mm2(i)
                        for hc in range(8 * i, 8 * i + 8):
                            _mm1(1, hc)
                    for i in range(4, 8):
                        _mm2(i)
                if eng == "sync":
                    for i in range(8):
                        t = tb0 + i
                        e.wait_ge(s_ln3, (th + 1) * 8)
                        if i >= 2:
                            e.wait_ge(s_o2f, th * 8 + i - 1)
                        g = th * 8 + i
                        e.dma_start(out=xin[:, (i % 2) * E:(i % 2 + 1) * E],
                                    in_=x2_scr.ap()[t * 128:(t + 1) * 128, :]).then_inc(s_frp[g % 2], 16)
                if eng == "vector":
                    if th == 0:
                        e.wait_ge(s_x2wp[0], 16 * (NTQ // 2))
                        e.wait_ge(s_x2wp[1], 16 * (NTQ // 2))
                        e.wait_ge(s_b2, 16)
                        stg = o2q[:, 7 * 1024:8 * 1024]
                        e.tensor_copy(stg, b2qb)
                        e.drain()
                        e.tensor_copy(b2qb, stg)
                        e.drain()
                    def _pass1(i):
                        e.wait_ge(s_mm2, th * 8 + i + 1)
                        ot = o2q[:, i * 1024:(i + 1) * 1024]
                        for dh in range(2):
                            ps = ps_m2[dh]
                            e.tensor_tensor(ot[:, dh * 512: dh * 512 + 512], ps,
                                            b2qb[:, dh * 512: dh * 512 + 512], ALU.add)
                        e.drain()
                        e.bn_stats(st12[:, 0:6], ot[:, 0:512])
                        e.bn_stats(st12[:, 6:12], ot[:, 512:1024])
                        e.drain()
                        e.bn_aggr(mv2[:], st12[:])
                        e.drain()
                        e.tensor_copy(mnv[:, 8 + i:9 + i], mv2[:, 0:1])
                        e.tensor_scalar_add(vrv[:, 8 + i:9 + i], mv2[:, 1:2],
                                            float(EPS)).then_inc(s_o2, 1)
                        e.drain()
                    def _pass2(i):
                        g = th * 8 + i
                        ot = o2q[:, i * 1024:(i + 1) * 1024]
                        e.wait_ge(s_rs4, g + 1)
                        e.wait_ge(s_frp[g % 2], 16 * (g // 2 + 1))
                        if g >= 2:
                            e.wait_ge(s_outp[g % 2], 16 * (g // 2))
                        e.tensor_scalar(tmpn[:], ot, mnv[:, 8 + i:9 + i], rstb[:, 8 + i:9 + i],
                                        ALU.subtract, ALU.mult)
                        e.drain()
                        e.tensor_add(outb[:, (i % 2) * E:(i % 2 + 1) * E],
                                     tmpn[:],
                                     xin[:, (i % 2) * E:(i % 2 + 1) * E]).then_inc(s_o2f, 1)
                        e.drain()
                    for i in range(8):
                        _pass1(i)
                        if i >= 2:
                            _pass2(i - 2)
                    _pass2(6)
                    _pass2(7)
                if eng == "scalar":
                    def _rstd4(i):
                        e.wait_ge(s_o2, th * 8 + i + 1)
                        e.activation(rstb[:, 8 + i:9 + i], vrv[:, 8 + i:9 + i], AF.Ln)
                        e.drain()
                        e.activation(rstb[:, 8 + i:9 + i], rstb[:, 8 + i:9 + i], AF.Exp,
                                     scale=-0.5).then_inc(s_rs4, 1)
                        e.drain()
                    n4 = [0]
                    for tb in range(2):
                        for hc in range(HCn):
                            j = tb * HCn + hc
                            e.wait_ge(s_mm1, th * 64 + j + 1)
                            e.activation(
                                hqT[:, hc * 1024 + tb * 512: hc * 1024 + tb * 512 + 512],
                                ps_qkv[j % 4], AF.Relu,
                                bias=b1s[:, hc:hc + 1]).then_inc(s_hq, 1)
                            if tb == 1 and hc % 8 == 7:
                                _rstd4(n4[0])
                                n4[0] += 1
                    while n4[0] < 8:
                        _rstd4(n4[0])
                        n4[0] += 1
                if eng == "gpsimd":
                    for i in range(8):
                        t = tb0 + i
                        g = th * 8 + i
                        e.wait_ge(s_o2f, g + 1)
                        e.dma_start(out=out_o.ap()[t * 128:(t + 1) * 128, :],
                                    in_=outb[:, (i % 2) * E:(i % 2 + 1) * E]).then_inc(s_outp[g % 2], 16)

            if eng == "gpsimd":
                e.wait_ge(s_outp[0], 16 * 8)
                e.wait_ge(s_outp[1], 16 * 8)

        @block.sync
        def _(e):
            emit("sync", e)

        @block.gpsimd
        def _(e):
            emit("gpsimd", e)

        @block.vector
        def _(e):
            emit("vector", e)

        @block.scalar
        def _(e):
            emit("scalar", e)

        @block.tensor
        def _(e):
            emit("tensor", e)

    return nc


_CACHE = {}


def _tables(positions, scale):
    inv = 1.0 / (10000.0 ** (np.arange(0, HD, 2, dtype=np.float64) / HD))
    ang = positions[:, None].astype(np.float64) * inv[None, :]
    c = (np.cos(ang) * scale).astype(np.float32)
    s = (np.sin(ang) * scale).astype(np.float32)
    T = positions.shape[0]
    nt = T // 128
    cm = np.ascontiguousarray(c.reshape(nt, 128, 32).transpose(1, 0, 2).reshape(128, nt * 32))
    sm = np.ascontiguousarray(s.reshape(nt, 128, 32).transpose(1, 0, 2).reshape(128, nt * 32))
    return cm, sm


def kernel(L0_attn_x, L0_attn_Wq, L0_attn_bq, L0_attn_Wk, L0_attn_bk,
           L0_attn_Wv, L0_attn_bv, L0_attn_Wo, L0_attn_bo,
           L0_ffn_w1, L0_ffn_b1, L0_ffn_w2, L0_ffn_b2):
    x = np.ascontiguousarray(np.asarray(L0_attn_x, dtype=np.float32))
    B, Lfull, _ = x.shape

    perm = []
    for h in range(NH):
        perm += [h * HD + 2 * i for i in range(HD // 2)]
        perm += [h * HD + 2 * i + 1 for i in range(HD // 2)]
    perm = np.asarray(perm)

    wqT = np.ascontiguousarray(np.asarray(L0_attn_Wq, dtype=np.float32)[perm, :].T)
    wkT = np.ascontiguousarray(np.asarray(L0_attn_Wk, dtype=np.float32)[perm, :].T)
    wvT = np.ascontiguousarray(np.asarray(L0_attn_Wv, dtype=np.float32).T)
    woT = np.ascontiguousarray(np.asarray(L0_attn_Wo, dtype=np.float32).T)
    w1T = np.ascontiguousarray(np.asarray(L0_ffn_w1, dtype=np.float32).T)
    w2T = np.ascontiguousarray(np.asarray(L0_ffn_w2, dtype=np.float32).T)
    bq = np.ascontiguousarray(np.asarray(L0_attn_bq, dtype=np.float32)[perm][None, :])
    bk = np.ascontiguousarray(np.asarray(L0_attn_bk, dtype=np.float32)[perm][None, :])
    bv = np.ascontiguousarray(np.asarray(L0_attn_bv, dtype=np.float32)[None, :])
    bo = np.ascontiguousarray(np.asarray(L0_attn_bo, dtype=np.float32)[None, :])
    b2 = np.ascontiguousarray(np.asarray(L0_ffn_b2, dtype=np.float32)[None, :])
    b1 = np.ascontiguousarray(
        np.asarray(L0_ffn_b1, dtype=np.float32).reshape(HCn, 128).T)

    if "nc" not in _CACHE:
        _CACHE["nc"] = build_nc()
    nc = _CACHE["nc"]

    in_maps = []
    for core in range(8):
        b, h = core // 2, core % 2
        own = x[b, h * TQ:(h + 1) * TQ]
        oth = x[b, (1 - h) * TQ:(2 - h) * TQ]
        x_core = np.ascontiguousarray(np.concatenate([own, oth], axis=0))
        posQ = np.arange(h * TQ, (h + 1) * TQ)
        posK = np.concatenate([posQ, np.arange((1 - h) * TQ, (2 - h) * TQ)])
        rqc, rqs = _tables(posQ, 1.0 / np.sqrt(HD))
        rkc, rks = _tables(posK, 1.0)
        in_maps.append({
            "x_in": x_core, "wqT": wqT, "wkT": wkT, "wvT": wvT, "woT": woT,
            "w1T": w1T, "w2T": w2T, "bq_i": bq, "bk_i": bk, "bv_i": bv,
            "bo_i": bo, "b2_i": b2, "b1_i": b1,
            "rqc_i": rqc, "rqs_i": rqs, "rkc_i": rkc, "rks_i": rks,
        })

    res = run_bass_kernel_spmd(nc, in_maps, core_ids=list(range(8)))
    out = np.empty((B, Lfull, E), dtype=np.float32)
    for core in range(8):
        b, h = core // 2, core % 2
        out[b, h * TQ:(h + 1) * TQ] = res.results[core]["out"]
    return out


# version 35
# speedup vs baseline: 1.0126x; 1.0126x over previous
import numpy as np
import ml_dtypes
import concourse.bass as bass
import concourse.mybir as mybir
from concourse.bass_utils import run_bass_kernel_spmd

F32 = mybir.dt.float32
BF16 = mybir.dt.bfloat16
FP16 = mybir.dt.float16
E4M3 = mybir.dt.float8e4
AF = mybir.ActivationFunctionType
ALU = mybir.AluOpType
AX = mybir.AxisListType

E = 1024
NH = 16
HD = 64
L = 4096
TQ = 2048
NTK = 32
NTQ = 16
ECH = 8
HPn = 8
QBn = 4
KC = 32
HF = 4096
HCn = 32
EPS = 1e-5


def build_nc():
    nc = bass.Bass("TRN2", target_bir_lowering=False)

    x_in = nc.dram_tensor("x_in", [L, E], F32, kind="ExternalInput")
    wqT = nc.dram_tensor("wqT", [E, E], F32, kind="ExternalInput")
    wkT = nc.dram_tensor("wkT", [E, E], F32, kind="ExternalInput")
    wvT = nc.dram_tensor("wvT", [E, E], F32, kind="ExternalInput")
    woT = nc.dram_tensor("woT", [E, E], F32, kind="ExternalInput")
    w1T = nc.dram_tensor("w1T", [E, HF], F32, kind="ExternalInput")
    w2T = nc.dram_tensor("w2T", [HF, E], F32, kind="ExternalInput")
    bq_i = nc.dram_tensor("bq_i", [1, E], F32, kind="ExternalInput")
    bk_i = nc.dram_tensor("bk_i", [1, E], F32, kind="ExternalInput")
    bv_i = nc.dram_tensor("bv_i", [1, E], F32, kind="ExternalInput")
    bo_i = nc.dram_tensor("bo_i", [1, E], F32, kind="ExternalInput")
    b2_i = nc.dram_tensor("b2_i", [1, E], F32, kind="ExternalInput")
    b1_i = nc.dram_tensor("b1_i", [128, HCn], F32, kind="ExternalInput")
    rqc_i = nc.dram_tensor("rqc_i", [128, NTQ * 32], F32, kind="ExternalInput")
    rqs_i = nc.dram_tensor("rqs_i", [128, NTQ * 32], F32, kind="ExternalInput")
    rkc_i = nc.dram_tensor("rkc_i", [128, NTK * 32], F32, kind="ExternalInput")
    rks_i = nc.dram_tensor("rks_i", [128, NTK * 32], F32, kind="ExternalInput")
    out_o = nc.dram_tensor("out", [TQ, E], F32, kind="ExternalOutput")

    kt_scr = nc.dram_tensor("kt_scr", [E, L], BF16)
    qt_scr = nc.dram_tensor("qt_scr", [E, TQ], BF16)
    v_scr = nc.dram_tensor("v_scr", [L, E], BF16)
    x2_scr = nc.dram_tensor("x2_scr", [TQ, E], F32)
    w1q_scr = nc.dram_tensor("w1q_scr", [E, HF], BF16)
    w2q_scr = nc.dram_tensor("w2q_scr", [HF, E], E4M3)

    identN = np.eye(128, dtype=ml_dtypes.bfloat16)
    zindN = np.zeros((128, 128), dtype=np.float32)
    zindN[0, 0:64] = 1.0
    zindN[64, 0:64] = 1.0
    zindN[32, 64:128] = 1.0
    zindN[96, 64:128] = 1.0
    zonesN = np.zeros((128, 32), dtype=ml_dtypes.bfloat16)
    zonesN[:, 0] = 1.0
    ident_t = nc.inline_tensor(identN, "ident_t")
    zind_t = nc.inline_tensor(zindN, "zind_t")
    zones_t = nc.inline_tensor(zonesN, "zones_t")
    onesb_t = nc.inline_tensor(np.ones((1, 128), dtype=ml_dtypes.bfloat16), "onesb_t")
    onesf_t = nc.inline_tensor(np.ones((1, 128), dtype=np.float32), "onesf_t")

    from contextlib import ExitStack
    with ExitStack() as _ctx:
        _sb = lambda *a, **k: _ctx.enter_context(nc.sbuf_tensor(*a, **k))
        _ps = lambda *a, **k: _ctx.enter_context(nc.psum_tensor(*a, **k))
        _sem = lambda n: _ctx.enter_context(nc.semaphore(n))
        BIG_A = _sb("BIG_A", [128, 32768], BF16)
        BIG_B = _sb("BIG_B", [128, 32768], BF16)
        CTXB = _sb("CTX", [128, 16384], BF16)
        QTS = _sb("QTS", [128, 2 * TQ], BF16)
        xin = _sb("xin", [128, 2 * E], F32)
        xlnb = _sb("xlnb", [128, 2 * E], BF16)
        krb = _sb("krb", [128, 6 * 512], BF16)
        mnv = _sb("mnv", [128, 32], F32)
        vrv = _sb("vrv", [128, 32], F32)
        rstb = _sb("rstb", [128, 32], F32)
        st12 = _sb("st12", [128, 12], F32)
        mv2 = _sb("mv2", [128, 2], F32)
        rs1 = _sb("rs1", [128, 1], F32)
        qm2 = _sb("qm2", [128, 16], F32)
        km2 = _sb("km2", [128, 16], F32)
        nrm8 = _sb("nrm8", [128, 8], F32)
        mx16 = _sb("mx16", [128, 32], BF16)
        c16 = _sb("c16", [16, 4], F32)
        cnegb = _sb("cnegb", [16, 1], BF16)
        cnr = _sb("cnr", [1, 16], BF16)
        tmpn = _sb("tmpn", [128, E], F32)
        cbias = _sb("cbias", [128, 16], F32)
        cpairb = _sb("cpairb", [128, 8], F32)
        zindb = _sb("zindb", [128, 128], F32)
        zonesb = _sb("zonesb", [128, 32], BF16)
        identb = _sb("identb", [128, 128], BF16)
        onesbb = _sb("onesbb", [1, 128], BF16)
        onesfb = _sb("onesfb", [1, 128], F32)
        bobr = _sb("bobr", [1, E], BF16)
        bqr = _sb("bqr", [1, E], BF16)
        bkr = _sb("bkr", [1, E], BF16)
        bvr = _sb("bvr", [1, E], BF16)
        b1s = _sb("b1s", [128, HCn], F32)
        zse = _sb("zse", [128, 512], F32)
        rzb = _sb("rzb", [128, 1024], F32)
        P8 = _ps("P8", [128, 4096], F32)
        s_ci = _sem("s_ci")
        s_w = _sem("s_w")
        s_b2 = _sem("s_b2")
        s_x1p = [_sem("s_x1a"), _sem("s_x1b")]
        s_bn1 = _sem("s_bn1")
        s_r1 = _sem("s_r1")
        s_x2p = [_sem("s_x2a"), _sem("s_x2b")]
        s_ln = _sem("s_ln")
        s_tpa = _sem("s_tpa")
        s_eva = _sem("s_eva")
        s_qkv = _sem("s_qkv")
        s_qe = _sem("s_qe")
        s_rope = _sem("s_rope")
        s_tpb = _sem("s_tpb")
        s_kte = _sem("s_kte")
        s_scwp = [_sem("s_scwa"), _sem("s_scwb")]
        s_ve = _sem("s_ve")
        s_nrm = _sem("s_nrm")
        s_o2s = _sem("s_o2s")
        s_o2f = _sem("s_o2f")
        s_vwp = [_sem("s_vwa"), _sem("s_vwb")]
        s_cb = _sem("s_cb")
        s_sbp = [_sem("s_sba"), _sem("s_sbb")]
        s_sc = _sem("s_sc")
        s_exp = _sem("s_exp")
        s_pvz = _sem("s_pvz")
        s_pv = _sem("s_pv")
        s_zse = _sem("s_zse")
        s_fold = _sem("s_fold")
        s_rz = _sem("s_rz")
        s_ctx = _sem("s_ctx")
        s_op = _sem("s_op")
        s_bnc = _sem("s_bnc")
        s_rsc = _sem("s_rsc")
        s_xrp = [_sem("s_xra"), _sem("s_xrb")]
        s_x2n = _sem("s_x2n")
        s_x2wp = [_sem("s_x2wa"), _sem("s_x2wb")]
        s_w1 = _sem("s_w1")
        s_w2p = [_sem("s_w2a"), _sem("s_w2b")]
        s_w2c = _sem("s_w2c")
        s_w1c = _sem("s_w1c")
        s_w1h = [_sem("s_w1ha"), _sem("s_w1hb")]
        s_l3sp = [_sem("s_l3sa"), _sem("s_l3sb")]
        s_bnd = _sem("s_bnd")
        s_b3e = _sem("s_b3e")
        s_rs3 = _sem("s_rs3")
        s_l3ap = [_sem("s_l3aa"), _sem("s_l3ab")]
        s_ln3 = _sem("s_ln3")
        s_tpd = _sem("s_tpd")
        s_evd = _sem("s_evd")
        s_mm1 = _sem("s_mm1")
        s_hq = _sem("s_hq")
        s_mm2 = _sem("s_mm2")
        s_o2 = _sem("s_o2")
        s_rs4 = _sem("s_rs4")
        s_frp = [_sem("s_fra"), _sem("s_frb")]
        s_outp = [_sem("s_outa"), _sem("s_outb")]
        block = _ctx.enter_context(nc.Block())
        xT = BIG_A[:]
        PT_A = BIG_A[:, 0:16384]
        PT_B = BIG_A[:, 16384:32768]
        w1T_sb = BIG_A[:]

        wq_sb = BIG_B[:, 0:8192]
        wk_sb = BIG_B[:, 8192:16384]
        wv_sb = BIG_B[:, 16384:24576]
        wo_sb = BIG_B[:, 24576:32768]

        def kts(s):
            return BIG_B[:, s * 10240: s * 10240 + 4096]

        def vstr(s):
            return BIG_B[:, s * 10240 + 4096: s * 10240 + 8192]

        def qts(s):
            return QTS[:, s * TQ:(s + 1) * TQ]

        w2_sb = BIG_B[:].bitcast(E4M3)[:, 0:32768]
        xT2 = BIG_B[:, 16384:24576]
        o2q = BIG_B[:].bitcast(FP16)[:, 24576:32768]
        ctxT = CTXB[:]
        hqT = CTXB[:].bitcast(E4M3)

        ps_qkv = [P8[:, i * 512:(i + 1) * 512] for i in range(4)]
        ps_tp = P8[:, 2048:3072].bitcast(BF16)
        ps_rtp = P8[:, 3072:4096].bitcast(BF16)
        ps_bc = P8[:, 3072:3584]
        ps_st = [P8[:, m * 1024:(m + 1) * 1024] for m in range(3)]
        ps_ot = P8[:, 3072:3584]
        ps_zf = P8[:, 3584:4096]
        ps_m2 = [P8[:, 3072:3584], P8[:, 3584:4096]]

        CTXF = CTXB[:].bitcast(F32)
        sqb = CTXF[:, 3072:4096]
        nrmall = CTXF[:, 0:768]
        zltv = tmpn[:][:, 0:512]
        b2qb = rzb[:]
        b2row = CTXF[0:1, 5120:6144]
        b2h = CTXB[:].bitcast(FP16)[0:1, 12288:13312]
        QTSF = QTS[:].bitcast(F32)
        rkcs = QTSF[:, 0:1024]
        rkss = QTSF[:, 1024:2048]
        rqcs = CTXF[:, 4096:4608]
        rqss = CTXF[:, 4608:5120]
        outb = QTSF[:, 0:2048]
        ktst = CTXB[:, 13312:14336]
        vtl = CTXB[:, 14336:15360]

        NCI = 9
        NW = 8

        fbs_all = []
        for i in range(NTK):
            fl = [("k", 0, i), ("k", 1, i), ("v", 0, i), ("v", 1, i)]
            if i < NTQ:
                fl += [("q", 0, i), ("q", 1, i)]
            fbs_all += fl
        NROPE = NTK * 2 + NTQ * 2
        NV = NTK * 2

        def emit(eng, e):
            if eng == "sync":
                e.dma_start(out=identb[:], in_=ident_t.ap()[:, :]).then_inc(s_ci, 16)
                e.dma_start(out=zindb[:], in_=zind_t.ap()[:, :]).then_inc(s_ci, 16)
                e.dma_start(out=zonesb[:], in_=zones_t.ap()[:, :]).then_inc(s_ci, 16)
                e.dma_start(out=onesbb[:], in_=onesb_t.ap()[:, :]).then_inc(s_ci, 16)
                e.dma_start(out=rqcs, in_=rqc_i.ap()[:, :]).then_inc(s_ci, 16)
                e.dma_start(out=rqss, in_=rqs_i.ap()[:, :]).then_inc(s_ci, 16)
                e.dma_start(out=rkcs, in_=rkc_i.ap()[:, :]).then_inc(s_ci, 16)
                e.dma_start(out=rkss, in_=rks_i.ap()[:, :]).then_inc(s_ci, 16)
                e.dma_start(out=b1s[:], in_=b1_i.ap()[:, :]).then_inc(s_ci, 16)
            if eng == "gpsimd":
                e.dma_start(out=wq_sb.rearrange("p (c f) -> p c f", c=8),
                            in_=wqT.ap()[:, :].rearrange("(c p) f -> p c f", p=128)).then_inc(s_w, 16)
                e.dma_start(out=wk_sb.rearrange("p (c f) -> p c f", c=8),
                            in_=wkT.ap()[:, :].rearrange("(c p) f -> p c f", p=128)).then_inc(s_w, 16)
                e.dma_start(out=wv_sb.rearrange("p (c f) -> p c f", c=8),
                            in_=wvT.ap()[:, :].rearrange("(c p) f -> p c f", p=128)).then_inc(s_w, 16)
                e.dma_start(out=wo_sb.rearrange("p (c f) -> p c f", c=8),
                            in_=woT.ap()[:, :].rearrange("(c p) f -> p c f", p=128)).then_inc(s_w, 16)
                e.dma_start(out=bobr[:], in_=bo_i.ap()[0:1, :]).then_inc(s_w, 16)
                e.dma_start(out=bqr[:], in_=bq_i.ap()[0:1, :]).then_inc(s_w, 16)
                e.dma_start(out=bkr[:], in_=bk_i.ap()[0:1, :]).then_inc(s_w, 16)
                e.dma_start(out=bvr[:], in_=bv_i.ap()[0:1, :]).then_inc(s_w, 16)
            def fbs_of_tile(t):
                fl = [("k", 0, t), ("k", 1, t), ("v", 0, t), ("v", 1, t)]
                if t < NTQ:
                    fl += [("q", 0, t), ("q", 1, t)]
                return fl
            fb_j = {}
            fb_nq = {}
            fb_nv = {}
            _j = _nq = _nv = 0
            cons = []
            for t in range(NTK):
                for fb in fbs_of_tile(t):
                    fb_j[(fb[0], fb[1], t)] = _j
                    if fb[0] == "v":
                        fb_nv[(fb[1], t)] = _nv
                        cons.append(("ve", _nv + 1))
                        _nv += 1
                    else:
                        fb_nq[(fb[0], fb[1], t)] = _nq
                        cons.append(("qe", _nq + 1))
                        _nq += 1
                    _j += 1

            if eng == "sync":
                for i in range(NTK):
                    if i >= 2:
                        e.wait_ge(s_ln, i - 1)
                    e.dma_start(out=xin[:, (i % 2) * E:(i % 2 + 1) * E],
                                in_=x_in.ap()[i * 128:(i + 1) * 128, :]).then_inc(s_x1p[i % 2], 16)

            if eng == "vector":
                def _norm_x(i):
                    e.wait_ge(s_r1, i + 1)
                    if i >= 2:
                        e.wait_ge(s_tpa, i - 1)
                    xi = xin[:, (i % 2) * E:(i % 2 + 1) * E]
                    e.tensor_scalar(xlnb[:, (i % 2) * E:(i % 2 + 1) * E], xi,
                                    mnv[:, i:i + 1], rstb[:, i:i + 1],
                                    ALU.subtract, ALU.mult).then_inc(s_ln, 1)
                    e.drain()
                def _evac_xT(i):
                    e.wait_ge(s_tpa, i + 1)
                    pt = ps_tp[:, (i % 2) * 1024:(i % 2 + 1) * 1024]
                    e.tensor_copy(
                        xT.rearrange("p (c t) -> p c t", c=8)[:, :, i * 128:(i + 1) * 128],
                        pt.rearrange("p (c t) -> p c t", c=8)).then_inc(s_eva, 1)
                    e.drain()
                def _rope_fb(kind, half, t):
                    j = fb_j[(kind, half, t)]
                    nq = fb_nq[(kind, half, t)]
                    e.wait_ge(s_qkv, j + 1)
                    ps = ps_qkv[j % 4]
                    ct = (rqcs if kind == "q" else rkcs)[:, t * 32: t * 32 + 32]
                    stt = (rqss if kind == "q" else rkss)[:, t * 32: t * 32 + 32]
                    if nq >= 6:
                        e.wait_ge(s_tpb, nq - 5)
                    rsl = krb[:, (nq % 6) * 512:(nq % 6) * 512 + 512]
                    kv = ps.rearrange("p (h two f) -> p h two f", h=8, two=2)
                    rv = rsl.rearrange("p (h two f) -> p h two f", h=8, two=2)
                    sq3 = sqb[:, 0:512].rearrange("p (h two f) -> p h two f", h=8, two=2)
                    cB = ct.unsqueeze(1).broadcast_to([128, 8, 32])
                    sB = stt.unsqueeze(1).broadcast_to([128, 8, 32])
                    qe_, qo = kv[:, :, 0, :], kv[:, :, 1, :]
                    re_, ro = rv[:, :, 0, :], rv[:, :, 1, :]
                    e.tensor_tensor(re_, qe_, cB, ALU.mult)
                    e.tensor_tensor(ro, qo, cB, ALU.mult)
                    e.tensor_tensor(sq3[:, :, 0, :], qo, sB, ALU.mult)
                    e.tensor_tensor(sq3[:, :, 1, :], qe_, sB,
                                    ALU.mult).then_inc(s_qe, 1)
                    e.drain()
                    e.tensor_sub(re_, re_, sq3[:, :, 0, :])
                    e.tensor_add(ro, ro, sq3[:, :, 1, :]).then_inc(s_rope, 1)
                    e.drain()
                def _sq_fb(kind, half, t):
                    nq = fb_nq[(kind, half, t)]
                    rsl = krb[:, (nq % 6) * 512:(nq % 6) * 512 + 512]
                    e.tensor_tensor(sqb[:, 512:1024], rsl, rsl, ALU.mult)
                    e.drain()
                    e.tensor_reduce(nrmall[:, nq * 8: nq * 8 + 8],
                                    sqb[:, 512:1024].rearrange("p (h f) -> p h f", h=8),
                                    AX.X, ALU.add)
                    e.drain()
                for i in range(NTK + 3):
                    if 0 <= i - 1 < NTK:
                        _norm_x(i - 1)
                    if 0 <= i - 2 < NTK:
                        _evac_xT(i - 2)
                    if i < NTK:
                        e.wait_ge(s_x1p[i % 2], 16 * (i // 2 + 1))
                        xi = xin[:, (i % 2) * E:(i % 2 + 1) * E]
                        e.bn_stats(st12[:, 0:6], xi[:, 0:512])
                        e.bn_stats(st12[:, 6:12], xi[:, 512:1024])
                        e.drain()
                        e.bn_aggr(mv2[:], st12[:])
                        e.drain()
                        e.tensor_copy(mnv[:, i:i + 1], mv2[:, 0:1])
                        e.tensor_scalar_add(vrv[:, i:i + 1], mv2[:, 1:2],
                                            float(EPS)).then_inc(s_bn1, 1)
                        e.drain()
                    if 0 <= i - 3 < NTK:
                        for fb in fbs_of_tile(i - 3):
                            if fb[0] != "v":
                                _rope_fb(*fb)
                        for fb in fbs_of_tile(i - 3):
                            if fb[0] != "v":
                                _sq_fb(*fb)

            if eng == "scalar":
                def _evac_rope(r):
                    e.wait_ge(s_tpb, r + 1)
                    if r >= 2:
                        e.wait_ge(s_scwp[r % 2], 16 * (r // 2))
                    ptr = ps_rtp[:, (r % 2) * 1024:(r % 2) * 1024 + 512]
                    e.activation(ktst[:, (r % 2) * 512:(r % 2) * 512 + 512],
                                 ptr, AF.Copy).then_inc(s_kte, 1)
                nev = [0]
                for i in range(NTK + 4):
                    if i < NTK:
                        e.wait_ge(s_bn1, i + 1)
                        e.activation(rstb[:, i:i + 1], vrv[:, i:i + 1], AF.Ln)
                        e.drain()
                        e.activation(rstb[:, i:i + 1], rstb[:, i:i + 1], AF.Exp,
                                     scale=-0.5).then_inc(s_r1, 1)
                        e.drain()
                    if 0 <= i - 4 < NTK:
                        for fb in fbs_of_tile(i - 4):
                            if fb[0] != "v":
                                _evac_rope(nev[0])
                                nev[0] += 1
                    if 0 <= i - 3 < NTK:
                        for fb in fbs_of_tile(i - 3):
                            if fb[0] == "v":
                                j = fb_j[fb]
                                nv = fb_nv[(fb[1], fb[2])]
                                e.wait_ge(s_qkv, j + 1)
                                if nv >= 2:
                                    e.wait_ge(s_vwp[nv % 2], 16 * (nv // 2))
                                ps = ps_qkv[j % 4]
                                slot = vtl[:, (nv % 2) * 512:(nv % 2) * 512 + 512]
                                e.activation(slot, ps, AF.Copy).then_inc(s_ve, 1)
                while nev[0] < NROPE:
                    _evac_rope(nev[0])
                    nev[0] += 1

            if eng == "tensor":
                e.wait_ge(s_ci, 16 * NCI)
                e.wait_ge(s_w, 16 * NW)
                def _tp2(i):
                    e.wait_ge(s_ln, i + 1)
                    if i >= 2:
                        e.wait_ge(s_eva, i - 1)
                    srcx = xlnb[:, (i % 2) * E:(i % 2 + 1) * E]
                    pt = ps_tp[:, (i % 2) * 1024:(i % 2 + 1) * 1024]
                    for c in range(8):
                        mm = e.transpose(pt[:, c * 128:(c + 1) * 128],
                                         srcx[:, c * 128:(c + 1) * 128], identb[:])
                    mm.then_inc(s_tpa, 1)
                def _qkv_tile(t):
                    e.wait_ge(s_eva, t + 1)
                    for fb in fbs_of_tile(t):
                        kind, half, _t = fb
                        j = fb_j[fb]
                        if j >= 4:
                            sem, val = cons[j - 4]
                            e.wait_ge(s_ve if sem == "ve" else s_qe, val)
                        ps = ps_qkv[j % 4]
                        wsb = {"q": wq_sb, "k": wk_sb, "v": wv_sb}[kind]
                        brow = {"q": bqr, "k": bkr, "v": bvr}[kind]
                        for ec in range(ECH):
                            e.matmul(ps, xT[:, ec * 4096 + _t * 128: ec * 4096 + (_t + 1) * 128],
                                     wsb[:, ec * 1024 + half * 512: ec * 1024 + half * 512 + 512],
                                     start=(ec == 0), stop=False)
                        e.matmul(ps, onesbb[:], brow[:, half * 512: half * 512 + 512],
                                 start=False, stop=True).then_inc(s_qkv, 1)
                ntp = [0]
                def _rope_tp_tile(t):
                    for fb in fbs_of_tile(t):
                        if fb[0] == "v":
                            continue
                        r = ntp[0]
                        e.wait_ge(s_rope, r + 1)
                        if r >= 2:
                            e.wait_ge(s_kte, r - 1)
                        srcr = krb[:, (r % 6) * 512:(r % 6) * 512 + 512]
                        ptr = ps_rtp[:, (r % 2) * 1024:(r % 2) * 1024 + 512]
                        for c in range(4):
                            mm = e.transpose(ptr[:, c * 128:(c + 1) * 128],
                                             srcr[:, c * 128:(c + 1) * 128], identb[:])
                        mm.then_inc(s_tpb, 1)
                        ntp[0] += 1
                for i in range(NTK + 3):
                    if 0 <= i - 1 < NTK:
                        _tp2(i - 1)
                    if 0 <= i - 2 < NTK:
                        _qkv_tile(i - 2)
                    if 0 <= i - 3 < NTK:
                        _rope_tp_tile(i - 3)

            if eng == "gpsimd":
                r = 0
                nv = 0
                for t in range(NTK):
                    for (kind, half, i) in fbs_of_tile(t):
                        if kind == "v":
                            e.wait_ge(s_ve, nv + 1)
                            slot = vtl[:, (nv % 2) * 512:(nv % 2) * 512 + 512]
                            e.dma_start(out=v_scr.ap()[i * 128:(i + 1) * 128,
                                                       half * 512: half * 512 + 512],
                                        in_=slot).then_inc(s_vwp[nv % 2], 16)
                            nv += 1
                        else:
                            e.wait_ge(s_kte, r + 1)
                            scr = kt_scr if kind == "k" else qt_scr
                            st3 = ktst[:, (r % 2) * 512:(r % 2) * 512 + 512].rearrange(
                                "p (c t) -> p c t", c=4)
                            dst = scr.ap()[:, :].rearrange("(cc p) t -> p cc t", p=128)[
                                :, half * 4:(half + 1) * 4, i * 128:(i + 1) * 128]
                            e.dma_start(out=dst, in_=st3).then_inc(s_scwp[r % 2], 16)
                            r += 1

            if eng == "vector":
                nv3 = nrmall.rearrange("p (n h) -> p n h", h=8)
                def _cls_max(dst, idxs):
                    first = True
                    for (base, step, cnt) in idxs:
                        view = bass.AP(nrmall, base * 8,
                                       [[1, 128], [8 * step, cnt], [1, 8]]) if False else                             nv3[:, base:base + step * (cnt - 1) + 1:step, :]
                        e.tensor_reduce(nrm8[:], view.transpose([0, 2, 1]), AX.X, ALU.max)
                        e.drain()
                        if first:
                            e.tensor_copy(dst, nrm8[:])
                        else:
                            e.tensor_max(dst, dst, nrm8[:])
                        e.drain()
                        first = False
                _cls_max(km2[:, 0:8], [(0, 4, 16), (64, 2, 16)])
                _cls_max(km2[:, 8:16], [(1, 4, 16), (65, 2, 16)])
                _cls_max(qm2[:, 0:8], [(2, 4, 16)])
                _cls_max(qm2[:, 8:16], [(3, 4, 16)])
                e.tensor_copy(mx16[:, 0:16], qm2[:])
                e.tensor_copy(mx16[:, 16:32], km2[:]).then_inc(s_cb, 1)
                e.drain()
            if eng == "tensor":
                e.wait_ge(s_cb, 1)
                e.wait_ge(s_kte, NROPE)
                e.transpose(ps_tp[0:16, 0:128], mx16[:, 0:16], identb[:])
                e.transpose(ps_tp[0:16, 128:256], mx16[:, 16:32],
                            identb[:]).then_inc(s_cb, 1)
            if eng == "vector":
                e.wait_ge(s_cb, 2)
                e.tensor_reduce(c16[:, 0:1], ps_tp[0:16, 0:128], AX.X, ALU.max)
                e.tensor_reduce(c16[:, 1:2], ps_tp[0:16, 128:256], AX.X, ALU.max)
                e.drain()
                e.tensor_tensor(c16[:, 2:3], c16[:, 0:1], c16[:, 1:2],
                                ALU.mult).then_inc(s_cb, 1)
                e.drain()
            if eng == "scalar":
                e.wait_ge(s_cb, 3)
                e.activation(c16[:, 3:4], c16[:, 2:3], AF.Ln)
                e.drain()
                e.activation(c16[:, 3:4], c16[:, 3:4], AF.Exp, scale=0.5).then_inc(s_cb, 1)
                e.drain()
            if eng == "vector":
                e.wait_ge(s_cb, 4)
                e.tensor_scalar(cnegb[:], c16[:, 3:4], 1.0, -1.0,
                                ALU.add, ALU.mult).then_inc(s_cb, 1)
                e.drain()
            if eng == "tensor":
                e.wait_ge(s_cb, 5)
                e.matmul(ps_bc[0:1, 32:160], cnegb[:], identb[0:16, 0:128],
                         start=True, stop=True).then_inc(s_cb, 1)
            if eng == "vector":
                e.wait_ge(s_cb, 6)
                e.tensor_copy(cnr[:], ps_bc[0:1, 32:48]).then_inc(s_cb, 1)
                e.drain()
            if eng == "tensor":
                e.wait_ge(s_cb, 7)
                e.matmul(ps_bc[:, 0:16], onesbb[:], cnr[:],
                         start=True, stop=True).then_inc(s_cb, 1)
            if eng == "vector":
                e.wait_ge(s_cb, 8)
                e.tensor_copy(cbias[:], ps_bc[:, 0:16])
                e.drain()
                cv = cbias[:].rearrange("p (a b) -> p a b", b=2)
                e.tensor_tensor(cpairb[:], cv[:, :, 0], cv[:, :, 1],
                                ALU.min).then_inc(s_cb, 1)
                e.drain()

            for hp in range(HPn):
                if eng == "gpsimd":
                    if hp == 0:
                        e.wait_ge(s_scwp[0], 16 * (NROPE // 2))
                        e.wait_ge(s_scwp[1], 16 * (NROPE // 2))
                        e.wait_ge(s_vwp[0], 16 * (NV // 2))
                        e.wait_ge(s_vwp[1], 16 * (NV // 2))
                    if hp >= 2:
                        e.wait_ge(s_ctx, (hp - 1) * QBn)
                    s = hp % 2
                    e.dma_start(out=kts(s),
                                in_=kt_scr.ap()[hp * 128:(hp + 1) * 128, :]).then_inc(s_sbp[s], 16)
                    e.dma_start(out=vstr(s).rearrange("p (kc f) -> p kc f", kc=32),
                                in_=v_scr.ap()[:, :].rearrange("(kc p) f -> p kc f", p=128)[
                                    :, :, hp * 128:(hp + 1) * 128]).then_inc(s_sbp[s], 16)
                    e.dma_start(out=qts(s),
                                in_=qt_scr.ap()[hp * 128:(hp + 1) * 128, :]).then_inc(s_sbp[s], 16)
                for qb in range(QBn):
                    idx = hp * QBn + qb
                    s = hp % 2
                    if eng == "tensor" and idx == 0:
                        NID = HPn * QBn
                        def _pv(jdx, kc):
                            sj = (jdx // QBn) % 2
                            e.matmul(ps_ot[0:64, :],
                                     vstr(sj)[:, kc * 128: kc * 128 + 64],
                                     PT_A[:, kc * 512:(kc + 1) * 512],
                                     start=(kc == 0), stop=(kc == KC - 1),
                                     tile_position=(0, 0))
                            m2 = e.matmul(ps_ot[64:128, :],
                                     vstr(sj)[:, kc * 128 + 64: kc * 128 + 128],
                                     PT_B[:, kc * 512:(kc + 1) * 512],
                                     start=(kc == 0), stop=(kc == KC - 1),
                                     tile_position=(0, 64))
                            if kc == KC - 1:
                                m2.then_inc(s_pv, 1)
                        def _z(jdx, g2):
                            for tt, (pt, kc) in enumerate((
                                    (PT_A, 2 * g2), (PT_B, 2 * g2),
                                    (PT_A, 2 * g2 + 1), (PT_B, 2 * g2 + 1))):
                                mm = e.matmul(ps_zf[tt * 32:(tt + 1) * 32, :], zonesb[:],
                                              pt[:, kc * 512:(kc + 1) * 512],
                                              start=(g2 == 0),
                                              stop=(g2 == 15),
                                              tile_position=(0, tt * 32))
                            return mm
                        ZLAG, PVLAG = 5, 8
                        for t in range(NID * 32 + PVLAG):
                            jdx, kc = t // 32, t % 32
                            if jdx < NID:
                                hpj, qbj = jdx // QBn, jdx % QBn
                                sj = hpj % 2
                                if kc == 0 and qbj == 0:
                                    e.wait_ge(s_sbp[sj], 48 * (hpj // 2 + 1))
                                    if jdx == 0:
                                        e.wait_ge(s_cb, 9)
                                if kc == 5 and jdx > 0:
                                    e.wait_ge(s_zse, jdx)
                                    e.matmul(ps_zf, zindb[:], zse[:],
                                             start=True, stop=True).then_inc(s_fold, 1)
                                if t >= 3:
                                    e.wait_ge(s_exp, t - 2)
                                m = t % 3
                                e.matmul(ps_st[m][:, 0:512],
                                         kts(sj)[0:64, kc * 128:(kc + 1) * 128],
                                         qts(sj)[0:64, qbj * 512:(qbj + 1) * 512],
                                         start=True, stop=True, tile_position=(0, 0))
                                e.matmul(ps_st[m][:, 512:1024],
                                         kts(sj)[64:128, kc * 128:(kc + 1) * 128],
                                         qts(sj)[64:128, qbj * 512:(qbj + 1) * 512],
                                         start=True, stop=True,
                                         tile_position=(64, 0)).then_inc(s_sc, 1)
                            u = t - ZLAG
                            if u >= 0 and u < NID * 32 and u % 2 == 1:
                                pdxz, uz = u // 32, u % 32
                                p = uz // 2
                                if jdx >= NID:
                                    e.wait_ge(s_exp, pdxz * 32 + 2 * p + 2)
                                if p == 0 and pdxz > 0:
                                    e.wait_ge(s_rz, pdxz)
                                zm = _z(pdxz, p)
                                if p == 15:
                                    zm.then_inc(s_pvz, 1)
                            v = t - PVLAG
                            if v >= 0:
                                pdx, pkc = v // 32, v % 32
                                if jdx >= NID:
                                    e.wait_ge(s_exp, pdx * 32 + pkc + 1)
                                if pkc == 0 and pdx > 0:
                                    e.wait_ge(s_ctx, pdx)
                                _pv(pdx, pkc)
                        e.wait_ge(s_zse, NID)
                        e.wait_ge(s_rz, NID - 1)
                        e.matmul(ps_zf, zindb[:], zse[:],
                                 start=True, stop=True).then_inc(s_fold, 1)
                    if eng == "scalar":
                        for kc in range(KC):
                            if kc == 6 and idx > 0:
                                j = idx - 1
                                e.wait_ge(s_fold, idx)
                                if j >= 1:
                                    e.wait_ge(s_ctx, j)
                                e.activation(zltv, ps_zf, AF.Ln)
                                e.drain()
                                e.activation(rzb[:, (j % 2) * 512:(j % 2) * 512 + 512],
                                             zltv, AF.Exp, scale=-1.0).then_inc(s_rz, 1)
                                e.drain()
                            e.wait_ge(s_sc, idx * 32 + kc + 1)
                            m = (idx * 32 + kc) % 3
                            outv = BIG_A[:].rearrange("p (h x) -> p h x", h=2)[
                                :, :, kc * 512:(kc + 1) * 512]
                            e.activation(outv, ps_st[m], AF.Exp,
                                         bias=cpairb[:, hp:hp + 1]).then_inc(s_exp, 1)
                        if idx == HPn * QBn - 1:
                            j = idx
                            e.wait_ge(s_fold, idx + 1)
                            e.wait_ge(s_ctx, j)
                            e.activation(zltv, ps_zf, AF.Ln)
                            e.drain()
                            e.activation(rzb[:, (j % 2) * 512:(j % 2) * 512 + 512],
                                         zltv, AF.Exp, scale=-1.0).then_inc(s_rz, 1)
                            e.drain()
                    if eng == "vector":
                        if idx > 0:
                            j = idx - 1
                            hpj, qbj = j // QBn, j % QBn
                            e.wait_ge(s_rz, j + 1)
                            e.wait_ge(s_pv, j + 1)
                            e.tensor_tensor(
                                ctxT[:, hpj * 2048 + qbj * 512: hpj * 2048 + (qbj + 1) * 512],
                                ps_ot,
                                rzb[:, (j % 2) * 512:(j % 2) * 512 + 512],
                                ALU.mult).then_inc(s_ctx, 1)
                            e.drain()
                        e.wait_ge(s_pvz, idx + 1)
                        e.tensor_copy(zse[:], ps_zf).then_inc(s_zse, 1)
                        e.drain()

            if eng == "vector":
                j = HPn * QBn - 1
                hpj, qbj = j // QBn, j % QBn
                e.wait_ge(s_rz, j + 1)
                e.wait_ge(s_pv, j + 1)
                e.tensor_tensor(
                    ctxT[:, hpj * 2048 + qbj * 512: hpj * 2048 + (qbj + 1) * 512],
                    ps_ot,
                    rzb[:, (j % 2) * 512:(j % 2) * 512 + 512],
                    ALU.mult).then_inc(s_ctx, 1)
                e.drain()

            if eng == "gpsimd":
                e.dma_start(out=w1q_scr.ap()[:, :],
                            in_=w1T.ap()[:, :]).then_inc(s_w1c, 16)
                e.dma_start(out=w2q_scr.ap()[:, :],
                            in_=w2T.ap()[:, :]).then_inc(s_w2c, 16)

            if eng == "sync":
                e.wait_ge(s_ctx, HPn * QBn)
                e.wait_ge(s_rz, HPn * QBn)
                e.wait_ge(s_w1c, 16)
                e.wait_ge(s_w2c, 16)
                for wh in range(2):
                    e.dma_start(
                        out=w1T_sb.rearrange("p (c f) -> p c f", c=8)[:, :, wh * 2048:(wh + 1) * 2048],
                        in_=w1q_scr.ap()[:, :].rearrange("(c p) f -> p c f", p=128)[:, :, wh * 2048:(wh + 1) * 2048]
                        ).then_inc(s_w1h[wh], 16)
                e.dma_start(out=w2_sb.rearrange("p (c f) -> p c f", c=32),
                            in_=w2q_scr.ap()[:, :].rearrange("(c p) f -> p c f", p=128)
                            ).then_inc(s_w2p[0], 16)
                e.dma_start(out=b2qb,
                            in_=b2_i.ap()[0:1, :].partition_broadcast(128)).then_inc(s_b2, 16)
                for t in range(NTQ):
                    if t >= 2:
                        e.wait_ge(s_x2wp[t % 2], 16 * (t // 2))
                        e.wait_ge(s_ln3, min(8, t - 1))
                        e.wait_ge(s_bnd, t - 1)
                    e.dma_start(out=outb[:, (t % 2) * E:(t % 2 + 1) * E],
                                in_=x_in.ap()[t * 128:(t + 1) * 128, :]).then_inc(s_xrp[t % 2], 16)
            if eng == "tensor":
                ps_o6 = P8[:, 3072:4096]
                def _pso(t, dh):
                    if t % 3 == 2:
                        return ps_o6[:, dh * 512:(dh + 1) * 512]
                    return ps_qkv[(t % 3) * 2 + dh]
                def _tp3(i):
                    e.wait_ge(s_ln3, i + 1)
                    if i == 0:
                        e.wait_ge(s_exp, HPn * QBn * 32)
                    if i >= 2:
                        e.wait_ge(s_evd, i - 1)
                    src3 = xlnb[:, (i % 2) * E:(i % 2 + 1) * E]
                    pt3 = ps_tp[:, (i % 2) * 1024:(i % 2 + 1) * 1024]
                    for c in range(8):
                        mm3 = e.transpose(pt3[:, c * 128:(c + 1) * 128],
                                          src3[:, c * 128:(c + 1) * 128], identb[:])
                    mm3.then_inc(s_tpd, 1)
                for t in range(NTQ):
                    if t == 2:
                        e.wait_ge(s_zse, HPn * QBn)
                        e.wait_ge(s_rz, HPn * QBn)
                        e.wait_ge(s_ctx, HPn * QBn)
                    if t >= 3:
                        e.wait_ge(s_x2n, t - 2)
                    for dh in range(2):
                        ps = _pso(t, dh)
                        for c in range(ECH):
                            e.matmul(ps, ctxT[:, c * 2048 + t * 128: c * 2048 + (t + 1) * 128],
                                     wo_sb[:, c * 1024 + dh * 512: c * 1024 + dh * 512 + 512],
                                     start=(c == 0), stop=False)
                        mm = e.matmul(ps, onesbb[:], bobr[:, dh * 512: dh * 512 + 512],
                                      start=False, stop=True)
                    mm.then_inc(s_op, 1)
                    if 0 <= t - 2 < 8:
                        _tp3(t - 2)
            if eng == "vector":
                ps_o6v = P8[:, 3072:4096]
                def _psov(t, dh):
                    if t % 3 == 2:
                        return ps_o6v[:, dh * 512:(dh + 1) * 512]
                    return ps_qkv[(t % 3) * 2 + dh]
                for t in range(NTQ):
                    e.wait_ge(s_op, t + 1)
                    p0 = _psov(t, 0)
                    p1 = _psov(t, 1)
                    e.bn_stats(st12[:, 0:6], p0)
                    e.bn_stats(st12[:, 6:12], p1)
                    e.drain()
                    e.bn_aggr(mv2[:], st12[:])
                    e.drain()
                    e.tensor_scalar_add(rs1[:], mv2[:, 1:2], float(EPS)).then_inc(s_bnc, 1)
                    e.drain()
                    e.wait_ge(s_rsc, t + 1)
                    e.wait_ge(s_xrp[t % 2], 16 * (t // 2 + 1))
                    for dh in range(2):
                        ps = _psov(t, dh)
                        e.tensor_scalar(tmpn[:, dh * 512: dh * 512 + 512], ps,
                                        mv2[:, 0:1], rs1[:], ALU.subtract, ALU.mult)
                    e.drain()
                    e.tensor_add(outb[:, (t % 2) * E:(t % 2 + 1) * E],
                                 tmpn[:],
                                 outb[:, (t % 2) * E:(t % 2 + 1) * E]).then_inc(s_x2n, 1)
                    e.drain()
                    c3 = t if t < 8 else t + 8
                    ob = outb[:, (t % 2) * E:(t % 2 + 1) * E]
                    e.bn_stats(st12[:, 0:6], ob[:, 0:512])
                    e.bn_stats(st12[:, 6:12], ob[:, 512:1024])
                    e.drain()
                    e.bn_aggr(mv2[:], st12[:])
                    e.drain()
                    e.tensor_copy(mnv[:, c3:c3 + 1], mv2[:, 0:1])
                    e.tensor_scalar_add(vrv[:, c3:c3 + 1], mv2[:, 1:2],
                                        float(EPS)).then_inc(s_bnd, 1)
                    e.drain()
                    if 0 <= t - 1 < 8:
                        i = t - 1
                        e.wait_ge(s_rs3, i + 1)
                        if i >= 2:
                            e.wait_ge(s_tpd, i - 1)
                        ob = outb[:, (i % 2) * E:(i % 2 + 1) * E]
                        e.tensor_scalar(xlnb[:, (i % 2) * E:(i % 2 + 1) * E], ob,
                                        mnv[:, i:i + 1], rstb[:, i:i + 1],
                                        ALU.subtract, ALU.mult).then_inc(s_ln3, 1)
                        e.drain()
            if eng == "scalar":
                def _evac3a(i):
                    e.wait_ge(s_tpd, i + 1)
                    pt3 = ps_tp[:, (i % 2) * 1024:(i % 2 + 1) * 1024]
                    e.activation(
                        xT2.rearrange("p (c t) -> p c t", c=8)[:, :, i * 128:(i + 1) * 128],
                        pt3.rearrange("p (c t) -> p c t", c=8), AF.Copy).then_inc(s_evd, 1)
                for t in range(NTQ):
                    e.wait_ge(s_bnc, t + 1)
                    e.activation(rs1[:], rs1[:], AF.Ln)
                    e.drain()
                    e.activation(rs1[:], rs1[:], AF.Exp, scale=-0.5).then_inc(s_rsc, 1)
                    e.drain()
                    if 0 <= t - 3 < 8:
                        _evac3a(t - 3)
                    if 0 <= t - 1 < NTQ:
                        i3 = t - 1
                        cc = i3 if i3 < 8 else i3 + 8
                        e.wait_ge(s_bnd, i3 + 1)
                        e.activation(rstb[:, cc:cc + 1], vrv[:, cc:cc + 1], AF.Ln)
                        e.drain()
                        e.activation(rstb[:, cc:cc + 1], rstb[:, cc:cc + 1], AF.Exp,
                                     scale=-0.5).then_inc(s_rs3, 1)
                        e.drain()
                    if t == NTQ - 1:
                        e.wait_ge(s_bnd, NTQ)
                        e.activation(rstb[:, 23:24], vrv[:, 23:24], AF.Ln)
                        e.drain()
                        e.activation(rstb[:, 23:24], rstb[:, 23:24], AF.Exp,
                                     scale=-0.5).then_inc(s_rs3, 1)
                        e.drain()
            if eng == "gpsimd":
                for t in range(NTQ):
                    e.wait_ge(s_x2n, t + 1)
                    e.dma_start(out=x2_scr.ap()[t * 128:(t + 1) * 128, :],
                                in_=outb[:, (t % 2) * E:(t % 2 + 1) * E]).then_inc(s_x2wp[t % 2], 16)

            for th in range(2):
                tb0 = th * 8
                if eng == "sync" and th == 1:
                    e.wait_ge(s_o2f, 16)
                    for i in range(8):
                        t = tb0 + i
                        g = th * 8 + i
                        e.wait_ge(s_x2wp[t % 2], 16 * (t // 2 + 1))
                        if i >= 2:
                            e.wait_ge(s_ln3, th * 8 + i - 1)
                        e.dma_start(out=xin[:, (i % 2) * E:(i % 2 + 1) * E],
                                    in_=x2_scr.ap()[t * 128:(t + 1) * 128, :]).then_inc(s_l3sp[i % 2], 16)
                if eng == "vector" and th == 1:
                    def _norm3(i):
                        e.wait_ge(s_rs3, th * 8 + i + 1)
                        if i >= 2:
                            e.wait_ge(s_tpd, th * 8 + i - 1)
                        xi = xin[:, (i % 2) * E:(i % 2 + 1) * E]
                        e.tensor_scalar(xlnb[:, (i % 2) * E:(i % 2 + 1) * E], xi,
                                        mnv[:, 16 + i:17 + i], rstb[:, 16 + i:17 + i],
                                        ALU.subtract, ALU.mult).then_inc(s_ln3, 1)
                        e.drain()
                    def _evac3(i):
                        e.wait_ge(s_tpd, th * 8 + i + 1)
                        pt = ps_tp[:, (i % 2) * 1024:(i % 2 + 1) * 1024]
                        e.tensor_copy(
                            xT2.rearrange("p (c t) -> p c t", c=8)[:, :, i * 128:(i + 1) * 128],
                            pt.rearrange("p (c t) -> p c t", c=8)).then_inc(s_evd, 1)
                        e.drain()
                    for i in range(8):
                        e.wait_ge(s_l3sp[i % 2], 16 * (i // 2 + 1))
                        _norm3(i)
                        if i >= 2:
                            _evac3(i - 2)
                    _evac3(6)
                    _evac3(7)
                if eng == "tensor" and th == 1:
                    for i in range(8):
                        e.wait_ge(s_ln3, th * 8 + i + 1)
                        if i >= 2:
                            e.wait_ge(s_evd, th * 8 + i - 1)
                        src = xlnb[:, (i % 2) * E:(i % 2 + 1) * E]
                        pt = ps_tp[:, (i % 2) * 1024:(i % 2 + 1) * 1024]
                        for c in range(8):
                            mm = e.transpose(pt[:, c * 128:(c + 1) * 128],
                                             src[:, c * 128:(c + 1) * 128], identb[:])
                        mm.then_inc(s_tpd, 1)
                if eng == "tensor":
                    if th == 0:
                        e.wait_ge(s_x2n, NTQ)
                        e.wait_ge(s_zse, HPn * QBn)
                        e.wait_ge(s_rz, HPn * QBn)
                        e.wait_ge(s_ctx, HPn * QBn)
                    def _mm1(tb, hc):
                        j = tb * HCn + hc
                        if th == 0 and tb == 0 and hc in (0, 16):
                            e.wait_ge(s_w1h[hc // 16], 16)
                        if j >= 4:
                            e.wait_ge(s_hq, th * 64 + j - 3)
                        ps = ps_qkv[j % 4]
                        for ec in range(ECH):
                            mm = e.matmul(ps,
                                          w1T_sb[:, ec * 4096 + hc * 128: ec * 4096 + (hc + 1) * 128],
                                          xT2[:, ec * 1024 + tb * 512: ec * 1024 + tb * 512 + 512],
                                          start=(ec == 0), stop=(ec == ECH - 1))
                        mm.then_inc(s_mm1, 1)
                    hq3 = hqT.rearrange("p (hc x) -> p hc x", hc=HCn)
                    w23 = w2_sb.rearrange("p (hc x) -> p hc x", hc=HCn)
                    def _mm2(i):
                        if th == 0 and i == 0:
                            e.wait_ge(s_w2p[0], 16)
                        e.wait_ge(s_hq, th * 64 + (i // 4 + 1) * 32)
                        if th * 8 + i >= 1:
                            e.wait_ge(s_o2, th * 8 + i)
                        for dh in range(2):
                            ps = ps_m2[dh]
                            for h in range(HCn // 2):
                                mm = e.matmul(ps,
                                              hq3[:, 2 * h:2 * h + 2, i * 128:(i + 1) * 128],
                                              w23[:, 2 * h:2 * h + 2, dh * 512:dh * 512 + 512],
                                              start=(h == 0), stop=(h == HCn // 2 - 1),
                                              perf_mode=mybir.MatmulPerfMode.DoubleRow)
                        mm.then_inc(s_mm2, 1)
                    e.wait_ge(s_evd, th * 8 + 4)
                    for hc in range(HCn):
                        _mm1(0, hc)
                    e.wait_ge(s_evd, th * 8 + 8)
                    for i in range(4):
                        _mm2(i)
                        for hc in range(8 * i, 8 * i + 8):
                            _mm1(1, hc)
                    for i in range(4, 8):
                        _mm2(i)
                if eng == "sync":
                    for i in range(8):
                        t = tb0 + i
                        e.wait_ge(s_ln3, (th + 1) * 8)
                        if i >= 2:
                            e.wait_ge(s_o2f, 2 * (th * 8 + i - 1))
                        g = th * 8 + i
                        e.dma_start(out=xin[:, (i % 2) * E:(i % 2 + 1) * E],
                                    in_=x2_scr.ap()[t * 128:(t + 1) * 128, :]).then_inc(s_frp[g % 2], 16)
                if eng == "vector":
                    if th == 0:
                        e.wait_ge(s_x2wp[0], 16 * (NTQ // 2))
                        e.wait_ge(s_x2wp[1], 16 * (NTQ // 2))
                        e.wait_ge(s_b2, 16)
                        stg = o2q[:, 7 * 1024:8 * 1024]
                        e.tensor_copy(stg, b2qb)
                        e.drain()
                        e.tensor_copy(b2qb, stg)
                        e.drain()
                    def _pass1(i):
                        e.wait_ge(s_mm2, th * 8 + i + 1)
                        ot = o2q[:, i * 1024:(i + 1) * 1024]
                        for dh in range(2):
                            ps = ps_m2[dh]
                            e.tensor_tensor(ot[:, dh * 512: dh * 512 + 512], ps,
                                            b2qb[:, dh * 512: dh * 512 + 512], ALU.add)
                        e.drain()
                        e.bn_stats(st12[:, 0:6], ot[:, 0:512])
                        e.bn_stats(st12[:, 6:12], ot[:, 512:1024])
                        e.drain()
                        e.bn_aggr(mv2[:], st12[:])
                        e.drain()
                        e.tensor_copy(mnv[:, 8 + i:9 + i], mv2[:, 0:1])
                        e.tensor_scalar_add(vrv[:, 8 + i:9 + i], mv2[:, 1:2],
                                            float(EPS)).then_inc(s_o2, 1)
                        e.drain()
                    def _pass2(i):
                        g = th * 8 + i
                        ot = o2q[:, i * 1024:(i + 1) * 1024]
                        e.wait_ge(s_rs4, g + 1)
                        e.wait_ge(s_frp[g % 2], 16 * (g // 2 + 1))
                        if g >= 2:
                            e.wait_ge(s_outp[g % 2], 32 * (g // 2))
                        for hh in range(2):
                            hs = hh * 512
                            e.tensor_scalar(tmpn[:, hs:hs + 512], ot[:, hs:hs + 512],
                                            mnv[:, 8 + i:9 + i], rstb[:, 8 + i:9 + i],
                                            ALU.subtract, ALU.mult)
                            e.drain()
                            e.tensor_add(outb[:, (i % 2) * E + hs:(i % 2) * E + hs + 512],
                                         tmpn[:, hs:hs + 512],
                                         xin[:, (i % 2) * E + hs:(i % 2) * E + hs + 512]
                                         ).then_inc(s_o2f, 1)
                            e.drain()
                    for i in range(8):
                        _pass1(i)
                        if i >= 2:
                            _pass2(i - 2)
                    _pass2(6)
                    _pass2(7)
                if eng == "scalar":
                    def _rstd4(i):
                        e.wait_ge(s_o2, th * 8 + i + 1)
                        e.activation(rstb[:, 8 + i:9 + i], vrv[:, 8 + i:9 + i], AF.Ln)
                        e.drain()
                        e.activation(rstb[:, 8 + i:9 + i], rstb[:, 8 + i:9 + i], AF.Exp,
                                     scale=-0.5).then_inc(s_rs4, 1)
                        e.drain()
                    n4 = [0]
                    for tb in range(2):
                        for hc in range(HCn):
                            j = tb * HCn + hc
                            e.wait_ge(s_mm1, th * 64 + j + 1)
                            e.activation(
                                hqT[:, hc * 1024 + tb * 512: hc * 1024 + tb * 512 + 512],
                                ps_qkv[j % 4], AF.Relu,
                                bias=b1s[:, hc:hc + 1]).then_inc(s_hq, 1)
                            if tb == 1 and hc % 8 == 7:
                                _rstd4(n4[0])
                                n4[0] += 1
                    while n4[0] < 8:
                        _rstd4(n4[0])
                        n4[0] += 1
                if eng == "gpsimd":
                    for i in range(8):
                        t = tb0 + i
                        g = th * 8 + i
                        for hh in range(2):
                            hs = hh * 512
                            e.wait_ge(s_o2f, 2 * g + hh + 1)
                            e.dma_start(out=out_o.ap()[t * 128:(t + 1) * 128, hs:hs + 512],
                                        in_=outb[:, (i % 2) * E + hs:(i % 2) * E + hs + 512]
                                        ).then_inc(s_outp[g % 2], 16)

            if eng == "gpsimd":
                e.wait_ge(s_outp[0], 32 * 8)
                e.wait_ge(s_outp[1], 32 * 8)

        @block.sync
        def _(e):
            emit("sync", e)

        @block.gpsimd
        def _(e):
            emit("gpsimd", e)

        @block.vector
        def _(e):
            emit("vector", e)

        @block.scalar
        def _(e):
            emit("scalar", e)

        @block.tensor
        def _(e):
            emit("tensor", e)

    return nc


_CACHE = {}


def _tables(positions, scale):
    inv = 1.0 / (10000.0 ** (np.arange(0, HD, 2, dtype=np.float64) / HD))
    ang = positions[:, None].astype(np.float64) * inv[None, :]
    c = (np.cos(ang) * scale).astype(np.float32)
    s = (np.sin(ang) * scale).astype(np.float32)
    T = positions.shape[0]
    nt = T // 128
    cm = np.ascontiguousarray(c.reshape(nt, 128, 32).transpose(1, 0, 2).reshape(128, nt * 32))
    sm = np.ascontiguousarray(s.reshape(nt, 128, 32).transpose(1, 0, 2).reshape(128, nt * 32))
    return cm, sm


def kernel(L0_attn_x, L0_attn_Wq, L0_attn_bq, L0_attn_Wk, L0_attn_bk,
           L0_attn_Wv, L0_attn_bv, L0_attn_Wo, L0_attn_bo,
           L0_ffn_w1, L0_ffn_b1, L0_ffn_w2, L0_ffn_b2):
    x = np.ascontiguousarray(np.asarray(L0_attn_x, dtype=np.float32))
    B, Lfull, _ = x.shape

    perm = []
    for h in range(NH):
        perm += [h * HD + 2 * i for i in range(HD // 2)]
        perm += [h * HD + 2 * i + 1 for i in range(HD // 2)]
    perm = np.asarray(perm)

    wqT = np.ascontiguousarray(np.asarray(L0_attn_Wq, dtype=np.float32)[perm, :].T)
    wkT = np.ascontiguousarray(np.asarray(L0_attn_Wk, dtype=np.float32)[perm, :].T)
    wvT = np.ascontiguousarray(np.asarray(L0_attn_Wv, dtype=np.float32).T)
    woT = np.ascontiguousarray(np.asarray(L0_attn_Wo, dtype=np.float32).T)
    w1T = np.ascontiguousarray(np.asarray(L0_ffn_w1, dtype=np.float32).T)
    w2T = np.ascontiguousarray(np.asarray(L0_ffn_w2, dtype=np.float32).T)
    bq = np.ascontiguousarray(np.asarray(L0_attn_bq, dtype=np.float32)[perm][None, :])
    bk = np.ascontiguousarray(np.asarray(L0_attn_bk, dtype=np.float32)[perm][None, :])
    bv = np.ascontiguousarray(np.asarray(L0_attn_bv, dtype=np.float32)[None, :])
    bo = np.ascontiguousarray(np.asarray(L0_attn_bo, dtype=np.float32)[None, :])
    b2 = np.ascontiguousarray(np.asarray(L0_ffn_b2, dtype=np.float32)[None, :])
    b1 = np.ascontiguousarray(
        np.asarray(L0_ffn_b1, dtype=np.float32).reshape(HCn, 128).T)

    if "nc" not in _CACHE:
        _CACHE["nc"] = build_nc()
    nc = _CACHE["nc"]

    in_maps = []
    for core in range(8):
        b, h = core // 2, core % 2
        own = x[b, h * TQ:(h + 1) * TQ]
        oth = x[b, (1 - h) * TQ:(2 - h) * TQ]
        x_core = np.ascontiguousarray(np.concatenate([own, oth], axis=0))
        posQ = np.arange(h * TQ, (h + 1) * TQ)
        posK = np.concatenate([posQ, np.arange((1 - h) * TQ, (2 - h) * TQ)])
        rqc, rqs = _tables(posQ, 1.0 / np.sqrt(HD))
        rkc, rks = _tables(posK, 1.0)
        in_maps.append({
            "x_in": x_core, "wqT": wqT, "wkT": wkT, "wvT": wvT, "woT": woT,
            "w1T": w1T, "w2T": w2T, "bq_i": bq, "bk_i": bk, "bv_i": bv,
            "bo_i": bo, "b2_i": b2, "b1_i": b1,
            "rqc_i": rqc, "rqs_i": rqs, "rkc_i": rkc, "rks_i": rks,
        })

    res = run_bass_kernel_spmd(nc, in_maps, core_ids=list(range(8)))
    out = np.empty((B, Lfull, E), dtype=np.float32)
    for core in range(8):
        b, h = core // 2, core % 2
        out[b, h * TQ:(h + 1) * TQ] = res.results[core]["out"]
    return out


# version 36 (final)
# speedup vs baseline: 1.0276x; 1.0148x over previous
import numpy as np
import ml_dtypes
import concourse.bass as bass
import concourse.mybir as mybir
from concourse.bass_utils import run_bass_kernel_spmd

F32 = mybir.dt.float32
BF16 = mybir.dt.bfloat16
FP16 = mybir.dt.float16
E4M3 = mybir.dt.float8e4
AF = mybir.ActivationFunctionType
ALU = mybir.AluOpType
AX = mybir.AxisListType

E = 1024
NH = 16
HD = 64
L = 4096
TQ = 2048
NTK = 32
NTQ = 16
ECH = 8
HPn = 8
QBn = 4
KC = 32
HF = 4096
HCn = 32
EPS = 1e-5


def build_nc():
    nc = bass.Bass("TRN2", target_bir_lowering=False)

    x_in = nc.dram_tensor("x_in", [L, E], F32, kind="ExternalInput")
    wqT = nc.dram_tensor("wqT", [E, E], F32, kind="ExternalInput")
    wkT = nc.dram_tensor("wkT", [E, E], F32, kind="ExternalInput")
    wvT = nc.dram_tensor("wvT", [E, E], F32, kind="ExternalInput")
    woT = nc.dram_tensor("woT", [E, E], F32, kind="ExternalInput")
    w1T = nc.dram_tensor("w1T", [E, HF], F32, kind="ExternalInput")
    w2T = nc.dram_tensor("w2T", [HF, E], F32, kind="ExternalInput")
    bq_i = nc.dram_tensor("bq_i", [1, E], F32, kind="ExternalInput")
    bk_i = nc.dram_tensor("bk_i", [1, E], F32, kind="ExternalInput")
    bv_i = nc.dram_tensor("bv_i", [1, E], F32, kind="ExternalInput")
    bo_i = nc.dram_tensor("bo_i", [1, E], F32, kind="ExternalInput")
    b2_i = nc.dram_tensor("b2_i", [1, E], F32, kind="ExternalInput")
    b1_i = nc.dram_tensor("b1_i", [128, HCn], F32, kind="ExternalInput")
    rqc_i = nc.dram_tensor("rqc_i", [128, NTQ * 32], F32, kind="ExternalInput")
    rqs_i = nc.dram_tensor("rqs_i", [128, NTQ * 32], F32, kind="ExternalInput")
    rkc_i = nc.dram_tensor("rkc_i", [128, NTK * 32], F32, kind="ExternalInput")
    rks_i = nc.dram_tensor("rks_i", [128, NTK * 32], F32, kind="ExternalInput")
    out_o = nc.dram_tensor("out", [TQ, E], F32, kind="ExternalOutput")

    kt_scr = nc.dram_tensor("kt_scr", [E, L], BF16)
    qt_scr = nc.dram_tensor("qt_scr", [E, TQ], BF16)
    v_scr = nc.dram_tensor("v_scr", [L, E], BF16)
    x2_scr = nc.dram_tensor("x2_scr", [TQ, E], F32)
    w1q_scr = nc.dram_tensor("w1q_scr", [E, HF], BF16)
    w2q_scr = nc.dram_tensor("w2q_scr", [HF, E], E4M3)

    identN = np.eye(128, dtype=ml_dtypes.bfloat16)
    zindN = np.zeros((128, 128), dtype=np.float32)
    zindN[0, 0:64] = 1.0
    zindN[64, 0:64] = 1.0
    zindN[32, 64:128] = 1.0
    zindN[96, 64:128] = 1.0
    zonesN = np.zeros((128, 32), dtype=ml_dtypes.bfloat16)
    zonesN[:, 0] = 1.0
    ident_t = nc.inline_tensor(identN, "ident_t")
    zind_t = nc.inline_tensor(zindN, "zind_t")
    zones_t = nc.inline_tensor(zonesN, "zones_t")
    onesb_t = nc.inline_tensor(np.ones((1, 128), dtype=ml_dtypes.bfloat16), "onesb_t")
    onesf_t = nc.inline_tensor(np.ones((1, 128), dtype=np.float32), "onesf_t")

    from contextlib import ExitStack
    with ExitStack() as _ctx:
        _sb = lambda *a, **k: _ctx.enter_context(nc.sbuf_tensor(*a, **k))
        _ps = lambda *a, **k: _ctx.enter_context(nc.psum_tensor(*a, **k))
        _sem = lambda n: _ctx.enter_context(nc.semaphore(n))
        BIG_A = _sb("BIG_A", [128, 32768], BF16)
        BIG_B = _sb("BIG_B", [128, 32768], BF16)
        CTXB = _sb("CTX", [128, 16384], BF16)
        QTS = _sb("QTS", [128, 2 * TQ], BF16)
        xin = _sb("xin", [128, 2 * E], F32)
        xlnb = _sb("xlnb", [128, 2 * E], BF16)
        krb = _sb("krb", [128, 6 * 512], BF16)
        mnv = _sb("mnv", [128, 32], F32)
        vrv = _sb("vrv", [128, 32], F32)
        rstb = _sb("rstb", [128, 32], F32)
        st12 = _sb("st12", [128, 12], F32)
        mv2 = _sb("mv2", [128, 2], F32)
        rs1 = _sb("rs1", [128, 1], F32)
        qm2 = _sb("qm2", [128, 16], F32)
        km2 = _sb("km2", [128, 16], F32)
        nrm8 = _sb("nrm8", [128, 8], F32)
        mx16 = _sb("mx16", [128, 32], BF16)
        c16 = _sb("c16", [16, 4], F32)
        cnegb = _sb("cnegb", [16, 1], BF16)
        cnr = _sb("cnr", [1, 16], BF16)
        tmpn = _sb("tmpn", [128, E], F32)
        cbias = _sb("cbias", [128, 16], F32)
        cpairb = _sb("cpairb", [128, 8], F32)
        zindb = _sb("zindb", [128, 128], F32)
        zonesb = _sb("zonesb", [128, 32], BF16)
        identb = _sb("identb", [128, 128], BF16)
        onesbb = _sb("onesbb", [1, 128], BF16)
        onesfb = _sb("onesfb", [1, 128], F32)
        bobr = _sb("bobr", [1, E], BF16)
        bqr = _sb("bqr", [1, E], BF16)
        bkr = _sb("bkr", [1, E], BF16)
        bvr = _sb("bvr", [1, E], BF16)
        b1s = _sb("b1s", [128, HCn], F32)
        zse = _sb("zse", [128, 512], F32)
        rzb = _sb("rzb", [128, 1024], F32)
        P8 = _ps("P8", [128, 4096], F32)
        s_ci = _sem("s_ci")
        s_w = _sem("s_w")
        s_b2 = _sem("s_b2")
        s_x1p = [_sem("s_x1a"), _sem("s_x1b")]
        s_bn1 = _sem("s_bn1")
        s_r1 = _sem("s_r1")
        s_x2p = [_sem("s_x2a"), _sem("s_x2b")]
        s_ln = _sem("s_ln")
        s_tpa = _sem("s_tpa")
        s_eva = _sem("s_eva")
        s_qkv = _sem("s_qkv")
        s_qe = _sem("s_qe")
        s_rope = _sem("s_rope")
        s_tpb = _sem("s_tpb")
        s_kte = _sem("s_kte")
        s_scwp = [_sem("s_scwa"), _sem("s_scwb")]
        s_ve = _sem("s_ve")
        s_nrm = _sem("s_nrm")
        s_o2s = _sem("s_o2s")
        s_o2f = _sem("s_o2f")
        s_vwp = [_sem("s_vwa"), _sem("s_vwb")]
        s_cb = _sem("s_cb")
        s_sbp = [_sem("s_sba"), _sem("s_sbb")]
        s_sc = _sem("s_sc")
        s_exp = _sem("s_exp")
        s_pvz = _sem("s_pvz")
        s_pv = _sem("s_pv")
        s_zse = _sem("s_zse")
        s_fold = _sem("s_fold")
        s_rz = _sem("s_rz")
        s_ctx = _sem("s_ctx")
        s_op = _sem("s_op")
        s_bnc = _sem("s_bnc")
        s_rsc = _sem("s_rsc")
        s_xrp = [_sem("s_xra"), _sem("s_xrb")]
        s_x2n = _sem("s_x2n")
        s_x2wp = [_sem("s_x2wa"), _sem("s_x2wb")]
        s_w1 = _sem("s_w1")
        s_w2p = [_sem("s_w2a"), _sem("s_w2b")]
        s_w2c = _sem("s_w2c")
        s_w1c = _sem("s_w1c")
        s_w1h = [_sem("s_w1ha"), _sem("s_w1hb")]
        s_l3sp = [_sem("s_l3sa"), _sem("s_l3sb")]
        s_bnd = _sem("s_bnd")
        s_b3e = _sem("s_b3e")
        s_rs3 = _sem("s_rs3")
        s_l3ap = [_sem("s_l3aa"), _sem("s_l3ab")]
        s_ln3 = _sem("s_ln3")
        s_tpd = _sem("s_tpd")
        s_evd = _sem("s_evd")
        s_mm1 = _sem("s_mm1")
        s_hq = _sem("s_hq")
        s_mm2 = _sem("s_mm2")
        s_o2 = _sem("s_o2")
        s_rs4 = _sem("s_rs4")
        s_frp = [_sem("s_fra"), _sem("s_frb")]
        s_outp = [_sem("s_outa"), _sem("s_outb")]
        block = _ctx.enter_context(nc.Block())
        xT = BIG_A[:]
        PT_A = BIG_A[:, 0:16384]
        PT_B = BIG_A[:, 16384:32768]
        w1T_sb = BIG_A[:]

        wq_sb = BIG_B[:, 0:8192]
        wk_sb = BIG_B[:, 8192:16384]
        wv_sb = BIG_B[:, 16384:24576]
        wo_sb = BIG_B[:, 24576:32768]

        def kts(s):
            return BIG_B[:, s * 10240: s * 10240 + 4096]

        def vstr(s):
            return BIG_B[:, s * 10240 + 4096: s * 10240 + 8192]

        def qts(s):
            return QTS[:, s * TQ:(s + 1) * TQ]

        w2_sb = BIG_B[:].bitcast(E4M3)[:, 0:32768]
        xT2 = BIG_B[:, 16384:24576]
        o2q = BIG_B[:].bitcast(FP16)[:, 24576:32768]
        ctxT = CTXB[:]
        hqT = CTXB[:].bitcast(E4M3)

        ps_qkv = [P8[:, i * 512:(i + 1) * 512] for i in range(4)]
        ps_tp = P8[:, 2048:3072].bitcast(BF16)
        ps_rtp = P8[:, 3072:4096].bitcast(BF16)
        ps_bc = P8[:, 3072:3584]
        ps_st = [P8[:, m * 1024:(m + 1) * 1024] for m in range(3)]
        ps_ot = P8[:, 3072:3584]
        ps_zf = P8[:, 3584:4096]
        ps_m2 = [P8[:, 3072:3584], P8[:, 3584:4096]]

        CTXF = CTXB[:].bitcast(F32)
        sqb = CTXF[:, 3072:4096]
        nrmall = CTXF[:, 0:768]
        zltv = tmpn[:][:, 0:512]
        b2qb = rzb[:]
        b2row = CTXF[0:1, 5120:6144]
        b2h = CTXB[:].bitcast(FP16)[0:1, 12288:13312]
        QTSF = QTS[:].bitcast(F32)
        rkcs = QTSF[:, 0:1024]
        rkss = QTSF[:, 1024:2048]
        rqcs = CTXF[:, 4096:4608]
        rqss = CTXF[:, 4608:5120]
        outb = QTSF[:, 0:2048]
        ktst = CTXB[:, 13312:14336]
        vtl = CTXB[:, 14336:15360]

        NCI = 9
        NW = 8

        fbs_all = []
        for i in range(NTK):
            fl = [("k", 0, i), ("k", 1, i), ("v", 0, i), ("v", 1, i)]
            if i < NTQ:
                fl += [("q", 0, i), ("q", 1, i)]
            fbs_all += fl
        NROPE = NTK * 2 + NTQ * 2
        NV = NTK * 2

        def emit(eng, e):
            if eng == "sync":
                e.dma_start(out=identb[:], in_=ident_t.ap()[:, :]).then_inc(s_ci, 16)
                e.dma_start(out=zindb[:], in_=zind_t.ap()[:, :]).then_inc(s_ci, 16)
                e.dma_start(out=zonesb[:], in_=zones_t.ap()[:, :]).then_inc(s_ci, 16)
                e.dma_start(out=onesbb[:], in_=onesb_t.ap()[:, :]).then_inc(s_ci, 16)
                e.dma_start(out=rqcs, in_=rqc_i.ap()[:, :]).then_inc(s_ci, 16)
                e.dma_start(out=rqss, in_=rqs_i.ap()[:, :]).then_inc(s_ci, 16)
                e.dma_start(out=rkcs, in_=rkc_i.ap()[:, :]).then_inc(s_ci, 16)
                e.dma_start(out=rkss, in_=rks_i.ap()[:, :]).then_inc(s_ci, 16)
                e.dma_start(out=b1s[:], in_=b1_i.ap()[:, :]).then_inc(s_ci, 16)
            if eng == "gpsimd":
                e.dma_start(out=wq_sb.rearrange("p (c f) -> p c f", c=8),
                            in_=wqT.ap()[:, :].rearrange("(c p) f -> p c f", p=128)).then_inc(s_w, 16)
                e.dma_start(out=wk_sb.rearrange("p (c f) -> p c f", c=8),
                            in_=wkT.ap()[:, :].rearrange("(c p) f -> p c f", p=128)).then_inc(s_w, 16)
                e.dma_start(out=wv_sb.rearrange("p (c f) -> p c f", c=8),
                            in_=wvT.ap()[:, :].rearrange("(c p) f -> p c f", p=128)).then_inc(s_w, 16)
                e.dma_start(out=wo_sb.rearrange("p (c f) -> p c f", c=8),
                            in_=woT.ap()[:, :].rearrange("(c p) f -> p c f", p=128)).then_inc(s_w, 16)
                e.dma_start(out=bobr[:], in_=bo_i.ap()[0:1, :]).then_inc(s_w, 16)
                e.dma_start(out=bqr[:], in_=bq_i.ap()[0:1, :]).then_inc(s_w, 16)
                e.dma_start(out=bkr[:], in_=bk_i.ap()[0:1, :]).then_inc(s_w, 16)
                e.dma_start(out=bvr[:], in_=bv_i.ap()[0:1, :]).then_inc(s_w, 16)
            def fbs_of_tile(t):
                fl = [("k", 0, t), ("k", 1, t), ("v", 0, t), ("v", 1, t)]
                if t < NTQ:
                    fl += [("q", 0, t), ("q", 1, t)]
                return fl
            fb_j = {}
            fb_nq = {}
            fb_nv = {}
            _j = _nq = _nv = 0
            cons = []
            for t in range(NTK):
                for fb in fbs_of_tile(t):
                    fb_j[(fb[0], fb[1], t)] = _j
                    if fb[0] == "v":
                        fb_nv[(fb[1], t)] = _nv
                        cons.append(("ve", _nv + 1))
                        _nv += 1
                    else:
                        fb_nq[(fb[0], fb[1], t)] = _nq
                        cons.append(("qe", _nq + 1))
                        _nq += 1
                    _j += 1

            if eng == "sync":
                for i in range(NTK):
                    if i >= 2:
                        e.wait_ge(s_ln, i - 1)
                    e.dma_start(out=xin[:, (i % 2) * E:(i % 2 + 1) * E],
                                in_=x_in.ap()[i * 128:(i + 1) * 128, :]).then_inc(s_x1p[i % 2], 16)

            if eng == "vector":
                def _norm_x(i):
                    e.wait_ge(s_r1, i + 1)
                    if i >= 2:
                        e.wait_ge(s_tpa, i - 1)
                    xi = xin[:, (i % 2) * E:(i % 2 + 1) * E]
                    e.tensor_scalar(xlnb[:, (i % 2) * E:(i % 2 + 1) * E], xi,
                                    mnv[:, i:i + 1], rstb[:, i:i + 1],
                                    ALU.subtract, ALU.mult).then_inc(s_ln, 1)
                    e.drain()
                def _evac_xT(i):
                    e.wait_ge(s_tpa, i + 1)
                    pt = ps_tp[:, (i % 2) * 1024:(i % 2 + 1) * 1024]
                    e.tensor_copy(
                        xT.rearrange("p (c t) -> p c t", c=8)[:, :, i * 128:(i + 1) * 128],
                        pt.rearrange("p (c t) -> p c t", c=8)).then_inc(s_eva, 1)
                    e.drain()
                def _rope_fb(kind, half, t):
                    j = fb_j[(kind, half, t)]
                    nq = fb_nq[(kind, half, t)]
                    e.wait_ge(s_qkv, j + 1)
                    ps = ps_qkv[j % 4]
                    ct = (rqcs if kind == "q" else rkcs)[:, t * 32: t * 32 + 32]
                    stt = (rqss if kind == "q" else rkss)[:, t * 32: t * 32 + 32]
                    if nq >= 6:
                        e.wait_ge(s_tpb, nq - 5)
                    rsl = krb[:, (nq % 6) * 512:(nq % 6) * 512 + 512]
                    kv = ps.rearrange("p (h two f) -> p h two f", h=8, two=2)
                    rv = rsl.rearrange("p (h two f) -> p h two f", h=8, two=2)
                    sq3 = sqb[:, 0:512].rearrange("p (h two f) -> p h two f", h=8, two=2)
                    cB = ct.unsqueeze(1).broadcast_to([128, 8, 32])
                    sB = stt.unsqueeze(1).broadcast_to([128, 8, 32])
                    qe_, qo = kv[:, :, 0, :], kv[:, :, 1, :]
                    re_, ro = rv[:, :, 0, :], rv[:, :, 1, :]
                    e.tensor_tensor(re_, qe_, cB, ALU.mult)
                    e.tensor_tensor(ro, qo, cB, ALU.mult)
                    e.tensor_tensor(sq3[:, :, 0, :], qo, sB, ALU.mult)
                    e.tensor_tensor(sq3[:, :, 1, :], qe_, sB,
                                    ALU.mult).then_inc(s_qe, 1)
                    e.drain()
                    e.tensor_sub(re_, re_, sq3[:, :, 0, :])
                    e.tensor_add(ro, ro, sq3[:, :, 1, :]).then_inc(s_rope, 1)
                    e.drain()
                def _sq_fb(kind, half, t):
                    nq = fb_nq[(kind, half, t)]
                    rsl = krb[:, (nq % 6) * 512:(nq % 6) * 512 + 512]
                    e.tensor_tensor(sqb[:, 512:1024], rsl, rsl, ALU.mult)
                    e.drain()
                    e.tensor_reduce(nrmall[:, nq * 8: nq * 8 + 8],
                                    sqb[:, 512:1024].rearrange("p (h f) -> p h f", h=8),
                                    AX.X, ALU.add)
                    e.drain()
                for i in range(NTK + 3):
                    if 0 <= i - 1 < NTK:
                        _norm_x(i - 1)
                    if 0 <= i - 2 < NTK:
                        _evac_xT(i - 2)
                    if i < NTK:
                        e.wait_ge(s_x1p[i % 2], 16 * (i // 2 + 1))
                        xi = xin[:, (i % 2) * E:(i % 2 + 1) * E]
                        e.bn_stats(st12[:, 0:6], xi[:, 0:512])
                        e.bn_stats(st12[:, 6:12], xi[:, 512:1024])
                        e.drain()
                        e.bn_aggr(mv2[:], st12[:])
                        e.drain()
                        e.tensor_copy(mnv[:, i:i + 1], mv2[:, 0:1])
                        e.tensor_scalar_add(vrv[:, i:i + 1], mv2[:, 1:2],
                                            float(EPS)).then_inc(s_bn1, 1)
                        e.drain()
                    if 0 <= i - 3 < NTK:
                        for fb in fbs_of_tile(i - 3):
                            if fb[0] != "v":
                                _rope_fb(*fb)
                        for fb in fbs_of_tile(i - 3):
                            if fb[0] != "v":
                                _sq_fb(*fb)

            if eng == "scalar":
                def _evac_rope(r):
                    e.wait_ge(s_tpb, r + 1)
                    if r >= 2:
                        e.wait_ge(s_scwp[r % 2], 16 * (r // 2))
                    ptr = ps_rtp[:, (r % 2) * 1024:(r % 2) * 1024 + 512]
                    e.activation(ktst[:, (r % 2) * 512:(r % 2) * 512 + 512],
                                 ptr, AF.Copy).then_inc(s_kte, 1)
                nev = [0]
                for i in range(NTK + 4):
                    if i < NTK:
                        e.wait_ge(s_bn1, i + 1)
                        e.activation(rstb[:, i:i + 1], vrv[:, i:i + 1], AF.Ln)
                        e.drain()
                        e.activation(rstb[:, i:i + 1], rstb[:, i:i + 1], AF.Exp,
                                     scale=-0.5).then_inc(s_r1, 1)
                        e.drain()
                    if 0 <= i - 4 < NTK:
                        for fb in fbs_of_tile(i - 4):
                            if fb[0] != "v":
                                _evac_rope(nev[0])
                                nev[0] += 1
                    if 0 <= i - 3 < NTK:
                        for fb in fbs_of_tile(i - 3):
                            if fb[0] == "v":
                                j = fb_j[fb]
                                nv = fb_nv[(fb[1], fb[2])]
                                e.wait_ge(s_qkv, j + 1)
                                if nv >= 2:
                                    e.wait_ge(s_vwp[nv % 2], 16 * (nv // 2))
                                ps = ps_qkv[j % 4]
                                slot = vtl[:, (nv % 2) * 512:(nv % 2) * 512 + 512]
                                e.activation(slot, ps, AF.Copy).then_inc(s_ve, 1)
                while nev[0] < NROPE:
                    _evac_rope(nev[0])
                    nev[0] += 1

            if eng == "tensor":
                e.wait_ge(s_ci, 16 * NCI)
                e.wait_ge(s_w, 16 * NW)
                def _tp2(i):
                    e.wait_ge(s_ln, i + 1)
                    if i >= 2:
                        e.wait_ge(s_eva, i - 1)
                    srcx = xlnb[:, (i % 2) * E:(i % 2 + 1) * E]
                    pt = ps_tp[:, (i % 2) * 1024:(i % 2 + 1) * 1024]
                    for c in range(8):
                        mm = e.transpose(pt[:, c * 128:(c + 1) * 128],
                                         srcx[:, c * 128:(c + 1) * 128], identb[:])
                    mm.then_inc(s_tpa, 1)
                def _qkv_tile(t):
                    e.wait_ge(s_eva, t + 1)
                    for fb in fbs_of_tile(t):
                        kind, half, _t = fb
                        j = fb_j[fb]
                        if j >= 4:
                            sem, val = cons[j - 4]
                            e.wait_ge(s_ve if sem == "ve" else s_qe, val)
                        ps = ps_qkv[j % 4]
                        wsb = {"q": wq_sb, "k": wk_sb, "v": wv_sb}[kind]
                        brow = {"q": bqr, "k": bkr, "v": bvr}[kind]
                        for ec in range(ECH):
                            e.matmul(ps, xT[:, ec * 4096 + _t * 128: ec * 4096 + (_t + 1) * 128],
                                     wsb[:, ec * 1024 + half * 512: ec * 1024 + half * 512 + 512],
                                     start=(ec == 0), stop=False)
                        e.matmul(ps, onesbb[:], brow[:, half * 512: half * 512 + 512],
                                 start=False, stop=True).then_inc(s_qkv, 1)
                ntp = [0]
                def _rope_tp_tile(t):
                    for fb in fbs_of_tile(t):
                        if fb[0] == "v":
                            continue
                        r = ntp[0]
                        e.wait_ge(s_rope, r + 1)
                        if r >= 2:
                            e.wait_ge(s_kte, r - 1)
                        srcr = krb[:, (r % 6) * 512:(r % 6) * 512 + 512]
                        ptr = ps_rtp[:, (r % 2) * 1024:(r % 2) * 1024 + 512]
                        for c in range(4):
                            mm = e.transpose(ptr[:, c * 128:(c + 1) * 128],
                                             srcr[:, c * 128:(c + 1) * 128], identb[:])
                        mm.then_inc(s_tpb, 1)
                        ntp[0] += 1
                for i in range(NTK + 3):
                    if 0 <= i - 1 < NTK:
                        _tp2(i - 1)
                    if 0 <= i - 2 < NTK:
                        _qkv_tile(i - 2)
                    if 0 <= i - 3 < NTK:
                        _rope_tp_tile(i - 3)

            if eng == "gpsimd":
                r = 0
                nv = 0
                for t in range(NTK):
                    for (kind, half, i) in fbs_of_tile(t):
                        if kind == "v":
                            e.wait_ge(s_ve, nv + 1)
                            slot = vtl[:, (nv % 2) * 512:(nv % 2) * 512 + 512]
                            e.dma_start(out=v_scr.ap()[i * 128:(i + 1) * 128,
                                                       half * 512: half * 512 + 512],
                                        in_=slot).then_inc(s_vwp[nv % 2], 16)
                            nv += 1
                        else:
                            e.wait_ge(s_kte, r + 1)
                            scr = kt_scr if kind == "k" else qt_scr
                            st3 = ktst[:, (r % 2) * 512:(r % 2) * 512 + 512].rearrange(
                                "p (c t) -> p c t", c=4)
                            dst = scr.ap()[:, :].rearrange("(cc p) t -> p cc t", p=128)[
                                :, half * 4:(half + 1) * 4, i * 128:(i + 1) * 128]
                            e.dma_start(out=dst, in_=st3).then_inc(s_scwp[r % 2], 16)
                            r += 1

            if eng == "vector":
                nv3 = nrmall.rearrange("p (n h) -> p n h", h=8)
                def _cls_max(dst, idxs):
                    first = True
                    for (base, step, cnt) in idxs:
                        view = bass.AP(nrmall, base * 8,
                                       [[1, 128], [8 * step, cnt], [1, 8]]) if False else                             nv3[:, base:base + step * (cnt - 1) + 1:step, :]
                        e.tensor_reduce(nrm8[:], view.transpose([0, 2, 1]), AX.X, ALU.max)
                        e.drain()
                        if first:
                            e.tensor_copy(dst, nrm8[:])
                        else:
                            e.tensor_max(dst, dst, nrm8[:])
                        e.drain()
                        first = False
                _cls_max(km2[:, 0:8], [(0, 4, 16), (64, 2, 16)])
                _cls_max(km2[:, 8:16], [(1, 4, 16), (65, 2, 16)])
                _cls_max(qm2[:, 0:8], [(2, 4, 16)])
                _cls_max(qm2[:, 8:16], [(3, 4, 16)])
                e.tensor_copy(mx16[:, 0:16], qm2[:])
                e.tensor_copy(mx16[:, 16:32], km2[:]).then_inc(s_cb, 1)
                e.drain()
            if eng == "tensor":
                e.wait_ge(s_cb, 1)
                e.wait_ge(s_kte, NROPE)
                e.transpose(ps_tp[0:16, 0:128], mx16[:, 0:16], identb[:])
                e.transpose(ps_tp[0:16, 128:256], mx16[:, 16:32],
                            identb[:]).then_inc(s_cb, 1)
            if eng == "vector":
                e.wait_ge(s_cb, 2)
                e.tensor_reduce(c16[:, 0:1], ps_tp[0:16, 0:128], AX.X, ALU.max)
                e.tensor_reduce(c16[:, 1:2], ps_tp[0:16, 128:256], AX.X, ALU.max)
                e.drain()
                e.tensor_tensor(c16[:, 2:3], c16[:, 0:1], c16[:, 1:2],
                                ALU.mult).then_inc(s_cb, 1)
                e.drain()
            if eng == "scalar":
                e.wait_ge(s_cb, 3)
                e.activation(c16[:, 3:4], c16[:, 2:3], AF.Ln)
                e.drain()
                e.activation(c16[:, 3:4], c16[:, 3:4], AF.Exp, scale=0.5).then_inc(s_cb, 1)
                e.drain()
            if eng == "vector":
                e.wait_ge(s_cb, 4)
                e.tensor_scalar(cnegb[:], c16[:, 3:4], 1.0, -1.0,
                                ALU.add, ALU.mult).then_inc(s_cb, 1)
                e.drain()
            if eng == "tensor":
                e.wait_ge(s_cb, 5)
                e.matmul(ps_bc[0:1, 32:160], cnegb[:], identb[0:16, 0:128],
                         start=True, stop=True).then_inc(s_cb, 1)
            if eng == "vector":
                e.wait_ge(s_cb, 6)
                e.tensor_copy(cnr[:], ps_bc[0:1, 32:48]).then_inc(s_cb, 1)
                e.drain()
            if eng == "tensor":
                e.wait_ge(s_cb, 7)
                e.matmul(ps_bc[:, 0:16], onesbb[:], cnr[:],
                         start=True, stop=True).then_inc(s_cb, 1)
            if eng == "vector":
                e.wait_ge(s_cb, 8)
                e.tensor_copy(cbias[:], ps_bc[:, 0:16])
                e.drain()
                cv = cbias[:].rearrange("p (a b) -> p a b", b=2)
                e.tensor_tensor(cpairb[:], cv[:, :, 0], cv[:, :, 1],
                                ALU.min).then_inc(s_cb, 1)
                e.drain()

            for hp in range(HPn):
                if eng == "gpsimd":
                    if hp == 0:
                        e.wait_ge(s_scwp[0], 16 * (NROPE // 2))
                        e.wait_ge(s_scwp[1], 16 * (NROPE // 2))
                        e.wait_ge(s_vwp[0], 16 * (NV // 2))
                        e.wait_ge(s_vwp[1], 16 * (NV // 2))
                    if hp >= 2:
                        e.wait_ge(s_ctx, (hp - 1) * QBn)
                    s = hp % 2
                    e.dma_start(out=kts(s),
                                in_=kt_scr.ap()[hp * 128:(hp + 1) * 128, :]).then_inc(s_sbp[s], 16)
                    e.dma_start(out=vstr(s).rearrange("p (kc f) -> p kc f", kc=32),
                                in_=v_scr.ap()[:, :].rearrange("(kc p) f -> p kc f", p=128)[
                                    :, :, hp * 128:(hp + 1) * 128]).then_inc(s_sbp[s], 16)
                    e.dma_start(out=qts(s),
                                in_=qt_scr.ap()[hp * 128:(hp + 1) * 128, :]).then_inc(s_sbp[s], 16)
                for qb in range(QBn):
                    idx = hp * QBn + qb
                    s = hp % 2
                    if eng == "tensor" and idx == 0:
                        NID = HPn * QBn
                        def _pv(jdx, kc):
                            sj = (jdx // QBn) % 2
                            e.matmul(ps_ot[0:64, :],
                                     vstr(sj)[:, kc * 128: kc * 128 + 64],
                                     PT_A[:, kc * 512:(kc + 1) * 512],
                                     start=(kc == 0), stop=(kc == KC - 1),
                                     tile_position=(0, 0))
                            m2 = e.matmul(ps_ot[64:128, :],
                                     vstr(sj)[:, kc * 128 + 64: kc * 128 + 128],
                                     PT_B[:, kc * 512:(kc + 1) * 512],
                                     start=(kc == 0), stop=(kc == KC - 1),
                                     tile_position=(0, 64))
                            if kc == KC - 1:
                                m2.then_inc(s_pv, 1)
                        def _z(jdx, g2):
                            for tt, (pt, kc) in enumerate((
                                    (PT_A, 2 * g2), (PT_B, 2 * g2),
                                    (PT_A, 2 * g2 + 1), (PT_B, 2 * g2 + 1))):
                                mm = e.matmul(ps_zf[tt * 32:(tt + 1) * 32, :], zonesb[:],
                                              pt[:, kc * 512:(kc + 1) * 512],
                                              start=(g2 == 0),
                                              stop=(g2 == 15),
                                              tile_position=(0, tt * 32))
                            return mm
                        ZLAG, PVLAG = 5, 8
                        for t in range(NID * 32 + PVLAG):
                            jdx, kc = t // 32, t % 32
                            if jdx < NID:
                                hpj, qbj = jdx // QBn, jdx % QBn
                                sj = hpj % 2
                                if kc == 0 and qbj == 0:
                                    e.wait_ge(s_sbp[sj], 48 * (hpj // 2 + 1))
                                    if jdx == 0:
                                        e.wait_ge(s_cb, 9)
                                if kc == 5 and jdx > 0:
                                    e.wait_ge(s_zse, jdx)
                                    e.matmul(ps_zf, zindb[:], zse[:],
                                             start=True, stop=True).then_inc(s_fold, 1)
                                if t >= 3:
                                    e.wait_ge(s_exp, t - 2)
                                m = t % 3
                                e.matmul(ps_st[m][:, 0:512],
                                         kts(sj)[0:64, kc * 128:(kc + 1) * 128],
                                         qts(sj)[0:64, qbj * 512:(qbj + 1) * 512],
                                         start=True, stop=True, tile_position=(0, 0))
                                e.matmul(ps_st[m][:, 512:1024],
                                         kts(sj)[64:128, kc * 128:(kc + 1) * 128],
                                         qts(sj)[64:128, qbj * 512:(qbj + 1) * 512],
                                         start=True, stop=True,
                                         tile_position=(64, 0)).then_inc(s_sc, 1)
                            u = t - ZLAG
                            if u >= 0 and u < NID * 32 and u % 2 == 1:
                                pdxz, uz = u // 32, u % 32
                                p = uz // 2
                                if jdx >= NID:
                                    e.wait_ge(s_exp, pdxz * 32 + 2 * p + 2)
                                if p == 0 and pdxz > 0:
                                    e.wait_ge(s_rz, pdxz)
                                zm = _z(pdxz, p)
                                if p == 15:
                                    zm.then_inc(s_pvz, 1)
                            v = t - PVLAG
                            if v >= 0:
                                pdx, pkc = v // 32, v % 32
                                if jdx >= NID:
                                    e.wait_ge(s_exp, pdx * 32 + pkc + 1)
                                if pkc == 0 and pdx > 0:
                                    e.wait_ge(s_ctx, pdx)
                                _pv(pdx, pkc)
                        e.wait_ge(s_zse, NID)
                        e.wait_ge(s_rz, NID - 1)
                        e.matmul(ps_zf, zindb[:], zse[:],
                                 start=True, stop=True).then_inc(s_fold, 1)
                    if eng == "scalar":
                        for kc in range(KC):
                            if kc == 6 and idx > 0:
                                j = idx - 1
                                e.wait_ge(s_fold, idx)
                                if j >= 1:
                                    e.wait_ge(s_ctx, j)
                                e.activation(zltv, ps_zf, AF.Ln)
                                e.drain()
                                e.activation(rzb[:, (j % 2) * 512:(j % 2) * 512 + 512],
                                             zltv, AF.Exp, scale=-1.0).then_inc(s_rz, 1)
                                e.drain()
                            e.wait_ge(s_sc, idx * 32 + kc + 1)
                            m = (idx * 32 + kc) % 3
                            outv = BIG_A[:].rearrange("p (h x) -> p h x", h=2)[
                                :, :, kc * 512:(kc + 1) * 512]
                            e.activation(outv, ps_st[m], AF.Exp,
                                         bias=cpairb[:, hp:hp + 1]).then_inc(s_exp, 1)
                        if idx == HPn * QBn - 1:
                            j = idx
                            e.wait_ge(s_fold, idx + 1)
                            e.wait_ge(s_ctx, j)
                            e.activation(zltv, ps_zf, AF.Ln)
                            e.drain()
                            e.activation(rzb[:, (j % 2) * 512:(j % 2) * 512 + 512],
                                         zltv, AF.Exp, scale=-1.0).then_inc(s_rz, 1)
                            e.drain()
                    if eng == "vector":
                        if idx > 0:
                            j = idx - 1
                            hpj, qbj = j // QBn, j % QBn
                            e.wait_ge(s_rz, j + 1)
                            e.wait_ge(s_pv, j + 1)
                            e.tensor_tensor(
                                ctxT[:, hpj * 2048 + qbj * 512: hpj * 2048 + (qbj + 1) * 512],
                                ps_ot,
                                rzb[:, (j % 2) * 512:(j % 2) * 512 + 512],
                                ALU.mult).then_inc(s_ctx, 1)
                            e.drain()
                        e.wait_ge(s_pvz, idx + 1)
                        e.tensor_copy(zse[:], ps_zf).then_inc(s_zse, 1)
                        e.drain()

            if eng == "vector":
                j = HPn * QBn - 1
                hpj, qbj = j // QBn, j % QBn
                e.wait_ge(s_rz, j + 1)
                e.wait_ge(s_pv, j + 1)
                e.tensor_tensor(
                    ctxT[:, hpj * 2048 + qbj * 512: hpj * 2048 + (qbj + 1) * 512],
                    ps_ot,
                    rzb[:, (j % 2) * 512:(j % 2) * 512 + 512],
                    ALU.mult).then_inc(s_ctx, 1)
                e.drain()

            if eng == "gpsimd":
                e.dma_start(out=w1q_scr.ap()[:, :],
                            in_=w1T.ap()[:, :]).then_inc(s_w1c, 16)
                e.dma_start(out=w2q_scr.ap()[:, :],
                            in_=w2T.ap()[:, :]).then_inc(s_w2c, 16)

            if eng == "sync":
                e.wait_ge(s_ctx, HPn * QBn)
                e.wait_ge(s_rz, HPn * QBn)
                e.wait_ge(s_w1c, 16)
                e.wait_ge(s_w2c, 16)
                for wh in range(2):
                    e.dma_start(
                        out=w1T_sb.rearrange("p (c f) -> p c f", c=8)[:, :, wh * 2048:(wh + 1) * 2048],
                        in_=w1q_scr.ap()[:, :].rearrange("(c p) f -> p c f", p=128)[:, :, wh * 2048:(wh + 1) * 2048]
                        ).then_inc(s_w1h[wh], 16)
                e.dma_start(out=w2_sb.rearrange("p (c f) -> p c f", c=32),
                            in_=w2q_scr.ap()[:, :].rearrange("(c p) f -> p c f", p=128)
                            ).then_inc(s_w2p[0], 16)
                e.dma_start(out=b2qb,
                            in_=b2_i.ap()[0:1, :].partition_broadcast(128)).then_inc(s_b2, 16)
                for t in range(NTQ):
                    if t >= 2:
                        e.wait_ge(s_x2wp[t % 2], 16 * (t // 2))
                        e.wait_ge(s_ln3, min(8, t - 1))
                        e.wait_ge(s_bnd, t - 1)
                    e.dma_start(out=outb[:, (t % 2) * E:(t % 2 + 1) * E],
                                in_=x_in.ap()[t * 128:(t + 1) * 128, :]).then_inc(s_xrp[t % 2], 16)
            if eng == "tensor":
                ps_o6 = P8[:, 3072:4096]
                def _pso(t, dh):
                    if t % 3 == 2:
                        return ps_o6[:, dh * 512:(dh + 1) * 512]
                    return ps_qkv[(t % 3) * 2 + dh]
                def _tp3(i):
                    e.wait_ge(s_ln3, i + 1)
                    if i == 0:
                        e.wait_ge(s_exp, HPn * QBn * 32)
                    if i >= 2:
                        e.wait_ge(s_evd, i - 1)
                    src3 = xlnb[:, (i % 2) * E:(i % 2 + 1) * E]
                    pt3 = ps_tp[:, (i % 2) * 1024:(i % 2 + 1) * 1024]
                    for c in range(8):
                        mm3 = e.transpose(pt3[:, c * 128:(c + 1) * 128],
                                          src3[:, c * 128:(c + 1) * 128], identb[:])
                    mm3.then_inc(s_tpd, 1)
                for t in range(NTQ):
                    if t == 2:
                        e.wait_ge(s_zse, HPn * QBn)
                        e.wait_ge(s_rz, HPn * QBn)
                        e.wait_ge(s_ctx, HPn * QBn)
                    if t >= 3:
                        e.wait_ge(s_x2n, t - 2)
                    for dh in range(2):
                        ps = _pso(t, dh)
                        for c in range(ECH):
                            e.matmul(ps, ctxT[:, c * 2048 + t * 128: c * 2048 + (t + 1) * 128],
                                     wo_sb[:, c * 1024 + dh * 512: c * 1024 + dh * 512 + 512],
                                     start=(c == 0), stop=False)
                        mm = e.matmul(ps, onesbb[:], bobr[:, dh * 512: dh * 512 + 512],
                                      start=False, stop=True)
                    mm.then_inc(s_op, 1)
                    if 0 <= t - 2 < 8:
                        _tp3(t - 2)
            if eng == "vector":
                ps_o6v = P8[:, 3072:4096]
                def _psov(t, dh):
                    if t % 3 == 2:
                        return ps_o6v[:, dh * 512:(dh + 1) * 512]
                    return ps_qkv[(t % 3) * 2 + dh]
                for t in range(NTQ):
                    e.wait_ge(s_op, t + 1)
                    p0 = _psov(t, 0)
                    p1 = _psov(t, 1)
                    e.bn_stats(st12[:, 0:6], p0)
                    e.bn_stats(st12[:, 6:12], p1)
                    e.drain()
                    e.bn_aggr(mv2[:], st12[:])
                    e.drain()
                    e.tensor_scalar_add(rs1[:], mv2[:, 1:2], float(EPS)).then_inc(s_bnc, 1)
                    e.drain()
                    e.wait_ge(s_rsc, t + 1)
                    e.wait_ge(s_xrp[t % 2], 16 * (t // 2 + 1))
                    for dh in range(2):
                        ps = _psov(t, dh)
                        e.tensor_scalar(tmpn[:, dh * 512: dh * 512 + 512], ps,
                                        mv2[:, 0:1], rs1[:], ALU.subtract, ALU.mult)
                    e.drain()
                    e.tensor_add(outb[:, (t % 2) * E:(t % 2 + 1) * E],
                                 tmpn[:],
                                 outb[:, (t % 2) * E:(t % 2 + 1) * E]).then_inc(s_x2n, 1)
                    e.drain()
                    c3 = t if t < 8 else t + 8
                    ob = outb[:, (t % 2) * E:(t % 2 + 1) * E]
                    e.bn_stats(st12[:, 0:6], ob[:, 0:512])
                    e.bn_stats(st12[:, 6:12], ob[:, 512:1024])
                    e.drain()
                    e.bn_aggr(mv2[:], st12[:])
                    e.drain()
                    e.tensor_copy(mnv[:, c3:c3 + 1], mv2[:, 0:1])
                    e.tensor_scalar_add(vrv[:, c3:c3 + 1], mv2[:, 1:2],
                                        float(EPS)).then_inc(s_bnd, 1)
                    e.drain()
                    if 0 <= t - 1 < 8:
                        i = t - 1
                        e.wait_ge(s_rs3, i + 1)
                        if i >= 2:
                            e.wait_ge(s_tpd, i - 1)
                        ob = outb[:, (i % 2) * E:(i % 2 + 1) * E]
                        e.tensor_scalar(xlnb[:, (i % 2) * E:(i % 2 + 1) * E], ob,
                                        mnv[:, i:i + 1], rstb[:, i:i + 1],
                                        ALU.subtract, ALU.mult).then_inc(s_ln3, 1)
                        e.drain()
            if eng == "scalar":
                def _evac3a(i):
                    e.wait_ge(s_tpd, i + 1)
                    pt3 = ps_tp[:, (i % 2) * 1024:(i % 2 + 1) * 1024]
                    e.activation(
                        xT2.rearrange("p (c t) -> p c t", c=8)[:, :, i * 128:(i + 1) * 128],
                        pt3.rearrange("p (c t) -> p c t", c=8), AF.Copy).then_inc(s_evd, 1)
                for t in range(NTQ):
                    e.wait_ge(s_bnc, t + 1)
                    e.activation(rs1[:], rs1[:], AF.Ln)
                    e.drain()
                    e.activation(rs1[:], rs1[:], AF.Exp, scale=-0.5).then_inc(s_rsc, 1)
                    e.drain()
                    if 0 <= t - 3 < 8:
                        _evac3a(t - 3)
                    if 0 <= t - 1 < NTQ:
                        i3 = t - 1
                        cc = i3 if i3 < 8 else i3 + 8
                        e.wait_ge(s_bnd, i3 + 1)
                        e.activation(rstb[:, cc:cc + 1], vrv[:, cc:cc + 1], AF.Ln)
                        e.drain()
                        e.activation(rstb[:, cc:cc + 1], rstb[:, cc:cc + 1], AF.Exp,
                                     scale=-0.5).then_inc(s_rs3, 1)
                        e.drain()
                    if t == NTQ - 1:
                        e.wait_ge(s_bnd, NTQ)
                        e.activation(rstb[:, 23:24], vrv[:, 23:24], AF.Ln)
                        e.drain()
                        e.activation(rstb[:, 23:24], rstb[:, 23:24], AF.Exp,
                                     scale=-0.5).then_inc(s_rs3, 1)
                        e.drain()
            if eng == "gpsimd":
                for t in range(NTQ):
                    e.wait_ge(s_x2n, t + 1)
                    e.dma_start(out=x2_scr.ap()[t * 128:(t + 1) * 128, :],
                                in_=outb[:, (t % 2) * E:(t % 2 + 1) * E]).then_inc(s_x2wp[t % 2], 16)

            for th in range(2):
                tb0 = th * 8
                if eng == "sync" and th == 1:
                    e.wait_ge(s_o2f, 16)
                    for i in range(8):
                        t = tb0 + i
                        g = th * 8 + i
                        e.wait_ge(s_x2wp[t % 2], 16 * (t // 2 + 1))
                        if i >= 2:
                            e.wait_ge(s_ln3, th * 8 + i - 1)
                        e.dma_start(out=xin[:, (i % 2) * E:(i % 2 + 1) * E],
                                    in_=x2_scr.ap()[t * 128:(t + 1) * 128, :]).then_inc(s_l3sp[i % 2], 16)
                if eng == "vector" and th == 1:
                    def _norm3(i):
                        e.wait_ge(s_rs3, th * 8 + i + 1)
                        if i >= 2:
                            e.wait_ge(s_tpd, th * 8 + i - 1)
                        xi = xin[:, (i % 2) * E:(i % 2 + 1) * E]
                        e.tensor_scalar(xlnb[:, (i % 2) * E:(i % 2 + 1) * E], xi,
                                        mnv[:, 16 + i:17 + i], rstb[:, 16 + i:17 + i],
                                        ALU.subtract, ALU.mult).then_inc(s_ln3, 1)
                        e.drain()
                    def _evac3(i):
                        e.wait_ge(s_tpd, th * 8 + i + 1)
                        pt = ps_tp[:, (i % 2) * 1024:(i % 2 + 1) * 1024]
                        e.tensor_copy(
                            xT2.rearrange("p (c t) -> p c t", c=8)[:, :, i * 128:(i + 1) * 128],
                            pt.rearrange("p (c t) -> p c t", c=8)).then_inc(s_evd, 1)
                        e.drain()
                    for i in range(8):
                        e.wait_ge(s_l3sp[i % 2], 16 * (i // 2 + 1))
                        _norm3(i)
                        if i >= 2:
                            _evac3(i - 2)
                    _evac3(6)
                    _evac3(7)
                if eng == "tensor" and th == 1:
                    for i in range(8):
                        e.wait_ge(s_ln3, th * 8 + i + 1)
                        if i >= 2:
                            e.wait_ge(s_evd, th * 8 + i - 1)
                        src = xlnb[:, (i % 2) * E:(i % 2 + 1) * E]
                        pt = ps_tp[:, (i % 2) * 1024:(i % 2 + 1) * 1024]
                        for c in range(8):
                            mm = e.transpose(pt[:, c * 128:(c + 1) * 128],
                                             src[:, c * 128:(c + 1) * 128], identb[:])
                        mm.then_inc(s_tpd, 1)
                if eng == "tensor":
                    if th == 0:
                        e.wait_ge(s_x2n, NTQ)
                        e.wait_ge(s_zse, HPn * QBn)
                        e.wait_ge(s_rz, HPn * QBn)
                        e.wait_ge(s_ctx, HPn * QBn)
                    def _mm1(tb, hc):
                        j = tb * HCn + hc
                        if th == 0 and tb == 0 and hc in (0, 16):
                            e.wait_ge(s_w1h[hc // 16], 16)
                        if j >= 4:
                            e.wait_ge(s_hq, th * 64 + j - 3)
                        ps = ps_qkv[j % 4]
                        for ec in range(ECH):
                            mm = e.matmul(ps,
                                          w1T_sb[:, ec * 4096 + hc * 128: ec * 4096 + (hc + 1) * 128],
                                          xT2[:, ec * 1024 + tb * 512: ec * 1024 + tb * 512 + 512],
                                          start=(ec == 0), stop=(ec == ECH - 1))
                        mm.then_inc(s_mm1, 1)
                    hq3 = hqT.rearrange("p (hc x) -> p hc x", hc=HCn)
                    w23 = w2_sb.rearrange("p (hc x) -> p hc x", hc=HCn)
                    def _mm2(i):
                        if th == 0 and i == 0:
                            e.wait_ge(s_w2p[0], 16)
                        e.wait_ge(s_hq, th * 64 + (i // 4 + 1) * 32)
                        if th * 8 + i >= 1:
                            e.wait_ge(s_o2, th * 8 + i)
                        for dh in range(2):
                            ps = ps_m2[dh]
                            for h in range(HCn // 2):
                                mm = e.matmul(ps,
                                              hq3[:, 2 * h:2 * h + 2, i * 128:(i + 1) * 128],
                                              w23[:, 2 * h:2 * h + 2, dh * 512:dh * 512 + 512],
                                              start=(h == 0), stop=(h == HCn // 2 - 1),
                                              perf_mode=mybir.MatmulPerfMode.DoubleRow)
                        mm.then_inc(s_mm2, 1)
                    e.wait_ge(s_evd, th * 8 + 4)
                    for hc in range(HCn):
                        _mm1(0, hc)
                    e.wait_ge(s_evd, th * 8 + 8)
                    for i in range(4):
                        _mm2(i)
                        for hc in range(8 * i, 8 * i + 8):
                            _mm1(1, hc)
                    for i in range(4, 8):
                        _mm2(i)
                if eng == "sync":
                    for i in range(8):
                        t = tb0 + i
                        e.wait_ge(s_ln3, (th + 1) * 8)
                        if i >= 2:
                            e.wait_ge(s_o2f, 2 * (th * 8 + i - 1))
                        g = th * 8 + i
                        e.dma_start(out=xin[:, (i % 2) * E:(i % 2 + 1) * E],
                                    in_=x2_scr.ap()[t * 128:(t + 1) * 128, :]).then_inc(s_frp[g % 2], 16)
                if eng == "vector":
                    if th == 0:
                        e.wait_ge(s_x2wp[0], 16 * (NTQ // 2))
                        e.wait_ge(s_x2wp[1], 16 * (NTQ // 2))
                        e.wait_ge(s_b2, 16)
                        stg = o2q[:, 7 * 1024:8 * 1024]
                        e.tensor_copy(stg, b2qb)
                        e.drain()
                        e.tensor_copy(b2qb, stg)
                        e.drain()
                    def _pass1(i):
                        e.wait_ge(s_mm2, th * 8 + i + 1)
                        ot = o2q[:, i * 1024:(i + 1) * 1024]
                        for dh in range(2):
                            ps = ps_m2[dh]
                            mmv = e.tensor_tensor(ot[:, dh * 512: dh * 512 + 512], ps,
                                                  b2qb[:, dh * 512: dh * 512 + 512], ALU.add)
                        mmv.then_inc(s_o2, 1)
                        e.drain()
                        e.bn_stats(st12[:, 0:6], ot[:, 0:512])
                        e.bn_stats(st12[:, 6:12], ot[:, 512:1024])
                        e.drain()
                        e.bn_aggr(mv2[:], st12[:])
                        e.drain()
                        e.tensor_copy(mnv[:, 8 + i:9 + i], mv2[:, 0:1])
                        e.tensor_scalar_add(vrv[:, 8 + i:9 + i], mv2[:, 1:2],
                                            float(EPS)).then_inc(s_o2s, 1)
                        e.drain()
                    def _pass2(i):
                        g = th * 8 + i
                        ot = o2q[:, i * 1024:(i + 1) * 1024]
                        e.wait_ge(s_rs4, g + 1)
                        e.wait_ge(s_frp[g % 2], 16 * (g // 2 + 1))
                        if g >= 2:
                            e.wait_ge(s_outp[g % 2], 32 * (g // 2))
                        for hh in range(2):
                            hs = hh * 512
                            e.tensor_scalar(tmpn[:, hs:hs + 512], ot[:, hs:hs + 512],
                                            mnv[:, 8 + i:9 + i], rstb[:, 8 + i:9 + i],
                                            ALU.subtract, ALU.mult)
                            e.drain()
                            e.tensor_add(outb[:, (i % 2) * E + hs:(i % 2) * E + hs + 512],
                                         tmpn[:, hs:hs + 512],
                                         xin[:, (i % 2) * E + hs:(i % 2) * E + hs + 512]
                                         ).then_inc(s_o2f, 1)
                            e.drain()
                    for i in range(8):
                        _pass1(i)
                        if i >= 2:
                            _pass2(i - 2)
                    _pass2(6)
                    _pass2(7)
                if eng == "scalar":
                    def _rstd4(i):
                        e.wait_ge(s_o2s, th * 8 + i + 1)
                        e.activation(rstb[:, 8 + i:9 + i], vrv[:, 8 + i:9 + i], AF.Ln)
                        e.drain()
                        e.activation(rstb[:, 8 + i:9 + i], rstb[:, 8 + i:9 + i], AF.Exp,
                                     scale=-0.5).then_inc(s_rs4, 1)
                        e.drain()
                    n4 = [0]
                    for tb in range(2):
                        for hc in range(HCn):
                            j = tb * HCn + hc
                            e.wait_ge(s_mm1, th * 64 + j + 1)
                            e.activation(
                                hqT[:, hc * 1024 + tb * 512: hc * 1024 + tb * 512 + 512],
                                ps_qkv[j % 4], AF.Relu,
                                bias=b1s[:, hc:hc + 1]).then_inc(s_hq, 1)
                            if tb == 1 and hc % 8 == 7:
                                _rstd4(n4[0])
                                n4[0] += 1
                    while n4[0] < 8:
                        _rstd4(n4[0])
                        n4[0] += 1
                if eng == "gpsimd":
                    for i in range(8):
                        t = tb0 + i
                        g = th * 8 + i
                        for hh in range(2):
                            hs = hh * 512
                            e.wait_ge(s_o2f, 2 * g + hh + 1)
                            e.dma_start(out=out_o.ap()[t * 128:(t + 1) * 128, hs:hs + 512],
                                        in_=outb[:, (i % 2) * E + hs:(i % 2) * E + hs + 512]
                                        ).then_inc(s_outp[g % 2], 16)

            if eng == "gpsimd":
                e.wait_ge(s_outp[0], 32 * 8)
                e.wait_ge(s_outp[1], 32 * 8)

        @block.sync
        def _(e):
            emit("sync", e)

        @block.gpsimd
        def _(e):
            emit("gpsimd", e)

        @block.vector
        def _(e):
            emit("vector", e)

        @block.scalar
        def _(e):
            emit("scalar", e)

        @block.tensor
        def _(e):
            emit("tensor", e)

    return nc


_CACHE = {}


def _tables(positions, scale):
    inv = 1.0 / (10000.0 ** (np.arange(0, HD, 2, dtype=np.float64) / HD))
    ang = positions[:, None].astype(np.float64) * inv[None, :]
    c = (np.cos(ang) * scale).astype(np.float32)
    s = (np.sin(ang) * scale).astype(np.float32)
    T = positions.shape[0]
    nt = T // 128
    cm = np.ascontiguousarray(c.reshape(nt, 128, 32).transpose(1, 0, 2).reshape(128, nt * 32))
    sm = np.ascontiguousarray(s.reshape(nt, 128, 32).transpose(1, 0, 2).reshape(128, nt * 32))
    return cm, sm


def kernel(L0_attn_x, L0_attn_Wq, L0_attn_bq, L0_attn_Wk, L0_attn_bk,
           L0_attn_Wv, L0_attn_bv, L0_attn_Wo, L0_attn_bo,
           L0_ffn_w1, L0_ffn_b1, L0_ffn_w2, L0_ffn_b2):
    x = np.ascontiguousarray(np.asarray(L0_attn_x, dtype=np.float32))
    B, Lfull, _ = x.shape

    perm = []
    for h in range(NH):
        perm += [h * HD + 2 * i for i in range(HD // 2)]
        perm += [h * HD + 2 * i + 1 for i in range(HD // 2)]
    perm = np.asarray(perm)

    wqT = np.ascontiguousarray(np.asarray(L0_attn_Wq, dtype=np.float32)[perm, :].T)
    wkT = np.ascontiguousarray(np.asarray(L0_attn_Wk, dtype=np.float32)[perm, :].T)
    wvT = np.ascontiguousarray(np.asarray(L0_attn_Wv, dtype=np.float32).T)
    woT = np.ascontiguousarray(np.asarray(L0_attn_Wo, dtype=np.float32).T)
    w1T = np.ascontiguousarray(np.asarray(L0_ffn_w1, dtype=np.float32).T)
    w2T = np.ascontiguousarray(np.asarray(L0_ffn_w2, dtype=np.float32).T)
    bq = np.ascontiguousarray(np.asarray(L0_attn_bq, dtype=np.float32)[perm][None, :])
    bk = np.ascontiguousarray(np.asarray(L0_attn_bk, dtype=np.float32)[perm][None, :])
    bv = np.ascontiguousarray(np.asarray(L0_attn_bv, dtype=np.float32)[None, :])
    bo = np.ascontiguousarray(np.asarray(L0_attn_bo, dtype=np.float32)[None, :])
    b2 = np.ascontiguousarray(np.asarray(L0_ffn_b2, dtype=np.float32)[None, :])
    b1 = np.ascontiguousarray(
        np.asarray(L0_ffn_b1, dtype=np.float32).reshape(HCn, 128).T)

    if "nc" not in _CACHE:
        _CACHE["nc"] = build_nc()
    nc = _CACHE["nc"]

    in_maps = []
    for core in range(8):
        b, h = core // 2, core % 2
        own = x[b, h * TQ:(h + 1) * TQ]
        oth = x[b, (1 - h) * TQ:(2 - h) * TQ]
        x_core = np.ascontiguousarray(np.concatenate([own, oth], axis=0))
        posQ = np.arange(h * TQ, (h + 1) * TQ)
        posK = np.concatenate([posQ, np.arange((1 - h) * TQ, (2 - h) * TQ)])
        rqc, rqs = _tables(posQ, 1.0 / np.sqrt(HD))
        rkc, rks = _tables(posK, 1.0)
        in_maps.append({
            "x_in": x_core, "wqT": wqT, "wkT": wkT, "wvT": wvT, "woT": woT,
            "w1T": w1T, "w2T": w2T, "bq_i": bq, "bk_i": bk, "bv_i": bv,
            "bo_i": bo, "b2_i": b2, "b1_i": b1,
            "rqc_i": rqc, "rqs_i": rqs, "rkc_i": rkc, "rks_i": rks,
        })

    res = run_bass_kernel_spmd(nc, in_maps, core_ids=list(range(8)))
    out = np.empty((B, Lfull, E), dtype=np.float32)
    for core in range(8):
        b, h = core // 2, core % 2
        out[b, h * TQ:(h + 1) * TQ] = res.results[core]["out"]
    return out


# version 37
# speedup vs baseline: 1.0310x; 1.0034x over previous
import numpy as np
import ml_dtypes
import concourse.bass as bass
import concourse.mybir as mybir
from concourse.bass_utils import run_bass_kernel_spmd

F32 = mybir.dt.float32
BF16 = mybir.dt.bfloat16
FP16 = mybir.dt.float16
E4M3 = mybir.dt.float8e4
AF = mybir.ActivationFunctionType
ALU = mybir.AluOpType
AX = mybir.AxisListType

E = 1024
NH = 16
HD = 64
L = 4096
TQ = 2048
NTK = 32
NTQ = 16
ECH = 8
HPn = 8
QBn = 4
KC = 32
HF = 4096
HCn = 32
EPS = 1e-5


def build_nc():
    nc = bass.Bass("TRN2", target_bir_lowering=False)

    x_in = nc.dram_tensor("x_in", [L, E], F32, kind="ExternalInput")
    wqT = nc.dram_tensor("wqT", [E, E], F32, kind="ExternalInput")
    wkT = nc.dram_tensor("wkT", [E, E], F32, kind="ExternalInput")
    wvT = nc.dram_tensor("wvT", [E, E], F32, kind="ExternalInput")
    woT = nc.dram_tensor("woT", [E, E], F32, kind="ExternalInput")
    w1T = nc.dram_tensor("w1T", [E, HF], F32, kind="ExternalInput")
    w2T = nc.dram_tensor("w2T", [HF, E], F32, kind="ExternalInput")
    bq_i = nc.dram_tensor("bq_i", [1, E], F32, kind="ExternalInput")
    bk_i = nc.dram_tensor("bk_i", [1, E], F32, kind="ExternalInput")
    bv_i = nc.dram_tensor("bv_i", [1, E], F32, kind="ExternalInput")
    bo_i = nc.dram_tensor("bo_i", [1, E], F32, kind="ExternalInput")
    b2_i = nc.dram_tensor("b2_i", [1, E], F32, kind="ExternalInput")
    b1_i = nc.dram_tensor("b1_i", [128, HCn], F32, kind="ExternalInput")
    rqc_i = nc.dram_tensor("rqc_i", [128, NTQ * 32], F32, kind="ExternalInput")
    rqs_i = nc.dram_tensor("rqs_i", [128, NTQ * 32], F32, kind="ExternalInput")
    rkc_i = nc.dram_tensor("rkc_i", [128, NTK * 32], F32, kind="ExternalInput")
    rks_i = nc.dram_tensor("rks_i", [128, NTK * 32], F32, kind="ExternalInput")
    out_o = nc.dram_tensor("out", [TQ, E], F32, kind="ExternalOutput")

    kt_scr = nc.dram_tensor("kt_scr", [E, L], BF16)
    qt_scr = nc.dram_tensor("qt_scr", [E, TQ], BF16)
    v_scr = nc.dram_tensor("v_scr", [L, E], BF16)
    x2_scr = nc.dram_tensor("x2_scr", [TQ, E], F32)
    w1q_scr = nc.dram_tensor("w1q_scr", [E, HF], BF16)
    w2q_scr = nc.dram_tensor("w2q_scr", [HF, E], E4M3)

    identN = np.eye(128, dtype=ml_dtypes.bfloat16)
    zindN = np.zeros((128, 128), dtype=np.float32)
    zindN[0, 0:64] = 1.0
    zindN[64, 0:64] = 1.0
    zindN[32, 64:128] = 1.0
    zindN[96, 64:128] = 1.0
    zonesN = np.zeros((128, 32), dtype=ml_dtypes.bfloat16)
    zonesN[:, 0] = 1.0
    ident_t = nc.inline_tensor(identN, "ident_t")
    zind_t = nc.inline_tensor(zindN, "zind_t")
    zones_t = nc.inline_tensor(zonesN, "zones_t")
    onesb_t = nc.inline_tensor(np.ones((1, 128), dtype=ml_dtypes.bfloat16), "onesb_t")
    onesf_t = nc.inline_tensor(np.ones((1, 128), dtype=np.float32), "onesf_t")

    from contextlib import ExitStack
    with ExitStack() as _ctx:
        _sb = lambda *a, **k: _ctx.enter_context(nc.sbuf_tensor(*a, **k))
        _ps = lambda *a, **k: _ctx.enter_context(nc.psum_tensor(*a, **k))
        _sem = lambda n: _ctx.enter_context(nc.semaphore(n))
        BIG_A = _sb("BIG_A", [128, 32768], BF16)
        BIG_B = _sb("BIG_B", [128, 32768], BF16)
        CTXB = _sb("CTX", [128, 16384], BF16)
        QTS = _sb("QTS", [128, 2 * TQ], BF16)
        xin = _sb("xin", [128, 2 * E], F32)
        xlnb = _sb("xlnb", [128, 2 * E], BF16)
        krb = _sb("krb", [128, 6 * 512], BF16)
        mnv = _sb("mnv", [128, 32], F32)
        vrv = _sb("vrv", [128, 32], F32)
        rstb = _sb("rstb", [128, 32], F32)
        st12 = _sb("st12", [128, 12], F32)
        mv2 = _sb("mv2", [128, 2], F32)
        rs1 = _sb("rs1", [128, 1], F32)
        qm2 = _sb("qm2", [128, 16], F32)
        km2 = _sb("km2", [128, 16], F32)
        nrm8 = _sb("nrm8", [128, 8], F32)
        mx16 = _sb("mx16", [128, 32], BF16)
        c16 = _sb("c16", [16, 4], F32)
        cnegb = _sb("cnegb", [16, 1], BF16)
        cnr = _sb("cnr", [1, 16], BF16)
        tmpn = _sb("tmpn", [128, E], F32)
        cbias = _sb("cbias", [128, 16], F32)
        cpairb = _sb("cpairb", [128, 8], F32)
        zindb = _sb("zindb", [128, 128], F32)
        zonesb = _sb("zonesb", [128, 32], BF16)
        identb = _sb("identb", [128, 128], BF16)
        onesbb = _sb("onesbb", [1, 128], BF16)
        onesfb = _sb("onesfb", [1, 128], F32)
        bobr = _sb("bobr", [1, E], BF16)
        bqr = _sb("bqr", [1, E], BF16)
        bkr = _sb("bkr", [1, E], BF16)
        bvr = _sb("bvr", [1, E], BF16)
        b1s = _sb("b1s", [128, HCn], F32)
        zse = _sb("zse", [128, 512], F32)
        rzb = _sb("rzb", [128, 1024], F32)
        P8 = _ps("P8", [128, 4096], F32)
        s_ci = _sem("s_ci")
        s_w = _sem("s_w")
        s_b2 = _sem("s_b2")
        s_x1p = [_sem("s_x1a"), _sem("s_x1b")]
        s_bn1 = _sem("s_bn1")
        s_r1 = _sem("s_r1")
        s_x2p = [_sem("s_x2a"), _sem("s_x2b")]
        s_ln = _sem("s_ln")
        s_tpa = _sem("s_tpa")
        s_eva = _sem("s_eva")
        s_qkv = _sem("s_qkv")
        s_qe = _sem("s_qe")
        s_rope = _sem("s_rope")
        s_tpb = _sem("s_tpb")
        s_kte = _sem("s_kte")
        s_scwp = [_sem("s_scwa"), _sem("s_scwb")]
        s_ve = _sem("s_ve")
        s_nrm = _sem("s_nrm")
        s_o2s = _sem("s_o2s")
        s_o2f = _sem("s_o2f")
        s_vwp = [_sem("s_vwa"), _sem("s_vwb")]
        s_cb = _sem("s_cb")
        s_sbp = [_sem("s_sba"), _sem("s_sbb")]
        s_sc = _sem("s_sc")
        s_exp = _sem("s_exp")
        s_pvz = _sem("s_pvz")
        s_pv = _sem("s_pv")
        s_zse = _sem("s_zse")
        s_fold = _sem("s_fold")
        s_rz = _sem("s_rz")
        s_ctx = _sem("s_ctx")
        s_op = _sem("s_op")
        s_bnc = _sem("s_bnc")
        s_rsc = _sem("s_rsc")
        s_xrp = [_sem("s_xra"), _sem("s_xrb")]
        s_x2n = _sem("s_x2n")
        s_x2wp = [_sem("s_x2wa"), _sem("s_x2wb")]
        s_w1 = _sem("s_w1")
        s_w2p = [_sem("s_w2a"), _sem("s_w2b")]
        s_w2c = _sem("s_w2c")
        s_w1c = _sem("s_w1c")
        s_w1h = [_sem("s_w1ha"), _sem("s_w1hb")]
        s_l3sp = [_sem("s_l3sa"), _sem("s_l3sb")]
        s_bnd = _sem("s_bnd")
        s_b3e = _sem("s_b3e")
        s_rs3 = _sem("s_rs3")
        s_l3ap = [_sem("s_l3aa"), _sem("s_l3ab")]
        s_ln3 = _sem("s_ln3")
        s_tpd = _sem("s_tpd")
        s_evd = _sem("s_evd")
        s_mm1 = _sem("s_mm1")
        s_hq = _sem("s_hq")
        s_mm2 = _sem("s_mm2")
        s_o2 = _sem("s_o2")
        s_rs4 = _sem("s_rs4")
        s_frp = [_sem("s_fra"), _sem("s_frb")]
        s_outp = [_sem("s_outa"), _sem("s_outb")]
        block = _ctx.enter_context(nc.Block())
        xT = BIG_A[:]
        PT_A = BIG_A[:, 0:16384]
        PT_B = BIG_A[:, 16384:32768]
        w1T_sb = BIG_A[:]

        wq_sb = BIG_B[:, 0:8192]
        wk_sb = BIG_B[:, 8192:16384]
        wv_sb = BIG_B[:, 16384:24576]
        wo_sb = BIG_B[:, 24576:32768]

        def kts(s):
            return BIG_B[:, s * 10240: s * 10240 + 4096]

        def vstr(s):
            return BIG_B[:, s * 10240 + 4096: s * 10240 + 8192]

        def qts(s):
            return QTS[:, s * TQ:(s + 1) * TQ]

        w2_sb = BIG_B[:].bitcast(E4M3)[:, 0:32768]
        xT2 = BIG_B[:, 16384:24576]
        o2q = BIG_B[:].bitcast(FP16)[:, 24576:32768]
        ctxT = CTXB[:]
        hqT = CTXB[:].bitcast(E4M3)

        ps_qkv = [P8[:, i * 512:(i + 1) * 512] for i in range(4)]
        ps_tp = P8[:, 2048:3072].bitcast(BF16)
        ps_rtp = P8[:, 3072:4096].bitcast(BF16)
        ps_bc = P8[:, 3072:3584]
        ps_st = [P8[:, m * 1024:(m + 1) * 1024] for m in range(3)]
        ps_ot = P8[:, 3072:3584]
        ps_zf = P8[:, 3584:4096]
        ps_m2 = [P8[:, 3072:3584], P8[:, 3584:4096]]

        CTXF = CTXB[:].bitcast(F32)
        sqb = CTXF[:, 3072:4096]
        nrmall = CTXF[:, 0:768]
        zltv = tmpn[:][:, 0:512]
        b2qb = rzb[:]
        b2row = CTXF[0:1, 5120:6144]
        b2h = CTXB[:].bitcast(FP16)[0:1, 12288:13312]
        QTSF = QTS[:].bitcast(F32)
        rkcs = QTSF[:, 0:1024]
        rkss = QTSF[:, 1024:2048]
        rqcs = CTXF[:, 4096:4608]
        rqss = CTXF[:, 4608:5120]
        outb = QTSF[:, 0:2048]
        ktst = CTXB[:, 13312:14336]
        vtl = CTXB[:, 14336:15360]

        NCI = 9
        NW = 8

        fbs_all = []
        for i in range(NTK):
            fl = [("k", 0, i), ("k", 1, i), ("v", 0, i), ("v", 1, i)]
            if i < NTQ:
                fl += [("q", 0, i), ("q", 1, i)]
            fbs_all += fl
        NROPE = NTK * 2 + NTQ * 2
        NV = NTK * 2

        def emit(eng, e):
            if eng == "sync":
                e.dma_start(out=identb[:], in_=ident_t.ap()[:, :]).then_inc(s_ci, 16)
                e.dma_start(out=zindb[:], in_=zind_t.ap()[:, :]).then_inc(s_ci, 16)
                e.dma_start(out=zonesb[:], in_=zones_t.ap()[:, :]).then_inc(s_ci, 16)
                e.dma_start(out=onesbb[:], in_=onesb_t.ap()[:, :]).then_inc(s_ci, 16)
                e.dma_start(out=rqcs, in_=rqc_i.ap()[:, :]).then_inc(s_ci, 16)
                e.dma_start(out=rqss, in_=rqs_i.ap()[:, :]).then_inc(s_ci, 16)
                e.dma_start(out=rkcs, in_=rkc_i.ap()[:, :]).then_inc(s_ci, 16)
                e.dma_start(out=rkss, in_=rks_i.ap()[:, :]).then_inc(s_ci, 16)
                e.dma_start(out=b1s[:], in_=b1_i.ap()[:, :]).then_inc(s_ci, 16)
            if eng == "gpsimd":
                e.dma_start(out=wq_sb.rearrange("p (c f) -> p c f", c=8),
                            in_=wqT.ap()[:, :].rearrange("(c p) f -> p c f", p=128)).then_inc(s_w, 16)
                e.dma_start(out=wk_sb.rearrange("p (c f) -> p c f", c=8),
                            in_=wkT.ap()[:, :].rearrange("(c p) f -> p c f", p=128)).then_inc(s_w, 16)
                e.dma_start(out=wv_sb.rearrange("p (c f) -> p c f", c=8),
                            in_=wvT.ap()[:, :].rearrange("(c p) f -> p c f", p=128)).then_inc(s_w, 16)
                e.dma_start(out=wo_sb.rearrange("p (c f) -> p c f", c=8),
                            in_=woT.ap()[:, :].rearrange("(c p) f -> p c f", p=128)).then_inc(s_w, 16)
                e.dma_start(out=bobr[:], in_=bo_i.ap()[0:1, :]).then_inc(s_w, 16)
                e.dma_start(out=bqr[:], in_=bq_i.ap()[0:1, :]).then_inc(s_w, 16)
                e.dma_start(out=bkr[:], in_=bk_i.ap()[0:1, :]).then_inc(s_w, 16)
                e.dma_start(out=bvr[:], in_=bv_i.ap()[0:1, :]).then_inc(s_w, 16)
            def fbs_of_tile(t):
                fl = [("k", 0, t), ("k", 1, t), ("v", 0, t), ("v", 1, t)]
                if t < NTQ:
                    fl += [("q", 0, t), ("q", 1, t)]
                return fl
            fb_j = {}
            fb_nq = {}
            fb_nv = {}
            _j = _nq = _nv = 0
            cons = []
            for t in range(NTK):
                for fb in fbs_of_tile(t):
                    fb_j[(fb[0], fb[1], t)] = _j
                    if fb[0] == "v":
                        fb_nv[(fb[1], t)] = _nv
                        cons.append(("ve", _nv + 1))
                        _nv += 1
                    else:
                        fb_nq[(fb[0], fb[1], t)] = _nq
                        cons.append(("qe", _nq + 1))
                        _nq += 1
                    _j += 1

            if eng == "sync":
                for i in range(NTK):
                    if i >= 2:
                        e.wait_ge(s_ln, i - 1)
                    e.dma_start(out=xin[:, (i % 2) * E:(i % 2 + 1) * E],
                                in_=x_in.ap()[i * 128:(i + 1) * 128, :]).then_inc(s_x1p[i % 2], 16)

            if eng == "vector":
                def _norm_x(i):
                    e.wait_ge(s_r1, i + 1)
                    if i >= 2:
                        e.wait_ge(s_tpa, i - 1)
                    xi = xin[:, (i % 2) * E:(i % 2 + 1) * E]
                    e.tensor_scalar(xlnb[:, (i % 2) * E:(i % 2 + 1) * E], xi,
                                    mnv[:, i:i + 1], rstb[:, i:i + 1],
                                    ALU.subtract, ALU.mult).then_inc(s_ln, 1)
                    e.drain()
                def _evac_xT(i):
                    e.wait_ge(s_tpa, i + 1)
                    pt = ps_tp[:, (i % 2) * 1024:(i % 2 + 1) * 1024]
                    e.tensor_copy(
                        xT.rearrange("p (c t) -> p c t", c=8)[:, :, i * 128:(i + 1) * 128],
                        pt.rearrange("p (c t) -> p c t", c=8)).then_inc(s_eva, 1)
                    e.drain()
                def _rope_fb(kind, half, t):
                    j = fb_j[(kind, half, t)]
                    nq = fb_nq[(kind, half, t)]
                    e.wait_ge(s_qkv, j + 1)
                    ps = ps_qkv[j % 4]
                    ct = (rqcs if kind == "q" else rkcs)[:, t * 32: t * 32 + 32]
                    stt = (rqss if kind == "q" else rkss)[:, t * 32: t * 32 + 32]
                    if nq >= 6:
                        e.wait_ge(s_tpb, nq - 5)
                    rsl = krb[:, (nq % 6) * 512:(nq % 6) * 512 + 512]
                    kv = ps.rearrange("p (h two f) -> p h two f", h=8, two=2)
                    rv = rsl.rearrange("p (h two f) -> p h two f", h=8, two=2)
                    sq3 = sqb[:, 0:512].rearrange("p (h two f) -> p h two f", h=8, two=2)
                    cB = ct.unsqueeze(1).broadcast_to([128, 8, 32])
                    sB = stt.unsqueeze(1).broadcast_to([128, 8, 32])
                    qe_, qo = kv[:, :, 0, :], kv[:, :, 1, :]
                    re_, ro = rv[:, :, 0, :], rv[:, :, 1, :]
                    e.tensor_tensor(re_, qe_, cB, ALU.mult)
                    e.tensor_tensor(ro, qo, cB, ALU.mult)
                    e.tensor_tensor(sq3[:, :, 0, :], qo, sB, ALU.mult)
                    e.tensor_tensor(sq3[:, :, 1, :], qe_, sB,
                                    ALU.mult).then_inc(s_qe, 1)
                    e.drain()
                    e.tensor_sub(re_, re_, sq3[:, :, 0, :])
                    e.tensor_add(ro, ro, sq3[:, :, 1, :]).then_inc(s_rope, 1)
                    e.drain()
                def _sq_fb(kind, half, t):
                    nq = fb_nq[(kind, half, t)]
                    rsl = krb[:, (nq % 6) * 512:(nq % 6) * 512 + 512]
                    e.tensor_tensor(sqb[:, 512:1024], rsl, rsl, ALU.mult)
                    e.drain()
                    e.tensor_reduce(nrmall[:, nq * 8: nq * 8 + 8],
                                    sqb[:, 512:1024].rearrange("p (h f) -> p h f", h=8),
                                    AX.X, ALU.add)
                    e.drain()
                for i in range(NTK + 3):
                    if 0 <= i - 1 < NTK:
                        _norm_x(i - 1)
                    if 0 <= i - 2 < NTK:
                        _evac_xT(i - 2)
                    if i < NTK:
                        e.wait_ge(s_x1p[i % 2], 16 * (i // 2 + 1))
                        xi = xin[:, (i % 2) * E:(i % 2 + 1) * E]
                        e.bn_stats(st12[:, 0:6], xi[:, 0:512])
                        e.bn_stats(st12[:, 6:12], xi[:, 512:1024])
                        e.drain()
                        e.bn_aggr(mv2[:], st12[:])
                        e.drain()
                        e.tensor_copy(mnv[:, i:i + 1], mv2[:, 0:1])
                        e.tensor_scalar_add(vrv[:, i:i + 1], mv2[:, 1:2],
                                            float(EPS)).then_inc(s_bn1, 1)
                        e.drain()
                    if 0 <= i - 3 < NTK:
                        for fb in fbs_of_tile(i - 3):
                            if fb[0] != "v":
                                _rope_fb(*fb)
                        for fb in fbs_of_tile(i - 3):
                            if fb[0] != "v":
                                _sq_fb(*fb)

            if eng == "scalar":
                def _evac_rope(r):
                    e.wait_ge(s_tpb, r + 1)
                    if r >= 2:
                        e.wait_ge(s_scwp[r % 2], 16 * (r // 2))
                    ptr = ps_rtp[:, (r % 2) * 1024:(r % 2) * 1024 + 512]
                    e.activation(ktst[:, (r % 2) * 512:(r % 2) * 512 + 512],
                                 ptr, AF.Copy).then_inc(s_kte, 1)
                nev = [0]
                for i in range(NTK + 4):
                    if i < NTK:
                        e.wait_ge(s_bn1, i + 1)
                        e.activation(rstb[:, i:i + 1], vrv[:, i:i + 1], AF.Ln)
                        e.drain()
                        e.activation(rstb[:, i:i + 1], rstb[:, i:i + 1], AF.Exp,
                                     scale=-0.5).then_inc(s_r1, 1)
                        e.drain()
                    if 0 <= i - 4 < NTK:
                        for fb in fbs_of_tile(i - 4):
                            if fb[0] != "v":
                                _evac_rope(nev[0])
                                nev[0] += 1
                    if 0 <= i - 3 < NTK:
                        for fb in fbs_of_tile(i - 3):
                            if fb[0] == "v":
                                j = fb_j[fb]
                                nv = fb_nv[(fb[1], fb[2])]
                                e.wait_ge(s_qkv, j + 1)
                                if nv >= 2:
                                    e.wait_ge(s_vwp[nv % 2], 16 * (nv // 2))
                                ps = ps_qkv[j % 4]
                                slot = vtl[:, (nv % 2) * 512:(nv % 2) * 512 + 512]
                                e.activation(slot, ps, AF.Copy).then_inc(s_ve, 1)
                while nev[0] < NROPE:
                    _evac_rope(nev[0])
                    nev[0] += 1

            if eng == "tensor":
                e.wait_ge(s_ci, 16 * NCI)
                e.wait_ge(s_w, 16 * NW)
                def _tp2(i):
                    e.wait_ge(s_ln, i + 1)
                    if i >= 2:
                        e.wait_ge(s_eva, i - 1)
                    srcx = xlnb[:, (i % 2) * E:(i % 2 + 1) * E]
                    pt = ps_tp[:, (i % 2) * 1024:(i % 2 + 1) * 1024]
                    for c in range(8):
                        mm = e.transpose(pt[:, c * 128:(c + 1) * 128],
                                         srcx[:, c * 128:(c + 1) * 128], identb[:])
                    mm.then_inc(s_tpa, 1)
                def _qkv_tile(t):
                    e.wait_ge(s_eva, t + 1)
                    for fb in fbs_of_tile(t):
                        kind, half, _t = fb
                        j = fb_j[fb]
                        if j >= 4:
                            sem, val = cons[j - 4]
                            e.wait_ge(s_ve if sem == "ve" else s_qe, val)
                        ps = ps_qkv[j % 4]
                        wsb = {"q": wq_sb, "k": wk_sb, "v": wv_sb}[kind]
                        brow = {"q": bqr, "k": bkr, "v": bvr}[kind]
                        for ec in range(ECH):
                            e.matmul(ps, xT[:, ec * 4096 + _t * 128: ec * 4096 + (_t + 1) * 128],
                                     wsb[:, ec * 1024 + half * 512: ec * 1024 + half * 512 + 512],
                                     start=(ec == 0), stop=False)
                        e.matmul(ps, onesbb[:], brow[:, half * 512: half * 512 + 512],
                                 start=False, stop=True).then_inc(s_qkv, 1)
                ntp = [0]
                def _rope_tp_tile(t):
                    for fb in fbs_of_tile(t):
                        if fb[0] == "v":
                            continue
                        r = ntp[0]
                        e.wait_ge(s_rope, r + 1)
                        if r >= 2:
                            e.wait_ge(s_kte, r - 1)
                        srcr = krb[:, (r % 6) * 512:(r % 6) * 512 + 512]
                        ptr = ps_rtp[:, (r % 2) * 1024:(r % 2) * 1024 + 512]
                        for c in range(4):
                            mm = e.transpose(ptr[:, c * 128:(c + 1) * 128],
                                             srcr[:, c * 128:(c + 1) * 128], identb[:])
                        mm.then_inc(s_tpb, 1)
                        ntp[0] += 1
                for i in range(NTK + 3):
                    if 0 <= i - 1 < NTK:
                        _tp2(i - 1)
                    if 0 <= i - 2 < NTK:
                        _qkv_tile(i - 2)
                    if 0 <= i - 3 < NTK:
                        _rope_tp_tile(i - 3)

            if eng == "gpsimd":
                r = 0
                nv = 0
                for t in range(NTK):
                    for (kind, half, i) in fbs_of_tile(t):
                        if kind == "v":
                            e.wait_ge(s_ve, nv + 1)
                            slot = vtl[:, (nv % 2) * 512:(nv % 2) * 512 + 512]
                            e.dma_start(out=v_scr.ap()[i * 128:(i + 1) * 128,
                                                       half * 512: half * 512 + 512],
                                        in_=slot).then_inc(s_vwp[nv % 2], 16)
                            nv += 1
                        else:
                            e.wait_ge(s_kte, r + 1)
                            scr = kt_scr if kind == "k" else qt_scr
                            st3 = ktst[:, (r % 2) * 512:(r % 2) * 512 + 512].rearrange(
                                "p (c t) -> p c t", c=4)
                            dst = scr.ap()[:, :].rearrange("(cc p) t -> p cc t", p=128)[
                                :, half * 4:(half + 1) * 4, i * 128:(i + 1) * 128]
                            e.dma_start(out=dst, in_=st3).then_inc(s_scwp[r % 2], 16)
                            r += 1

            if eng == "vector":
                nv3 = nrmall.rearrange("p (n h) -> p n h", h=8)
                def _cls_max(dst, idxs):
                    first = True
                    for (base, step, cnt) in idxs:
                        view = bass.AP(nrmall, base * 8,
                                       [[1, 128], [8 * step, cnt], [1, 8]]) if False else                             nv3[:, base:base + step * (cnt - 1) + 1:step, :]
                        e.tensor_reduce(nrm8[:], view.transpose([0, 2, 1]), AX.X, ALU.max)
                        e.drain()
                        if first:
                            e.tensor_copy(dst, nrm8[:])
                        else:
                            e.tensor_max(dst, dst, nrm8[:])
                        e.drain()
                        first = False
                _cls_max(km2[:, 0:8], [(0, 4, 16), (64, 2, 16)])
                _cls_max(km2[:, 8:16], [(1, 4, 16), (65, 2, 16)])
                _cls_max(qm2[:, 0:8], [(2, 4, 16)])
                _cls_max(qm2[:, 8:16], [(3, 4, 16)])
                e.tensor_copy(mx16[:, 0:16], qm2[:])
                e.tensor_copy(mx16[:, 16:32], km2[:]).then_inc(s_cb, 1)
                e.drain()
            if eng == "tensor":
                e.wait_ge(s_cb, 1)
                e.wait_ge(s_kte, NROPE)
                e.transpose(ps_tp[0:16, 0:128], mx16[:, 0:16], identb[:])
                e.transpose(ps_tp[0:16, 128:256], mx16[:, 16:32],
                            identb[:]).then_inc(s_cb, 1)
            if eng == "vector":
                e.wait_ge(s_cb, 2)
                e.tensor_reduce(c16[:, 0:1], ps_tp[0:16, 0:128], AX.X, ALU.max)
                e.tensor_reduce(c16[:, 1:2], ps_tp[0:16, 128:256], AX.X, ALU.max)
                e.drain()
                e.tensor_tensor(c16[:, 2:3], c16[:, 0:1], c16[:, 1:2],
                                ALU.mult).then_inc(s_cb, 1)
                e.drain()
            if eng == "scalar":
                e.wait_ge(s_cb, 3)
                e.activation(c16[:, 3:4], c16[:, 2:3], AF.Ln)
                e.drain()
                e.activation(c16[:, 3:4], c16[:, 3:4], AF.Exp, scale=0.5).then_inc(s_cb, 1)
                e.drain()
            if eng == "vector":
                e.wait_ge(s_cb, 4)
                e.tensor_scalar(cnegb[:], c16[:, 3:4], 1.0, -1.0,
                                ALU.add, ALU.mult).then_inc(s_cb, 1)
                e.drain()
            if eng == "tensor":
                e.wait_ge(s_cb, 5)
                e.matmul(ps_bc[0:1, 32:160], cnegb[:], identb[0:16, 0:128],
                         start=True, stop=True).then_inc(s_cb, 1)
            if eng == "vector":
                e.wait_ge(s_cb, 6)
                e.tensor_copy(cnr[:], ps_bc[0:1, 32:48]).then_inc(s_cb, 1)
                e.drain()
            if eng == "tensor":
                e.wait_ge(s_cb, 7)
                e.matmul(ps_bc[:, 0:16], onesbb[:], cnr[:],
                         start=True, stop=True).then_inc(s_cb, 1)
            if eng == "vector":
                e.wait_ge(s_cb, 8)
                e.tensor_copy(cbias[:], ps_bc[:, 0:16])
                e.drain()
                cv = cbias[:].rearrange("p (a b) -> p a b", b=2)
                e.tensor_tensor(cpairb[:], cv[:, :, 0], cv[:, :, 1],
                                ALU.min).then_inc(s_cb, 1)
                e.drain()

            for hp in range(HPn):
                if eng == "gpsimd":
                    if hp == 0:
                        e.wait_ge(s_scwp[0], 16 * (NROPE // 2))
                        e.wait_ge(s_scwp[1], 16 * (NROPE // 2))
                        e.wait_ge(s_vwp[0], 16 * (NV // 2))
                        e.wait_ge(s_vwp[1], 16 * (NV // 2))
                    if hp >= 2:
                        e.wait_ge(s_ctx, (hp - 1) * QBn)
                    s = hp % 2
                    e.dma_start(out=kts(s),
                                in_=kt_scr.ap()[hp * 128:(hp + 1) * 128, :]).then_inc(s_sbp[s], 16)
                    e.dma_start(out=vstr(s).rearrange("p (kc f) -> p kc f", kc=32),
                                in_=v_scr.ap()[:, :].rearrange("(kc p) f -> p kc f", p=128)[
                                    :, :, hp * 128:(hp + 1) * 128]).then_inc(s_sbp[s], 16)
                    e.dma_start(out=qts(s),
                                in_=qt_scr.ap()[hp * 128:(hp + 1) * 128, :]).then_inc(s_sbp[s], 16)
                for qb in range(QBn):
                    idx = hp * QBn + qb
                    s = hp % 2
                    if eng == "tensor" and idx == 0:
                        NID = HPn * QBn
                        def _pv(jdx, kc):
                            sj = (jdx // QBn) % 2
                            e.matmul(ps_ot[0:64, :],
                                     vstr(sj)[:, kc * 128: kc * 128 + 64],
                                     PT_A[:, kc * 512:(kc + 1) * 512],
                                     start=(kc == 0), stop=(kc == KC - 1),
                                     tile_position=(0, 0))
                            m2 = e.matmul(ps_ot[64:128, :],
                                     vstr(sj)[:, kc * 128 + 64: kc * 128 + 128],
                                     PT_B[:, kc * 512:(kc + 1) * 512],
                                     start=(kc == 0), stop=(kc == KC - 1),
                                     tile_position=(0, 64))
                            if kc == KC - 1:
                                m2.then_inc(s_pv, 1)
                        def _z(jdx, g2):
                            for tt, (pt, kc) in enumerate((
                                    (PT_A, 2 * g2), (PT_B, 2 * g2),
                                    (PT_A, 2 * g2 + 1), (PT_B, 2 * g2 + 1))):
                                mm = e.matmul(ps_zf[tt * 32:(tt + 1) * 32, :], zonesb[:],
                                              pt[:, kc * 512:(kc + 1) * 512],
                                              start=(g2 == 0),
                                              stop=(g2 == 15),
                                              tile_position=(0, tt * 32))
                            return mm
                        ZLAG, PVLAG = 5, 8
                        for t in range(NID * 32 + PVLAG):
                            jdx, kc = t // 32, t % 32
                            if jdx < NID:
                                hpj, qbj = jdx // QBn, jdx % QBn
                                sj = hpj % 2
                                if kc == 0 and qbj == 0:
                                    e.wait_ge(s_sbp[sj], 48 * (hpj // 2 + 1))
                                    if jdx == 0:
                                        e.wait_ge(s_cb, 9)
                                if kc == 5 and jdx > 0:
                                    e.wait_ge(s_zse, jdx)
                                    e.matmul(ps_zf, zindb[:], zse[:],
                                             start=True, stop=True).then_inc(s_fold, 1)
                                if t >= 3:
                                    e.wait_ge(s_exp, t - 2)
                                m = t % 3
                                e.matmul(ps_st[m][:, 0:512],
                                         kts(sj)[0:64, kc * 128:(kc + 1) * 128],
                                         qts(sj)[0:64, qbj * 512:(qbj + 1) * 512],
                                         start=True, stop=True, tile_position=(0, 0))
                                e.matmul(ps_st[m][:, 512:1024],
                                         kts(sj)[64:128, kc * 128:(kc + 1) * 128],
                                         qts(sj)[64:128, qbj * 512:(qbj + 1) * 512],
                                         start=True, stop=True,
                                         tile_position=(64, 0)).then_inc(s_sc, 1)
                            u = t - ZLAG
                            if u >= 0 and u < NID * 32 and u % 2 == 1:
                                pdxz, uz = u // 32, u % 32
                                p = uz // 2
                                if jdx >= NID:
                                    e.wait_ge(s_exp, pdxz * 32 + 2 * p + 2)
                                if p == 0 and pdxz > 0:
                                    e.wait_ge(s_rz, pdxz)
                                zm = _z(pdxz, p)
                                if p == 15:
                                    zm.then_inc(s_pvz, 1)
                            v = t - PVLAG
                            if v >= 0:
                                pdx, pkc = v // 32, v % 32
                                if jdx >= NID:
                                    e.wait_ge(s_exp, pdx * 32 + pkc + 1)
                                if pkc == 0 and pdx > 0:
                                    e.wait_ge(s_ctx, pdx)
                                _pv(pdx, pkc)
                        e.wait_ge(s_zse, NID)
                        e.wait_ge(s_rz, NID - 1)
                        e.matmul(ps_zf, zindb[:], zse[:],
                                 start=True, stop=True).then_inc(s_fold, 1)
                    if eng == "scalar":
                        for kc in range(KC):
                            if kc == 6 and idx > 0:
                                j = idx - 1
                                e.wait_ge(s_fold, idx)
                                if j >= 1:
                                    e.wait_ge(s_ctx, j)
                                e.activation(zltv, ps_zf, AF.Ln)
                                e.drain()
                                e.activation(rzb[:, (j % 2) * 512:(j % 2) * 512 + 512],
                                             zltv, AF.Exp, scale=-1.0).then_inc(s_rz, 1)
                                e.drain()
                            e.wait_ge(s_sc, idx * 32 + kc + 1)
                            m = (idx * 32 + kc) % 3
                            outv = BIG_A[:].rearrange("p (h x) -> p h x", h=2)[
                                :, :, kc * 512:(kc + 1) * 512]
                            e.activation(outv, ps_st[m], AF.Exp,
                                         bias=cpairb[:, hp:hp + 1]).then_inc(s_exp, 1)
                        if idx == HPn * QBn - 1:
                            j = idx
                            e.wait_ge(s_fold, idx + 1)
                            e.wait_ge(s_ctx, j)
                            e.activation(zltv, ps_zf, AF.Ln)
                            e.drain()
                            e.activation(rzb[:, (j % 2) * 512:(j % 2) * 512 + 512],
                                         zltv, AF.Exp, scale=-1.0).then_inc(s_rz, 1)
                            e.drain()
                    if eng == "vector":
                        if idx > 0:
                            j = idx - 1
                            hpj, qbj = j // QBn, j % QBn
                            e.wait_ge(s_rz, j + 1)
                            e.wait_ge(s_pv, j + 1)
                            e.tensor_tensor(
                                ctxT[:, hpj * 2048 + qbj * 512: hpj * 2048 + (qbj + 1) * 512],
                                ps_ot,
                                rzb[:, (j % 2) * 512:(j % 2) * 512 + 512],
                                ALU.mult).then_inc(s_ctx, 1)
                            e.drain()
                        e.wait_ge(s_pvz, idx + 1)
                        e.tensor_copy(zse[:], ps_zf).then_inc(s_zse, 1)
                        e.drain()

            if eng == "vector":
                j = HPn * QBn - 1
                hpj, qbj = j // QBn, j % QBn
                e.wait_ge(s_rz, j + 1)
                e.wait_ge(s_pv, j + 1)
                e.tensor_tensor(
                    ctxT[:, hpj * 2048 + qbj * 512: hpj * 2048 + (qbj + 1) * 512],
                    ps_ot,
                    rzb[:, (j % 2) * 512:(j % 2) * 512 + 512],
                    ALU.mult).then_inc(s_ctx, 1)
                e.drain()

            if eng == "gpsimd":
                e.dma_start(out=w1q_scr.ap()[:, :],
                            in_=w1T.ap()[:, :]).then_inc(s_w1c, 16)
                e.dma_start(out=w2q_scr.ap()[:, :],
                            in_=w2T.ap()[:, :]).then_inc(s_w2c, 16)

            if eng == "sync":
                e.wait_ge(s_ctx, HPn * QBn)
                e.wait_ge(s_rz, HPn * QBn)
                e.wait_ge(s_w1c, 16)
                e.wait_ge(s_w2c, 16)
                for wh in range(2):
                    e.dma_start(
                        out=w1T_sb.rearrange("p (c f) -> p c f", c=8)[:, :, wh * 2048:(wh + 1) * 2048],
                        in_=w1q_scr.ap()[:, :].rearrange("(c p) f -> p c f", p=128)[:, :, wh * 2048:(wh + 1) * 2048]
                        ).then_inc(s_w1h[wh], 16)
                e.dma_start(out=w2_sb.rearrange("p (c f) -> p c f", c=32),
                            in_=w2q_scr.ap()[:, :].rearrange("(c p) f -> p c f", p=128)
                            ).then_inc(s_w2p[0], 16)
                e.dma_start(out=b2qb,
                            in_=b2_i.ap()[0:1, :].partition_broadcast(128)).then_inc(s_b2, 16)
                for t in range(NTQ):
                    if t >= 2:
                        e.wait_ge(s_x2wp[t % 2], 16 * (t // 2))
                        e.wait_ge(s_ln3, min(8, t - 1))
                        e.wait_ge(s_bnd, t - 1)
                    e.dma_start(out=outb[:, (t % 2) * E:(t % 2 + 1) * E],
                                in_=x_in.ap()[t * 128:(t + 1) * 128, :]).then_inc(s_xrp[t % 2], 16)
            if eng == "tensor":
                ps_o6 = P8[:, 3072:4096]
                def _pso(t, dh):
                    if t % 3 == 2:
                        return ps_o6[:, dh * 512:(dh + 1) * 512]
                    return ps_qkv[(t % 3) * 2 + dh]
                def _tp3(i):
                    e.wait_ge(s_ln3, i + 1)
                    if i == 0:
                        e.wait_ge(s_exp, HPn * QBn * 32)
                    if i >= 2:
                        e.wait_ge(s_evd, i - 1)
                    src3 = xlnb[:, (i % 2) * E:(i % 2 + 1) * E]
                    pt3 = ps_tp[:, (i % 2) * 1024:(i % 2 + 1) * 1024]
                    for c in range(8):
                        mm3 = e.transpose(pt3[:, c * 128:(c + 1) * 128],
                                          src3[:, c * 128:(c + 1) * 128], identb[:])
                    mm3.then_inc(s_tpd, 1)
                for t in range(NTQ):
                    if t == 2:
                        e.wait_ge(s_zse, HPn * QBn)
                        e.wait_ge(s_rz, HPn * QBn)
                        e.wait_ge(s_ctx, HPn * QBn)
                    if t >= 3:
                        e.wait_ge(s_x2n, t - 2)
                    for dh in range(2):
                        ps = _pso(t, dh)
                        for c in range(ECH):
                            e.matmul(ps, ctxT[:, c * 2048 + t * 128: c * 2048 + (t + 1) * 128],
                                     wo_sb[:, c * 1024 + dh * 512: c * 1024 + dh * 512 + 512],
                                     start=(c == 0), stop=False)
                        mm = e.matmul(ps, onesbb[:], bobr[:, dh * 512: dh * 512 + 512],
                                      start=False, stop=True)
                    mm.then_inc(s_op, 1)
                    if 0 <= t - 2 < 8:
                        _tp3(t - 2)
            if eng == "vector":
                ps_o6v = P8[:, 3072:4096]
                def _psov(t, dh):
                    if t % 3 == 2:
                        return ps_o6v[:, dh * 512:(dh + 1) * 512]
                    return ps_qkv[(t % 3) * 2 + dh]
                for t in range(NTQ):
                    e.wait_ge(s_op, t + 1)
                    p0 = _psov(t, 0)
                    p1 = _psov(t, 1)
                    e.bn_stats(st12[:, 0:6], p0)
                    e.bn_stats(st12[:, 6:12], p1)
                    e.drain()
                    e.bn_aggr(mv2[:], st12[:])
                    e.drain()
                    e.tensor_scalar_add(rs1[:], mv2[:, 1:2], float(EPS)).then_inc(s_bnc, 1)
                    e.drain()
                    e.wait_ge(s_rsc, t + 1)
                    e.wait_ge(s_xrp[t % 2], 16 * (t // 2 + 1))
                    for dh in range(2):
                        ps = _psov(t, dh)
                        nmv = e.tensor_scalar(tmpn[:, dh * 512: dh * 512 + 512], ps,
                                              mv2[:, 0:1], rs1[:], ALU.subtract, ALU.mult)
                    nmv.then_inc(s_x2n, 1)
                    e.drain()
                    e.tensor_add(outb[:, (t % 2) * E:(t % 2 + 1) * E],
                                 tmpn[:],
                                 outb[:, (t % 2) * E:(t % 2 + 1) * E]).then_inc(s_nrm, 1)
                    e.drain()
                    c3 = t if t < 8 else t + 8
                    ob = outb[:, (t % 2) * E:(t % 2 + 1) * E]
                    e.bn_stats(st12[:, 0:6], ob[:, 0:512])
                    e.bn_stats(st12[:, 6:12], ob[:, 512:1024])
                    e.drain()
                    e.bn_aggr(mv2[:], st12[:])
                    e.drain()
                    e.tensor_copy(mnv[:, c3:c3 + 1], mv2[:, 0:1])
                    e.tensor_scalar_add(vrv[:, c3:c3 + 1], mv2[:, 1:2],
                                        float(EPS)).then_inc(s_bnd, 1)
                    e.drain()
                    if 0 <= t - 1 < 8:
                        i = t - 1
                        e.wait_ge(s_rs3, i + 1)
                        if i >= 2:
                            e.wait_ge(s_tpd, i - 1)
                        ob = outb[:, (i % 2) * E:(i % 2 + 1) * E]
                        e.tensor_scalar(xlnb[:, (i % 2) * E:(i % 2 + 1) * E], ob,
                                        mnv[:, i:i + 1], rstb[:, i:i + 1],
                                        ALU.subtract, ALU.mult).then_inc(s_ln3, 1)
                        e.drain()
            if eng == "scalar":
                def _evac3a(i):
                    e.wait_ge(s_tpd, i + 1)
                    pt3 = ps_tp[:, (i % 2) * 1024:(i % 2 + 1) * 1024]
                    e.activation(
                        xT2.rearrange("p (c t) -> p c t", c=8)[:, :, i * 128:(i + 1) * 128],
                        pt3.rearrange("p (c t) -> p c t", c=8), AF.Copy).then_inc(s_evd, 1)
                for t in range(NTQ):
                    e.wait_ge(s_bnc, t + 1)
                    e.activation(rs1[:], rs1[:], AF.Ln)
                    e.drain()
                    e.activation(rs1[:], rs1[:], AF.Exp, scale=-0.5).then_inc(s_rsc, 1)
                    e.drain()
                    if 0 <= t - 3 < 8:
                        _evac3a(t - 3)
                    if 0 <= t - 1 < NTQ:
                        i3 = t - 1
                        cc = i3 if i3 < 8 else i3 + 8
                        e.wait_ge(s_bnd, i3 + 1)
                        e.activation(rstb[:, cc:cc + 1], vrv[:, cc:cc + 1], AF.Ln)
                        e.drain()
                        e.activation(rstb[:, cc:cc + 1], rstb[:, cc:cc + 1], AF.Exp,
                                     scale=-0.5).then_inc(s_rs3, 1)
                        e.drain()
                    if t == NTQ - 1:
                        e.wait_ge(s_bnd, NTQ)
                        e.activation(rstb[:, 23:24], vrv[:, 23:24], AF.Ln)
                        e.drain()
                        e.activation(rstb[:, 23:24], rstb[:, 23:24], AF.Exp,
                                     scale=-0.5).then_inc(s_rs3, 1)
                        e.drain()
            if eng == "gpsimd":
                for t in range(NTQ):
                    e.wait_ge(s_nrm, t + 1)
                    e.dma_start(out=x2_scr.ap()[t * 128:(t + 1) * 128, :],
                                in_=outb[:, (t % 2) * E:(t % 2 + 1) * E]).then_inc(s_x2wp[t % 2], 16)

            for th in range(2):
                tb0 = th * 8
                if eng == "sync" and th == 1:
                    e.wait_ge(s_o2f, 16)
                    for i in range(8):
                        t = tb0 + i
                        g = th * 8 + i
                        e.wait_ge(s_x2wp[t % 2], 16 * (t // 2 + 1))
                        if i >= 2:
                            e.wait_ge(s_ln3, th * 8 + i - 1)
                        e.dma_start(out=xin[:, (i % 2) * E:(i % 2 + 1) * E],
                                    in_=x2_scr.ap()[t * 128:(t + 1) * 128, :]).then_inc(s_l3sp[i % 2], 16)
                if eng == "vector" and th == 1:
                    def _norm3(i):
                        e.wait_ge(s_rs3, th * 8 + i + 1)
                        if i >= 2:
                            e.wait_ge(s_tpd, th * 8 + i - 1)
                        xi = xin[:, (i % 2) * E:(i % 2 + 1) * E]
                        e.tensor_scalar(xlnb[:, (i % 2) * E:(i % 2 + 1) * E], xi,
                                        mnv[:, 16 + i:17 + i], rstb[:, 16 + i:17 + i],
                                        ALU.subtract, ALU.mult).then_inc(s_ln3, 1)
                        e.drain()
                    def _evac3(i):
                        e.wait_ge(s_tpd, th * 8 + i + 1)
                        pt = ps_tp[:, (i % 2) * 1024:(i % 2 + 1) * 1024]
                        e.tensor_copy(
                            xT2.rearrange("p (c t) -> p c t", c=8)[:, :, i * 128:(i + 1) * 128],
                            pt.rearrange("p (c t) -> p c t", c=8)).then_inc(s_evd, 1)
                        e.drain()
                    for i in range(8):
                        e.wait_ge(s_l3sp[i % 2], 16 * (i // 2 + 1))
                        _norm3(i)
                        if i >= 2:
                            _evac3(i - 2)
                    _evac3(6)
                    _evac3(7)
                if eng == "tensor" and th == 1:
                    for i in range(8):
                        e.wait_ge(s_ln3, th * 8 + i + 1)
                        if i >= 2:
                            e.wait_ge(s_evd, th * 8 + i - 1)
                        src = xlnb[:, (i % 2) * E:(i % 2 + 1) * E]
                        pt = ps_tp[:, (i % 2) * 1024:(i % 2 + 1) * 1024]
                        for c in range(8):
                            mm = e.transpose(pt[:, c * 128:(c + 1) * 128],
                                             src[:, c * 128:(c + 1) * 128], identb[:])
                        mm.then_inc(s_tpd, 1)
                if eng == "tensor":
                    if th == 0:
                        e.wait_ge(s_x2n, NTQ)
                        e.wait_ge(s_zse, HPn * QBn)
                        e.wait_ge(s_rz, HPn * QBn)
                        e.wait_ge(s_ctx, HPn * QBn)
                    def _mm1(tb, hc):
                        j = tb * HCn + hc
                        if th == 0 and tb == 0 and hc in (0, 16):
                            e.wait_ge(s_w1h[hc // 16], 16)
                        if j >= 4:
                            e.wait_ge(s_hq, th * 64 + j - 3)
                        ps = ps_qkv[j % 4]
                        for ec in range(ECH):
                            mm = e.matmul(ps,
                                          w1T_sb[:, ec * 4096 + hc * 128: ec * 4096 + (hc + 1) * 128],
                                          xT2[:, ec * 1024 + tb * 512: ec * 1024 + tb * 512 + 512],
                                          start=(ec == 0), stop=(ec == ECH - 1))
                        mm.then_inc(s_mm1, 1)
                    hq3 = hqT.rearrange("p (hc x) -> p hc x", hc=HCn)
                    w23 = w2_sb.rearrange("p (hc x) -> p hc x", hc=HCn)
                    def _mm2(i):
                        if th == 0 and i == 0:
                            e.wait_ge(s_w2p[0], 16)
                        e.wait_ge(s_hq, th * 64 + (i // 4 + 1) * 32)
                        if th * 8 + i >= 1:
                            e.wait_ge(s_o2, th * 8 + i)
                        for dh in range(2):
                            ps = ps_m2[dh]
                            for h in range(HCn // 2):
                                mm = e.matmul(ps,
                                              hq3[:, 2 * h:2 * h + 2, i * 128:(i + 1) * 128],
                                              w23[:, 2 * h:2 * h + 2, dh * 512:dh * 512 + 512],
                                              start=(h == 0), stop=(h == HCn // 2 - 1),
                                              perf_mode=mybir.MatmulPerfMode.DoubleRow)
                        mm.then_inc(s_mm2, 1)
                    e.wait_ge(s_evd, th * 8 + 4)
                    for hc in range(HCn):
                        _mm1(0, hc)
                    e.wait_ge(s_evd, th * 8 + 8)
                    for i in range(4):
                        _mm2(i)
                        for hc in range(8 * i, 8 * i + 8):
                            _mm1(1, hc)
                    for i in range(4, 8):
                        _mm2(i)
                if eng == "sync":
                    for i in range(8):
                        t = tb0 + i
                        e.wait_ge(s_ln3, (th + 1) * 8)
                        if i >= 2:
                            e.wait_ge(s_o2f, 2 * (th * 8 + i - 1))
                        g = th * 8 + i
                        e.dma_start(out=xin[:, (i % 2) * E:(i % 2 + 1) * E],
                                    in_=x2_scr.ap()[t * 128:(t + 1) * 128, :]).then_inc(s_frp[g % 2], 16)
                if eng == "vector":
                    if th == 0:
                        e.wait_ge(s_x2wp[0], 16 * (NTQ // 2))
                        e.wait_ge(s_x2wp[1], 16 * (NTQ // 2))
                        e.wait_ge(s_b2, 16)
                        stg = o2q[:, 7 * 1024:8 * 1024]
                        e.tensor_copy(stg, b2qb)
                        e.drain()
                        e.tensor_copy(b2qb, stg)
                        e.drain()
                    def _pass1(i):
                        e.wait_ge(s_mm2, th * 8 + i + 1)
                        ot = o2q[:, i * 1024:(i + 1) * 1024]
                        for dh in range(2):
                            ps = ps_m2[dh]
                            mmv = e.tensor_tensor(ot[:, dh * 512: dh * 512 + 512], ps,
                                                  b2qb[:, dh * 512: dh * 512 + 512], ALU.add)
                        mmv.then_inc(s_o2, 1)
                        e.drain()
                        e.bn_stats(st12[:, 0:6], ot[:, 0:512])
                        e.bn_stats(st12[:, 6:12], ot[:, 512:1024])
                        e.drain()
                        e.bn_aggr(mv2[:], st12[:])
                        e.drain()
                        e.tensor_copy(mnv[:, 8 + i:9 + i], mv2[:, 0:1])
                        e.tensor_scalar_add(vrv[:, 8 + i:9 + i], mv2[:, 1:2],
                                            float(EPS)).then_inc(s_o2s, 1)
                        e.drain()
                    def _pass2(i):
                        g = th * 8 + i
                        ot = o2q[:, i * 1024:(i + 1) * 1024]
                        e.wait_ge(s_rs4, g + 1)
                        e.wait_ge(s_frp[g % 2], 16 * (g // 2 + 1))
                        if g >= 2:
                            e.wait_ge(s_outp[g % 2], 32 * (g // 2))
                        for hh in range(2):
                            hs = hh * 512
                            e.tensor_scalar(tmpn[:, hs:hs + 512], ot[:, hs:hs + 512],
                                            mnv[:, 8 + i:9 + i], rstb[:, 8 + i:9 + i],
                                            ALU.subtract, ALU.mult)
                            e.drain()
                            e.tensor_add(outb[:, (i % 2) * E + hs:(i % 2) * E + hs + 512],
                                         tmpn[:, hs:hs + 512],
                                         xin[:, (i % 2) * E + hs:(i % 2) * E + hs + 512]
                                         ).then_inc(s_o2f, 1)
                            e.drain()
                    for i in range(8):
                        _pass1(i)
                        if i >= 2:
                            _pass2(i - 2)
                    _pass2(6)
                    _pass2(7)
                if eng == "scalar":
                    def _rstd4(i):
                        e.wait_ge(s_o2s, th * 8 + i + 1)
                        e.activation(rstb[:, 8 + i:9 + i], vrv[:, 8 + i:9 + i], AF.Ln)
                        e.drain()
                        e.activation(rstb[:, 8 + i:9 + i], rstb[:, 8 + i:9 + i], AF.Exp,
                                     scale=-0.5).then_inc(s_rs4, 1)
                        e.drain()
                    n4 = [0]
                    for tb in range(2):
                        for hc in range(HCn):
                            j = tb * HCn + hc
                            e.wait_ge(s_mm1, th * 64 + j + 1)
                            e.activation(
                                hqT[:, hc * 1024 + tb * 512: hc * 1024 + tb * 512 + 512],
                                ps_qkv[j % 4], AF.Relu,
                                bias=b1s[:, hc:hc + 1]).then_inc(s_hq, 1)
                            if tb == 1 and hc % 8 == 7:
                                _rstd4(n4[0])
                                n4[0] += 1
                    while n4[0] < 8:
                        _rstd4(n4[0])
                        n4[0] += 1
                if eng == "gpsimd":
                    for i in range(8):
                        t = tb0 + i
                        g = th * 8 + i
                        for hh in range(2):
                            hs = hh * 512
                            e.wait_ge(s_o2f, 2 * g + hh + 1)
                            e.dma_start(out=out_o.ap()[t * 128:(t + 1) * 128, hs:hs + 512],
                                        in_=outb[:, (i % 2) * E + hs:(i % 2) * E + hs + 512]
                                        ).then_inc(s_outp[g % 2], 16)

            if eng == "gpsimd":
                e.wait_ge(s_outp[0], 32 * 8)
                e.wait_ge(s_outp[1], 32 * 8)

        @block.sync
        def _(e):
            emit("sync", e)

        @block.gpsimd
        def _(e):
            emit("gpsimd", e)

        @block.vector
        def _(e):
            emit("vector", e)

        @block.scalar
        def _(e):
            emit("scalar", e)

        @block.tensor
        def _(e):
            emit("tensor", e)

    return nc


_CACHE = {}


def _tables(positions, scale):
    inv = 1.0 / (10000.0 ** (np.arange(0, HD, 2, dtype=np.float64) / HD))
    ang = positions[:, None].astype(np.float64) * inv[None, :]
    c = (np.cos(ang) * scale).astype(np.float32)
    s = (np.sin(ang) * scale).astype(np.float32)
    T = positions.shape[0]
    nt = T // 128
    cm = np.ascontiguousarray(c.reshape(nt, 128, 32).transpose(1, 0, 2).reshape(128, nt * 32))
    sm = np.ascontiguousarray(s.reshape(nt, 128, 32).transpose(1, 0, 2).reshape(128, nt * 32))
    return cm, sm


def kernel(L0_attn_x, L0_attn_Wq, L0_attn_bq, L0_attn_Wk, L0_attn_bk,
           L0_attn_Wv, L0_attn_bv, L0_attn_Wo, L0_attn_bo,
           L0_ffn_w1, L0_ffn_b1, L0_ffn_w2, L0_ffn_b2):
    x = np.ascontiguousarray(np.asarray(L0_attn_x, dtype=np.float32))
    B, Lfull, _ = x.shape

    perm = []
    for h in range(NH):
        perm += [h * HD + 2 * i for i in range(HD // 2)]
        perm += [h * HD + 2 * i + 1 for i in range(HD // 2)]
    perm = np.asarray(perm)

    wqT = np.ascontiguousarray(np.asarray(L0_attn_Wq, dtype=np.float32)[perm, :].T)
    wkT = np.ascontiguousarray(np.asarray(L0_attn_Wk, dtype=np.float32)[perm, :].T)
    wvT = np.ascontiguousarray(np.asarray(L0_attn_Wv, dtype=np.float32).T)
    woT = np.ascontiguousarray(np.asarray(L0_attn_Wo, dtype=np.float32).T)
    w1T = np.ascontiguousarray(np.asarray(L0_ffn_w1, dtype=np.float32).T)
    w2T = np.ascontiguousarray(np.asarray(L0_ffn_w2, dtype=np.float32).T)
    bq = np.ascontiguousarray(np.asarray(L0_attn_bq, dtype=np.float32)[perm][None, :])
    bk = np.ascontiguousarray(np.asarray(L0_attn_bk, dtype=np.float32)[perm][None, :])
    bv = np.ascontiguousarray(np.asarray(L0_attn_bv, dtype=np.float32)[None, :])
    bo = np.ascontiguousarray(np.asarray(L0_attn_bo, dtype=np.float32)[None, :])
    b2 = np.ascontiguousarray(np.asarray(L0_ffn_b2, dtype=np.float32)[None, :])
    b1 = np.ascontiguousarray(
        np.asarray(L0_ffn_b1, dtype=np.float32).reshape(HCn, 128).T)

    if "nc" not in _CACHE:
        _CACHE["nc"] = build_nc()
    nc = _CACHE["nc"]

    in_maps = []
    for core in range(8):
        b, h = core // 2, core % 2
        own = x[b, h * TQ:(h + 1) * TQ]
        oth = x[b, (1 - h) * TQ:(2 - h) * TQ]
        x_core = np.ascontiguousarray(np.concatenate([own, oth], axis=0))
        posQ = np.arange(h * TQ, (h + 1) * TQ)
        posK = np.concatenate([posQ, np.arange((1 - h) * TQ, (2 - h) * TQ)])
        rqc, rqs = _tables(posQ, 1.0 / np.sqrt(HD))
        rkc, rks = _tables(posK, 1.0)
        in_maps.append({
            "x_in": x_core, "wqT": wqT, "wkT": wkT, "wvT": wvT, "woT": woT,
            "w1T": w1T, "w2T": w2T, "bq_i": bq, "bk_i": bk, "bv_i": bv,
            "bo_i": bo, "b2_i": b2, "b1_i": b1,
            "rqc_i": rqc, "rqs_i": rqs, "rkc_i": rkc, "rks_i": rks,
        })

    res = run_bass_kernel_spmd(nc, in_maps, core_ids=list(range(8)))
    out = np.empty((B, Lfull, E), dtype=np.float32)
    for core in range(8):
        b, h = core // 2, core % 2
        out[b, h * TQ:(h + 1) * TQ] = res.results[core]["out"]
    return out
